# version 12
# baseline (speedup 1.0000x reference)
import numpy as np
import ml_dtypes
from contextlib import ExitStack

import concourse.bass as bass
import concourse.mybir as mybir

F32 = mybir.dt.float32
BF16 = mybir.dt.bfloat16
AF = mybir.ActivationFunctionType
ALU = mybir.AluOpType
AX = mybir.AxisListType

PHASES = 9
APART = 9
D = 1024
NC = 8
H = 16
LN8 = float(np.log(8.0))


class Buf:
    __slots__ = ("w", "rs")

    def __init__(self):
        self.w = None
        self.rs = []

    def rd(self):
        return [self.w] if self.w else []

    def wr(self):
        d = list(self.rs)
        if self.w:
            d.append(self.w)
        return d

    def setw(self, tok):
        if tok:
            self.w = tok
            self.rs = []

    def addr(self, tok):
        if tok:
            self.rs.append(tok)


class Prog:
    ENG = ("sync", "pe", "act", "dve", "pool")

    def __init__(self):
        self.ops = {e: [] for e in self.ENG}
        self.cnt = {}
        self.seen = {e: {} for e in self.ENG}
        self.floor = []

    def barrier(self):
        self.floor = [(s, c) for s, c in self.cnt.items() if c > 0]

    def tok(self, s):
        return (s, self.cnt.get(s, 0))

    def op(self, eng, fn, deps=(), inc=None, amt=1, nofloor=False):
        need = {}
        dd = list(deps)
        if not nofloor:
            dd += self.floor
        for t in dd:
            if not t:
                continue
            s, v = t
            if v <= self.seen[eng].get(s, 0):
                continue
            if need.get(s, 0) < v:
                need[s] = v
        for s, v in need.items():
            self.seen[eng][s] = v
        tok = None
        incs = []
        if inc:
            self.cnt[inc] = self.cnt.get(inc, 0) + amt
            tok = (inc, self.cnt[inc])
            incs = [(inc, amt)]
        self.ops[eng].append((fn, sorted(need.items()), incs))
        return tok


def build(NTOK=4096, debug=False):
    NB = NTOK // 512
    NT128 = NTOK // 128
    nc = bass.Bass("TRN2", target_bir_lowering=False)
    p = Prog()

    def din(name, shape, dt=F32):
        return nc.dram_tensor(name, list(shape), dt, kind="ExternalInput").ap()

    xT_d = din("xT", [D, NTOK])
    wqT_d = din("wqT", [D, D])
    wkT_d = din("wkT", [D, D])
    wvT_d = din("wvT", [D, D])
    woT_d = din("woT", [D, D])
    omA_d = din("omA", [128, 64])
    boP_d = din("boP", [128, 8])
    Wqkv_d = din("Wqkv", [D, 3 * D])
    bqkvP_d = din("bqkvP", [128, 24])
    bqkvV_d = din("bqkvV", [1, D])
    WoF_d = din("WoF", [D, D])
    boFP_d = din("boFP", [128, 8])
    ln1wP_d = din("ln1wP", [128, 8])
    ln1bP_d = din("ln1bP", [128, 8])
    ln2wP_d = din("ln2wP", [128, 8])
    ln2bP_d = din("ln2bP", [128, 8])
    gateWP_d = din("gateWP", [128, 64])
    fc1ts_d = din("fc1ts", [8, 32, D, 128])
    fc2ts_d = din("fc2ts", [8, 32, 128, D])
    fc1bT_d = din("fc1bT", [128, 256])
    fc2b_d = din("fc2b", [8, D])
    iden_d = din("iden", [128, 128], BF16)
    idenF_d = din("idenF", [128, 128])
    triu_d = din("triu", [128, 128], BF16)
    fconst_d = din("fconst", [128, 3])

    out_d = nc.dram_tensor("out", [D, NTOK], F32, kind="ExternalOutput").ap()
    skind = "ExternalOutput" if debug else "Internal"
    y1S = nc.dram_tensor("y1S", [D, NTOK], F32, kind=skind).ap()
    x2S = nc.dram_tensor("x2S", [D, NTOK], F32, kind=skind).ap()
    nx2S = nc.dram_tensor("nx2S", [D, NTOK], BF16, kind=skind).ap()
    accS = nc.dram_tensor("accS", [NTOK, D], F32, kind=skind).ap()
    gTS = nc.dram_tensor("gTS", [8, NTOK], BF16, kind="Internal").ap()

    xT_r = xT_d.rearrange("(c p) t -> p c t", p=128)
    y1S_r = y1S.rearrange("(c p) t -> p c t", p=128)
    x2S_r = x2S.rearrange("(c p) t -> p c t", p=128)
    nx2S_r = nx2S.rearrange("(c p) t -> p c t", p=128)
    out_r = out_d.rearrange("(c p) t -> p c t", p=128)

    def pe_mm(o, a, b, deps, start=True, stop=True, inc=None, tp=None):
        def f(e, o=o, a=a, b=b, start=start, stop=stop, tp=tp):
            return e.matmul(o, a, b, start=start, stop=stop, tile_position=tp,
                            skip_group_check=True)
        return p.op("pe", f, deps, inc=inc)

    def pe_tr(o, a, ident, deps, inc="pe"):
        def f(e, o=o, a=a, i=ident):
            with nc.allow_low_precision(reason="bf16 transpose of bf16 data"):
                return e.transpose(o, a, i)
        return p.op("pe", f, deps, inc=inc)

    def act(func, o, i, deps, bias=0.0, scale=1.0, inc="act"):
        def f(e, o=o, i=i, func=func, bias=bias, scale=scale):
            return e.activation(o, i, func, bias=bias, scale=scale)
        return p.op("act", f, deps, inc=inc)

    def act_copy(o, i, deps, inc="act"):
        def f(e, o=o, i=i):
            return e.copy(o, i)
        return p.op("act", f, deps, inc=inc)

    def act_add(o, i, b, deps, inc="act"):
        def f(e, o=o, i=i, b=b):
            return e.add(o, i, b)
        return p.op("act", f, deps, inc=inc)

    def dve_copy(o, i, deps, inc="dve"):
        def f(e, o=o, i=i):
            return e.tensor_copy(o, i)
        return p.op("dve", f, deps, inc=inc)

    def dve_tt(op, o, a, b, deps, inc="dve"):
        def f(e, o=o, a=a, b=b, op=op):
            return e.tensor_tensor(o, a, b, op)
        return p.op("dve", f, deps, inc=inc)

    def dve_ts(o, i, s1, s2, op0, op1, deps, inc="dve"):
        def f(e, o=o, i=i, s1=s1, s2=s2, op0=op0, op1=op1):
            return e.tensor_scalar(o, i, s1, s2, op0, op1)
        return p.op("dve", f, deps, inc=inc)

    def dve_stt(o, i, s, t, op0, op1, deps, inc="dve"):
        def f(e, o=o, i=i, s=s, t=t, op0=op0, op1=op1):
            return e.scalar_tensor_tensor(o, i, s, t, op0, op1)
        return p.op("dve", f, deps, inc=inc)

    def dve_red(o, i, op, deps, negate=False, inc="dve"):
        def f(e, o=o, i=i, op=op, negate=negate):
            return e.tensor_reduce(o, i, AX.X, op, negate=negate)
        return p.op("dve", f, deps, inc=inc)

    def dve_recip(o, i, deps, inc="dve"):
        def f(e, o=o, i=i):
            with nc.allow_low_precision(reason="recip rounded to bf16 output"):
                return e.reciprocal(o, i)
        return p.op("dve", f, deps, inc=inc)

    def dve_memset(o, val, deps, inc="dve"):
        def f(e, o=o, val=val):
            return e.memset(o, val)
        return p.op("dve", f, deps, inc=inc)

    def ldma(dst, src, deps, ch="dl", nofloor=False):
        def f(e, dst=dst, src=src):
            return e.dma_start(dst, src)
        return p.op("sync", f, list(deps) + [p.tok(ch)], inc=ch, amt=16,
                    nofloor=nofloor)

    def sdma(dst, src, deps, ch="dacc"):
        def f(e, dst=dst, src=src):
            return e.dma_start(dst, src)
        return p.op("pool", f, list(deps) + [p.tok(ch)], inc=ch, amt=16)

    def pdma(dst, src, deps, accum=False, ch="dacc"):
        def f(e, dst=dst, src=src, accum=accum):
            if accum:
                return e.dma_start(dst, src, accum_op=ALU.add)
            return e.dma_start(dst, src)
        return p.op("pool", f, list(deps) + [p.tok(ch)], inc=ch, amt=16)

    es = ExitStack()
    sems = {}
    for s in ("pe", "act", "dve", "pool", "dl", "dacc"):
        sems[s] = es.enter_context(nc.semaphore("sem_" + s))

    def flush():
        with nc.Block() as block:
            for name, deco in (("sync", block.sync), ("pe", block.tensor),
                               ("act", block.scalar), ("dve", block.vector),
                               ("pool", block.gpsimd)):
                ops = p.ops[name]
                if not ops:
                    continue

                def make(ops=ops):
                    def run(eng):
                        for fn, waits, incs in ops:
                            for s, v in waits:
                                eng.wait_ge(sems[s], v)
                            ins = fn(eng)
                            for s, a in incs:
                                ins.then_inc(sems[s], a)
                    return run
                deco(make())
        p.ops = {e: [] for e in p.ENG}

    def SB(st, name, shape, dt=F32, side=None):
        if side:
            return st.enter_context(nc.sbuf_tensor(name, list(shape), dt, side=side))
        return st.enter_context(nc.sbuf_tensor(name, list(shape), dt))

    class PsA:
        def __init__(self, st, tag):
            self.st = st
            self.tag = tag
            self.n = 0

        def alloc(self, name, shape, dt=F32):
            self.n += 1
            assert self.n <= 8, f"psum overflow {self.tag}"
            return self.st.enter_context(nc.psum_tensor(name, list(shape), dt))

    class LNKit:
        def __init__(self, st, tag, ps, pbc, bpbc, ps12=None, bps12=None):
            self.Rb = SB(st, f"{tag}_Rb", [128, 512], BF16)
            self.Rsq = SB(st, f"{tag}_Rsq", [128, 512], BF16)
            self.rA = SB(st, f"{tag}_rA", [1, 512])
            self.rB = SB(st, f"{tag}_rB", [1, 512])
            self.rC = SB(st, f"{tag}_rC", [1, 512])
            self.rowb = SB(st, f"{tag}_rowb", [1, 1024], BF16)
            if ps12 is None:
                self.ps1 = ps.alloc(f"{tag}_ps1", [1, 512])[:]
                self.ps2 = ps.alloc(f"{tag}_ps2", [1, 512])[:]
                self.bps1 = Buf()
                self.bps2 = Buf()
            else:
                self.ps1, self.ps2 = ps12
                self.bps1, self.bps2 = bps12
            self.pbc = pbc
            self.bpbc = bpbc
            self.bRb = Buf()
            self.bRsq = Buf()
            self.brow = Buf()

        def stats(self, src_ap, bsrc, dc, ones_c):
            t1 = dve_copy(self.Rb[:], src_ap, bsrc.rd() + self.bRb.wr())
            self.bRb.setw(t1)
            bsrc.addr(t1)
            t2 = act(AF.Square, self.Rsq[:], src_ap, bsrc.rd() + self.bRsq.wr())
            self.bRsq.setw(t2)
            bsrc.addr(t2)
            m1 = pe_mm(self.ps1, self.ones_c[:], self.Rb[:],
                       [t1] + (self.bps1.wr() if dc == 0 else []),
                       start=(dc == 0), stop=(dc == NC - 1), inc="pe")
            self.bps1.setw(m1)
            self.bRb.addr(m1)
            m2 = pe_mm(self.ps2, self.ones_c[:], self.Rsq[:],
                       [t2] + (self.bps2.wr() if dc == 0 else []),
                       start=(dc == 0), stop=(dc == NC - 1), inc="pe")
            self.bps2.setw(m2)
            self.bRsq.addr(m2)

        def finish(self, ones_r, eps=1e-5):
            A_, B_, C_ = self.rA, self.rB, self.rC
            d1 = dve_ts(A_[:], self.ps1, 1.0 / D, None, ALU.mult, ALU.bypass,
                        [self.bps1.w] + self.brow.wr())
            d2 = dve_ts(B_[:], self.ps2, 1.0 / D, None, ALU.mult, ALU.bypass,
                        [self.bps2.w, d1])
            self.bps1.addr(d1)
            self.bps2.addr(d2)
            d3 = dve_tt(ALU.mult, C_[:], A_[:], A_[:], [d2])
            d4 = dve_tt(ALU.subtract, B_[:], B_[:], C_[:], [d3])
            a1 = act(AF.Ln, C_[:], B_[:], [d4],
                     bias=self.fconst[0:1, 1:2])
            d5 = act(AF.Exp, B_[:], C_[:], [a1], scale=-0.5)
            d6 = dve_tt(ALU.mult, C_[:], A_[:], B_[:], [d5])
            d7 = dve_copy(self.rowb[0:1, 0:512], B_[:], [d6])
            d8 = dve_copy(self.rowb[0:1, 512:1024], C_[:], [d7])
            self.brow.setw(d8)
            t1 = pe_mm(self.pbc[0][:], self.ones_r[0:1, 0:128],
                       self.rowb[0:1, 0:512],
                       [d8] + self.bpbc[0].wr(), inc="pe")
            t2 = pe_mm(self.pbc[1][:], self.ones_r[0:1, 0:128],
                       self.rowb[0:1, 512:1024],
                       [d8] + self.bpbc[1].wr(), inc="pe")
            self.bpbc[0].setw(t1)
            self.bpbc[1].setw(t2)
            self.brow.addr(t1)
            self.brow.addr(t2)

    with es:
        triu = SB(es, "triu_b", [128, 128], BF16)
        ones_r = SB(es, "ones_r", [1, 128], BF16)
        ones_c = SB(es, "ones_c", [128, 1], BF16)
        LNKit.ones_r = ones_r
        LNKit.ones_c = ones_c
        omA = SB(es, "omA_b", [128, 64], BF16)
        boP = SB(es, "boP_b", [128, 8])
        boFP = SB(es, "boFP_b", [128, 8])
        bqkvP = SB(es, "bqkvP_b", [128, 24])
        ln1wP = SB(es, "ln1wP_b", [128, 8])
        ln1bP = SB(es, "ln1bP_b", [128, 8])
        ln2wP = SB(es, "ln2wP_b", [128, 8])
        ln2bP = SB(es, "ln2bP_b", [128, 8])
        gateWb = SB(es, "gateWb", [128, 64], BF16)
        fconst = SB(es, "fconst_b", [128, 3])
        LNKit.fconst = None
        g_tm = SB(es, "g_tm", [128, NT128 * 8])
        b_small = Buf()
        b_gtm = Buf()

        stg0 = SB(es, "stg0", [128, 64])
        bstg0 = Buf()

        LNKit.fconst = fconst
        for dst, src in ((boP, boP_d), (boFP, boFP_d), (bqkvP, bqkvP_d),
                         (ln1wP, ln1wP_d), (ln1bP, ln1bP_d),
                         (ln2wP, ln2wP_d), (ln2bP, ln2bP_d), (triu, triu_d),
                         (fconst, fconst_d)):
            b_small.addr(ldma(dst[:], src, []))
        t1 = ldma(stg0[:, 0:64], omA_d, [])
        bstg0.setw(t1)
        tc = dve_copy(omA[:], stg0[:, 0:64], [t1])
        bstg0.addr(tc)
        b_small.addr(tc)
        t1 = ldma(stg0[:, 0:64], gateWP_d, bstg0.wr())
        bstg0.setw(t1)
        tc = dve_copy(gateWb[:], stg0[:, 0:64], [t1])
        bstg0.addr(tc)
        b_small.addr(tc)
        b_small.addr(dve_memset(ones_r[:], 1.0, []))
        b_small.addr(dve_memset(ones_c[:], 1.0, []))

        esO = ExitStack()
        with esO:
            es1 = ExitStack()
            with es1:
                NX1 = [SB(es1, f"NX1_{c}", [128, NTOK], BF16) for c in range(NC)]
                bNX1 = [Buf() for _ in range(NC)]

                esA = ExitStack()
                with esA:
                    wQ = SB(esA, "wQ", [128, NC * D], BF16)
                    wK = SB(esA, "wK", [128, NC * D], BF16)
                    wV = SB(esA, "wV", [128, NC * D], BF16)
                    wO = SB(esA, "wO", [128, NC * D], BF16)
                    xb = SB(esA, "xb", [128, NC * 512], BF16)
                    xstg = SB(esA, "xstg", [128, 2048])
                    combq = [[SB(esA, f"combq{g}{i}", [128, 512], BF16)
                              for i in range(2)] for g in range(2)]
                    combk = [[SB(esA, f"combk{g}{i}", [128, 512], BF16)
                              for i in range(2)] for g in range(2)]
                    fkT = [[SB(esA, f"fkT{g}{i}", [128, 128], BF16)
                            for i in range(4)] for g in range(2)]
                    fq = [[combq[g][jj][0:64, :] for jj in range(2)]
                          for g in range(2)]
                    fk = [[combk[g][jj][0:64, :] for jj in range(2)]
                          for g in range(2)]
                    amS = SB(esA, "amS", [128, 256], BF16)
                    v2 = [SB(esA, f"v2_{i}", [128, H * 65], BF16) for i in range(4)]
                    SRUN = SB(esA, "SRUN", [64, H * 65])
                    Ssb = [SB(esA, f"Ssb{g}", [64, 2 * 4 * 65], BF16)
                           for g in range(2)]
                    srun_bf = [SB(esA, f"srun_bf{g}", [64, 2 * 65], BF16)
                               for g in range(2)]
                    csc = [SB(esA, f"csc{i}", [128, 512], BF16) for i in range(NC)]
                    RES = SB(esA, "RES", [128, NC * 512])
                    APL = SB(esA, "APL", [128, 512])

                    psa = PsA(esA, "A")
                    pq = psa.alloc("pq", [128, 512])
                    pk = psa.alloc("pk", [128, 512])
                    Apack = psa.alloc("Apack", [128, 512])
                    pA_ = Apack[:, 0:128]
                    po = [psa.alloc("poT0", [65, 128])[:],
                          psa.alloc("poT1", [65, 128])[:],
                          psa.alloc("poT2", [65, 128])[:]]
                    phiA = pq[0:64, 0:512]
                    phiB = pk[0:64, 0:512]
                    ptmA = pq[:, 0:64]
                    ptmB = pk[:, 0:64]
                    pwo0 = psa.alloc("pwo0", [128, 512])
                    pwo = [pwo0, Apack]
                    pScT = psa.alloc("pScT", [64, 65])
                    pSc = [pScT[:], pScT[:]]
                    pzb = pwo0[0:64, 0:128]
                    bpwo = None

                    bxb, bAPL = (Buf() for _ in range(2))
                    bSsb = [[Buf(), Buf()], [Buf(), Buf()]]
                    bsrun_bf = [[Buf(), Buf()], [Buf(), Buf()]]
                    bxstg = [Buf(), Buf()]
                    bw = Buf()
                    bcombq = [[Buf(), Buf()], [Buf(), Buf()]]
                    bcombk = [[Buf(), Buf()], [Buf(), Buf()]]
                    bfq = bcombq
                    bfk = bcombk
                    bam = Buf()
                    bfkT = [[Buf() for _ in range(4)] for _ in range(2)]
                    bv2 = [Buf() for _ in range(4)]
                    bSRUN = [Buf() for _ in range(H)]
                    bcsc = [Buf() for _ in range(NC)]
                    bRES = [Buf() for _ in range(NC)]
                    bpq, bpk, bpA = (Buf() for _ in range(3))
                    bpo = [Buf(), Buf(), Buf()]
                    bpwo = [Buf(), bpA]
                    bpzb = bpwo[0]
                    bpScT = Buf()
                    bpSc = [bpScT, bpScT]
                    kit = LNKit(esA, "A", psa, pwo, bpwo,
                                ps12=(pq[0:1, 0:512], pk[0:1, 0:512]),
                                bps12=(bpq, bpk))
                    bzrs = kit.brow
                    bzrecb = kit.brow
                    zrs = kit.rA[0:1, 0:128]
                    zrecb = kit.rowb.bitcast(BF16)[0:1, 0:128] if False else \
                        kit.rowb[0:1, 0:128]

                    wi = 0
                    for wsb, wd in ((wQ, wqT_d), (wK, wkT_d), (wV, wvT_d),
                                    (wO, woT_d)):
                        wr_ = wd.rearrange("(c p) i -> p c i", p=128)
                        for c in range(NC):
                            j = wi % 2
                            wi += 1
                            sl_ = xstg[:, j * 1024:j * 1024 + 1024]
                            td = ldma(sl_, wr_[:, c, :], bxstg[j].wr())
                            bxstg[j].setw(td)
                            tcw = dve_copy(wsb[:, c * D:(c + 1) * D], sl_, [td])
                            bxstg[j].addr(tcw)
                            bw.addr(tcw)

                    for i in range(4):
                        bv2[i].setw(dve_memset(v2[i][:], 1.0, []))
                    tz = dve_memset(SRUN[:], 0.0, [])
                    for h in range(H):
                        bSRUN[h].setw(tz)

                    for b in range(NB if PHASES >= 1 else 0):
                        T0 = b * 512
                        for hf in range(2):
                            sl_ = xstg[:].rearrange("p (c t) -> p c t", c=4)
                            td = ldma(sl_, xT_r[:, hf * 4:hf * 4 + 4, T0:T0 + 512],
                                      bxstg[0].wr() + bxstg[1].wr())
                            bxstg[0].setw(td)
                            bxstg[1].setw(td)
                            tcx = dve_copy(xb[:, hf * 2048:hf * 2048 + 2048],
                                           xstg[:],
                                           [td] + (bxb.wr() if hf == 0 else []))
                            bxb.setw(tcx)
                            bxstg[0].addr(tcx)
                            bxstg[1].addr(tcx)

                        for r in range(4 if APART >= 1 else 0):
                            for half in range(2):
                                pv = pwo[half]
                                bp = bpwo[half]
                                deps = bxb.rd() + bp.wr() + list(bw.rs)
                                tm = None
                                for kc in range(NC):
                                    tm = pe_mm(
                                        pv[:],
                                        xb[:, kc * 512 + r * 128:kc * 512 + r * 128 + 128],
                                        wV[:, kc * D + half * 512:kc * D + half * 512 + 512],
                                        deps if kc == 0 else [],
                                        start=(kc == 0), stop=(kc == NC - 1),
                                        inc="pe" if kc == NC - 1 else None)
                                bp.setw(tm)
                                bxb.addr(tm)
                                dst = v2[r][:].rearrange("p (h x) -> p h x", h=H)[
                                    :, half * 8:half * 8 + 8, 0:64]
                                src = pv[:].rearrange("p (h x) -> p h x", x=64)
                                te = dve_copy(dst, src, [tm] + bv2[r].wr())
                                bv2[r].setw(te)
                                bp.addr(te)

                        def prep(hp):
                            g = hp % 2
                            h0, h1 = 2 * hp, 2 * hp + 1
                            for pp_, bpp_, wsb in ((pq, bpq, wQ), (pk, bpk, wK)):
                                deps = bxb.rd() + bpp_.wr()
                                tm = None
                                for kc in range(NC):
                                    tm = pe_mm(
                                        pp_[:],
                                        wsb[:, kc * D + hp * 128:kc * D + hp * 128 + 128],
                                        xb[:, kc * 512:kc * 512 + 512],
                                        deps if kc == 0 else [],
                                        start=(kc == 0), stop=(kc == NC - 1),
                                        inc="pe" if kc == NC - 1 else None)
                                bpp_.setw(tm)
                                bxb.addr(tm)
                            for pp_, bpp_, comb, bcomb in (
                                    (pq, bpq, combq[g], bcombq[g]),
                                    (pk, bpk, combk[g], bcombk[g])):
                                for j in range(2):
                                    t1 = dve_copy(comb[j][0:64, :],
                                                  pp_[j * 64:j * 64 + 64, :],
                                                  [bpp_.w] + bcomb[j].wr())
                                    t2 = act(AF.Square, comb[j][64:128, :],
                                             pp_[j * 64:j * 64 + 64, :],
                                             [bpp_.w] + bcomb[j].wr() + [t1])
                                    bcomb[j].setw(t2)
                                    bcomb[j].addr(t1)
                                    bpp_.addr(t1)
                                    bpp_.addr(t2)
                            for comb, bcomb, f_, php, bphp in (
                                    (combq[g][0], bcombq[g][0], fq[g][0], phiA, bpq),
                                    (combq[g][1], bcombq[g][1], fq[g][1], phiB, bpk)):
                                tm = pe_mm(php, omA[:], comb[:],
                                           bcomb.rd() + list(bcomb.rs) +
                                           bphp.wr() + [b_small.w], inc="pe")
                                bphp.setw(tm)
                                bcomb.addr(tm)
                                te = act(AF.Exp, f_, php,
                                         [tm] + bcomb.wr(),
                                         bias=fconst[0:64, 0:1])
                                bcomb.setw(te)
                                bphp.addr(te)
                            for j in range(2):
                                for r in range(4):
                                    even = ((j * 4 + r) % 2 == 0)
                                    ptm = ptmA if even else ptmB
                                    bptm = bpq if even else bpk
                                    tt_ = pe_mm(ptm,
                                                combk[g][j][:, r * 128:r * 128 + 128],
                                                omA[:],
                                                bcombk[g][j].rd() +
                                                list(bcombk[g][j].rs) +
                                                bptm.wr() + [b_small.w], inc="pe")
                                    bptm.setw(tt_)
                                    bcombk[g][j].addr(tt_)
                                    tc2 = act(AF.Exp,
                                              fkT[g][r][:, j * 64:j * 64 + 64],
                                              ptm, [tt_] + bfkT[g][r].wr(),
                                              bias=fconst[:, 0:1])
                                    bfkT[g][r].setw(tc2)
                                    bptm.addr(tc2)
                            for comb, bcomb, f_, php, bphp in (
                                    (combk[g][0], bcombk[g][0], fk[g][0], phiA, bpq),
                                    (combk[g][1], bcombk[g][1], fk[g][1], phiB, bpk)):
                                tm = pe_mm(php, omA[:], comb[:],
                                           bcomb.rd() + list(bcomb.rs) +
                                           bphp.wr() + [b_small.w], inc="pe")
                                bphp.setw(tm)
                                bcomb.addr(tm)
                                te = act(AF.Exp, f_, php,
                                         [tm] + bcomb.wr(),
                                         bias=fconst[0:64, 0:1])
                                bcomb.setw(te)
                                bphp.addr(te)
                            for j, h in enumerate((h0, h1)):
                                srun_sl = SRUN[:, h * 65:h * 65 + 65]
                                sbsl = srun_bf[g][:, j * 65:j * 65 + 65]
                                t40 = dve_copy(sbsl, srun_sl,
                                               bSRUN[h].rd() + bsrun_bf[g][j].wr())
                                bsrun_bf[g][j].setw(t40)
                                bSRUN[h].addr(t40)
                                for c in range(4):
                                    pscb = pSc[c % 2]
                                    bpscb = bpSc[c % 2]
                                    tS = pe_mm(pscb,
                                               fkT[g][c][:, j * 64:j * 64 + 64],
                                               v2[c][:, h * 65:h * 65 + 65],
                                               bfkT[g][c].rd() + bv2[c].rd() +
                                               bpscb.wr(), inc="pe")
                                    bpscb.setw(tS)
                                    bfkT[g][c].addr(tS)
                                    bv2[c].addr(tS)
                                    ssl = Ssb[g][:, (j * 4 + c) * 65:
                                                (j * 4 + c) * 65 + 65]
                                    tE = dve_copy(ssl, pscb,
                                                  [tS] + bSsb[g][j].wr())
                                    bSsb[g][j].setw(tE)
                                    bpscb.addr(tE)
                                    tU = dve_tt(ALU.add, srun_sl, srun_sl, pscb,
                                                [tS, tE] + bSRUN[h].wr())
                                    bSRUN[h].setw(tU)
                                    bpscb.addr(tU)

                        def chunk_stage(hp):
                            g = hp % 2
                            h0, h1 = 2 * hp, 2 * hp + 1
                            for j, h in enumerate((h0, h1)):

                                def emit_A(c, j=j, h=h):
                                    ta = pe_mm(pA_,
                                               fk[g][j][:, c * 128:c * 128 + 128],
                                               fq[g][j][:, c * 128:c * 128 + 128],
                                               bfk[g][j].rd() + bfq[g][j].rd() +
                                               bpA.wr(), inc="pe")
                                    bpA.setw(ta)
                                    bfk[g][j].addr(ta)
                                    bfq[g][j].addr(ta)
                                    am = amS[:, (c % 2) * 128:
                                             (c % 2) * 128 + 128]
                                    tmsk = dve_tt(ALU.mult, am, pA_, triu[:],
                                                  [ta] + bam.wr())
                                    bam.setw(tmsk)
                                    bam.addr(tmsk)
                                    bpA.addr(tmsk)
                                    return am, tmsk

                                def emit_pv(c, am, tmsk, j=j, h=h):
                                    pob = po[c % 3]
                                    bpob = bpo[c % 3]
                                    vsl = v2[c][:, h * 65:h * 65 + 65]
                                    pe_mm(pob, vsl, am,
                                          [tmsk] + bv2[c].rd() + bpob.wr(),
                                          start=True, stop=False, inc=None)
                                    bam.addr(None)
                                    deps = bsrun_bf[g][j].rd() + bSsb[g][j].rd()
                                    t2_ = pe_mm(pob,
                                                srun_bf[g][:, j * 65:j * 65 + 65],
                                                fq[g][j][:, c * 128:c * 128 + 128],
                                                deps + bfq[g][j].rd(),
                                                start=False, stop=(c == 0),
                                                inc="pe" if c == 0 else None)
                                    for cp in range(c):
                                        t2_ = pe_mm(
                                            pob,
                                            Ssb[g][:, (j * 4 + cp) * 65:
                                                   (j * 4 + cp) * 65 + 65],
                                            fq[g][j][:, c * 128:c * 128 + 128],
                                            [], start=False, stop=(cp == c - 1),
                                            inc="pe" if cp == c - 1 else None)
                                    bpob.setw(t2_)
                                    bsrun_bf[g][j].addr(t2_)
                                    bSsb[g][j].addr(t2_)
                                    bfq[g][j].addr(t2_)
                                    bv2[c].addr(t2_)
                                    bam.addr(t2_)
                                    return t2_

                                def emit_z(c, t2_, j=j, h=h):
                                    pob = po[c % 3]
                                    bpob = bpo[c % 3]
                                    t5 = act(AF.Ln, zrs, pob[64:65, :],
                                             [t2_, b_small.w] + bzrs.wr(),
                                             bias=fconst[0:1, 2:3])
                                    bzrs.setw(t5)
                                    bpob.addr(t5)
                                    t7 = act(AF.Exp, zrecb, zrs,
                                             [t5] + bzrecb.wr(), scale=-1.0)
                                    bzrs.addr(t7)
                                    bzrecb.setw(t7)
                                    t8 = pe_mm(pzb, ones_r[0:1, 0:64], zrecb,
                                               [t7] + bpzb.wr(), inc="pe")
                                    bpzb.setw(t8)
                                    bzrecb.addr(t8)
                                    osb = combk[g][j][64:128, (c % 2) * 128:
                                                      (c % 2) * 128 + 128]
                                    t9 = dve_copy(osb, pob[0:64, :],
                                                  [t2_] + bcombk[g][j].wr())
                                    bcombk[g][j].setw(t9)
                                    bcombk[g][j].addr(t9)
                                    bpob.addr(t9)
                                    dst = csc[hp][j * 64:j * 64 + 64,
                                                  c * 128:c * 128 + 128]
                                    t10 = dve_tt(ALU.mult, dst, osb, pzb,
                                                 [t9, t8] + bcsc[hp].wr())
                                    bcsc[hp].setw(t10)
                                    bpzb.addr(t10)
                                    bcombk[g][j].addr(t10)

                                am0, tm0 = emit_A(0)
                                am1, tm1 = emit_A(1)
                                t2a = emit_pv(0, am0, tm0)
                                am2, tm2 = emit_A(2)
                                t2b = emit_pv(1, am1, tm1)
                                emit_z(0, t2a)
                                am3, tm3 = emit_A(3)
                                t2c = emit_pv(2, am2, tm2)
                                emit_z(1, t2b)
                                t2d = emit_pv(3, am3, tm3)
                                emit_z(2, t2c)
                                emit_z(3, t2d)

                        if APART >= 2:
                            prep(0)
                            for hp in range(8):
                                if hp + 1 < 8:
                                    prep(hp + 1)
                                if APART >= 5:
                                    chunk_stage(hp)
                        for dc in range(NC if APART >= 6 else 0):
                            j = dc % 2
                            if dc % 4 == 0:
                                hf = dc // 4
                                sl_ = xstg[:].rearrange("p (c t) -> p c t", c=4)
                                tdx4 = ldma(sl_,
                                            xT_r[:, hf * 4:hf * 4 + 4, T0:T0 + 512],
                                            bxstg[0].wr() + bxstg[1].wr())
                                bxstg[0].setw(tdx4)
                                bxstg[1].setw(tdx4)
                            tdx = tdx4
                            pv = pwo[j]
                            bp = bpwo[j]
                            tm = None
                            for ic in range(NC):
                                deps2 = ((bp.wr() + list(bw.rs)) if ic == 0 else []) \
                                    + bcsc[ic].rd()
                                tm = pe_mm(
                                    pv[:],
                                    wO[:, ic * D + dc * 128:ic * D + dc * 128 + 128],
                                    csc[ic][:, 0:512], deps2,
                                    start=(ic == 0), stop=(ic == NC - 1),
                                    inc="pe" if ic == NC - 1 else None)
                            bp.setw(tm)
                            for ic in range(NC):
                                bcsc[ic].addr(tm)
                            tr_ = dve_stt(RES[:, dc * 512:dc * 512 + 512], pv[:],
                                          boP[:, dc:dc + 1],
                                          xstg[:, (dc % 4) * 512:(dc % 4) * 512 + 512],
                                          ALU.add, ALU.add,
                                          [tm, tdx, b_small.w] + bRES[dc].wr())
                            bRES[dc].setw(tr_)
                            bp.addr(tr_)
                            bxstg[0].addr(tr_)
                            bxstg[1].addr(tr_)
                        for dc in range(NC if APART >= 6 else 0):
                            kit.stats(RES[:, dc * 512:dc * 512 + 512], bRES[dc], dc, ones_c)
                        if APART >= 6:
                            kit.finish(ones_r)
                        for dc in range(NC if APART >= 6 else 0):
                            rsl = RES[:, dc * 512:dc * 512 + 512]
                            t1 = dve_tt(ALU.mult, rsl, rsl,
                                        kit.pbc[0][:],
                                        bRES[dc].wr() + [kit.bpbc[0].w])
                            t2 = dve_tt(ALU.subtract, rsl, rsl,
                                        kit.pbc[1][:], [t1, kit.bpbc[1].w])
                            bRES[dc].setw(t2)
                            kit.bpbc[0].addr(t2)
                            kit.bpbc[1].addr(t2)
                        ts_ = sdma(y1S_r[:, :, T0:T0 + 512],
                                   RES[:].rearrange("p (c t) -> p c t", c=NC),
                                   [b.w for b in bRES])
                        for dc in range(NC):
                            bRES[dc].addr(ts_)
                        for dc in range(NC if APART >= 6 else 0):
                            kit.stats(RES[:, dc * 512:dc * 512 + 512], bRES[dc], dc, ones_c)
                        if APART >= 6:
                            kit.finish(ones_r)
                        for dc in range(NC if APART >= 6 else 0):
                            t1 = dve_tt(ALU.mult, APL[:],
                                        RES[:, dc * 512:dc * 512 + 512], kit.pbc[0][:],
                                        bRES[dc].rd() + [kit.bpbc[0].w] + bAPL.wr())
                            t2 = dve_tt(ALU.subtract, APL[:], APL[:], kit.pbc[1][:],
                                        [t1, kit.bpbc[1].w])
                            bRES[dc].addr(t2)
                            kit.bpbc[0].addr(t2)
                            kit.bpbc[1].addr(t2)
                            t3 = dve_ts(NX1[dc][:, T0:T0 + 512], APL[:],
                                        ln1wP[:, dc:dc + 1], ln1bP[:, dc:dc + 1],
                                        ALU.mult, ALU.add,
                                        [t2, b_small.w] + bNX1[dc].wr())
                            bAPL.setw(t3)
                            bNX1[dc].setw(t3)
                    flush()
                    p.barrier()

                O_sb = [SB(esO, f"O_{c}", [128, NTOK], BF16, side="right")
                        for c in range(NC)]
                bO = [Buf() for _ in range(NC)]
                esB2 = ExitStack()
                with esB2:
                    Qg = [SB(esB2, f"Qg{i}", [128, NTOK], BF16) for i in range(2)]
                    Kg = [SB(esB2, f"Kg{i}", [128, NTOK], BF16) for i in range(2)]
                    Vg = [SB(esB2, f"Vg{i}", [128, 4 * 65], BF16)
                          for i in range(NT128)]
                    wg = SB(esB2, "wg", [128, NC * 768], BF16)
                    bqkvV = SB(esB2, "bqkvV_b", [1, D], BF16)
                    stgB = [SB(esB2, f"stgB{i}", [128, 512]) for i in range(2)]
                    P_sb = [SB(esB2, f"P_{i}", [128, 512], BF16) for i in range(3)]
                    osb2 = SB(esB2, "osb2", [64, 512], BF16)
                    zrs2 = SB(esB2, "zrs2", [1, 512])
                    zrecb2 = SB(esB2, "zrecb2", [1, 512], BF16)
                    bQg = [Buf(), Buf()]
                    bKg = [Buf(), Buf()]
                    bVg = [Buf() for _ in range(NT128)]
                    bwg = Buf()
                    bbqkvV = Buf()
                    bstgB = [Buf(), Buf()]
                    bP = [Buf(), Buf(), Buf()]
                    bosb2, bzrs2, bzrecb2 = Buf(), Buf(), Buf()

                    psb = PsA(esB2, "B")
                    pqk = [psb.alloc(f"pqk{i}", [128, 512]) for i in range(3)]
                    po2 = psb.alloc("po2", [65, 512])
                    pzb2 = psb.alloc("pzb2", [64, 512])
                    pvtm = psb.alloc("pvtm", [128, 256])
                    bpqk = [Buf(), Buf(), Buf()]
                    bpo2, bpzb2, bpvtm = Buf(), Buf(), Buf()

                    td = ldma(stgB[0][0:1, 0:512], bqkvV_d[0:1, 0:512],
                              bstgB[0].wr())
                    bstgB[0].setw(td)
                    tc0 = dve_copy(bqkvV[0:1, 0:512], stgB[0][0:1, 0:512], [td])
                    bstgB[0].addr(tc0)
                    td = ldma(stgB[1][0:1, 0:512], bqkvV_d[0:1, 512:1024],
                              bstgB[1].wr())
                    bstgB[1].setw(td)
                    tc = dve_copy(bqkvV[0:1, 512:1024], stgB[1][0:1, 0:512],
                                  [td, tc0])
                    bstgB[0].addr(tc)
                    bbqkvV.setw(tc)

                    for i in range(NT128):
                        bVg[i].setw(dve_memset(Vg[i][:], 1.0, []))

                    Wq_r = Wqkv_d.rearrange("(c p) o -> p c o", p=128)
                    for g in range(4 if PHASES >= 2 else 0):
                        for part, coff in ((0, g * 256), (1, D + g * 256),
                                           (2, 2 * D + g * 256)):
                            for qc in range(4):
                                j = qc % 2
                                td = ldma(
                                    stgB[j][:].rearrange("p (c o) -> p c o", c=2),
                                    Wq_r[:, qc * 2:qc * 2 + 2, coff:coff + 256],
                                    bstgB[j].wr())
                                bstgB[j].setw(td)
                                dst = wg[:].rearrange("p (c o) -> p c o", o=768)[
                                    :, qc * 2:qc * 2 + 2,
                                    part * 256:part * 256 + 256]
                                tcw = dve_copy(
                                    dst,
                                    stgB[j][:].rearrange("p (c o) -> p c o", c=2),
                                    [td] + (bwg.wr() if (part == 0 and qc == 0)
                                            else [bwg.w]))
                                bwg.setw(tcw)
                                bstgB[j].addr(tcw)

                        for oc in range(2):
                            cq = g * 2 + oc
                            for dstT, bdstT, wcol, biascol in (
                                    (Qg[oc], bQg[oc], oc * 128, cq),
                                    (Kg[oc], bKg[oc], 256 + oc * 128, 8 + cq)):
                                for tt in range(NB):
                                    pp_ = pqk[tt % 2]
                                    bpp_ = bpqk[tt % 2]
                                    deps = bwg.rd() + bpp_.wr() + \
                                        (bdstT.wr() if tt == 0 else [])
                                    tm = None
                                    for kc in range(NC):
                                        tm = pe_mm(
                                            pp_[:],
                                            wg[:, kc * 768 + wcol:kc * 768 + wcol + 128],
                                            NX1[kc][:, tt * 512:tt * 512 + 512],
                                            (deps + bNX1[kc].rd()) if kc == 0 else [],
                                            start=(kc == 0), stop=(kc == NC - 1),
                                            inc="pe" if kc == NC - 1 else None)
                                        bNX1[kc].addr(tm)
                                    bpp_.setw(tm)
                                    bwg.addr(tm)
                                    te = act_add(
                                        dstT[:, tt * 512:tt * 512 + 512],
                                        pp_[:],
                                        bqkvP[:, biascol:biascol + 1],
                                        [tm, b_small.w])
                                    bdstT.setw(te)
                                    bpp_.addr(te)
                        for t128 in range(NT128):
                            deps = bwg.rd() + bpvtm.wr() + bbqkvV.rd()
                            for kc in range(NC):
                                pe_mm(pvtm[:],
                                      NX1[kc][:, t128 * 128:t128 * 128 + 128],
                                      wg[:, kc * 768 + 512:kc * 768 + 768],
                                      (deps + bNX1[kc].rd()) if kc == 0 else [],
                                      start=(kc == 0), stop=False, inc=None)
                            tm = pe_mm(pvtm[:], ones_r[0:1, 0:128],
                                       bqkvV[0:1, g * 256:g * 256 + 256],
                                       [], start=False, stop=True, inc="pe")
                            bpvtm.setw(tm)
                            bwg.addr(tm)
                            bbqkvV.addr(tm)
                            for kc in range(NC):
                                bNX1[kc].addr(tm)
                            dst = Vg[t128][:].rearrange("p (h x) -> p h x", h=4)[
                                :, :, 0:64]
                            src = pvtm[:].rearrange("p (h x) -> p h x", x=64)
                            te = dve_copy(dst, src, [tm] + bVg[t128].wr())
                            bVg[t128].setw(te)
                            bpvtm.addr(te)

                        for hh in range(4):
                            h = 4 * g + hh
                            oc = hh // 2
                            ro = (hh % 2) * 64
                            tp = (ro, 0)
                            for qt in range(NB):
                                kmax = 4 * qt + 3

                                def emit_score(jj):
                                    pp_ = pqk[jj % 3]
                                    bpp_ = bpqk[jj % 3]
                                    ts_ = pe_mm(
                                        pp_[:],
                                        Kg[oc][ro:ro + 64, jj * 128:jj * 128 + 128],
                                        Qg[oc][ro:ro + 64, qt * 512:qt * 512 + 512],
                                        bKg[oc].rd() + bQg[oc].rd() + bpp_.wr(),
                                        inc="pe", tp=tp)
                                    bpp_.setw(ts_)
                                    bKg[oc].addr(ts_)
                                    bQg[oc].addr(ts_)
                                    Pb = P_sb[jj % 3]
                                    bPb = bP[jj % 3]
                                    if jj < 4 * qt:
                                        te = act(AF.Exp, Pb[:], pp_[:],
                                                 [ts_] + bPb.wr(), scale=0.125)
                                        bPb.setw(te)
                                        bpp_.addr(te)
                                    else:
                                        r = jj - 4 * qt
                                        deps = [ts_] + bPb.wr()
                                        tz2 = None
                                        if r > 0:
                                            tz2 = dve_memset(Pb[:, 0:128 * r], 0.0,
                                                             deps)
                                        te0 = act(AF.Exp, Pb[:, 128 * r:512],
                                                  pp_[:, 128 * r:512], deps,
                                                  scale=0.125)
                                        te = dve_tt(ALU.mult,
                                                    Pb[:, 128 * r:128 * r + 128],
                                                    Pb[:, 128 * r:128 * r + 128],
                                                    triu[:],
                                                    [te0] + ([tz2] if tz2 else []))
                                        bPb.setw(te)
                                        bpp_.addr(te0)

                                def emit_pv(jj):
                                    Pb = P_sb[jj % 3]
                                    bPb = bP[jj % 3]
                                    tpv = pe_mm(po2[:],
                                                Vg[jj][:, hh * 65:hh * 65 + 65],
                                                Pb[:],
                                                bPb.rd() + bVg[jj].rd() +
                                                (bpo2.wr() if jj == 0 else []),
                                                start=(jj == 0), stop=(jj == kmax),
                                                inc="pe")
                                    bpo2.setw(tpv)
                                    bPb.addr(tpv)
                                    bVg[jj].addr(tpv)

                                emit_score(0)
                                if kmax >= 1:
                                    emit_score(1)
                                for jj in range(kmax + 1):
                                    if jj + 2 <= kmax:
                                        emit_score(jj + 2)
                                    emit_pv(jj)
                                tlast = bpo2.w
                                t5 = act(AF.Ln, zrs2[:], po2[64:65, :],
                                         [tlast] + bzrs2.wr())
                                bzrs2.setw(t5)
                                bpo2.addr(t5)
                                t7 = act(AF.Exp, zrecb2[:], zrs2[:],
                                         [t5] + bzrecb2.wr(), scale=-1.0)
                                bzrecb2.setw(t7)
                                bzrs2.addr(t7)
                                t8 = pe_mm(pzb2[:], ones_r[0:1, 0:64], zrecb2[:],
                                           [t7] + bpzb2.wr(), inc="pe")
                                bpzb2.setw(t8)
                                bzrecb2.addr(t8)
                                t9 = dve_copy(osb2[:], po2[0:64, :],
                                              [tlast] + bosb2.wr())
                                bosb2.setw(t9)
                                bpo2.addr(t9)
                                ch = h // 2
                                dst = O_sb[ch][(h % 2) * 64:(h % 2) * 64 + 64,
                                               qt * 512:qt * 512 + 512]
                                t10 = dve_tt(ALU.mult, dst, osb2[:], pzb2[:],
                                             [t9, t8] + bO[ch].wr())
                                bO[ch].setw(t10)
                                bpzb2.addr(t10)
                                bosb2.addr(t10)
                    flush()
                    p.barrier()

            esB3 = ExitStack()
            with esB3:
                woF = SB(esB3, "woF", [128, NC * D], BF16)
                stgC = [SB(esB3, f"stgC{i}", [128, 1024]) for i in range(2)]
                y1sl = SB(esB3, "y1sl", [128, NC * 512])
                X2blk = SB(esB3, "X2blk", [128, NC * 512])
                NX2blk = SB(esB3, "NX2blk", [128, NC * 512], BF16)
                APL2 = SB(esB3, "APL2", [128, 512])
                gsc = SB(esB3, "gsc", [128, 64])
                gbit = SB(esB3, "gbit", [128, 8], BF16)
                gtrow = SB(esB3, "gtrow", [8, 128], BF16)
                iden = SB(esB3, "iden_b", [128, 128], BF16)
                biden = Buf()
                biden.setw(ldma(iden[:], iden_d, []))
                bwoF = Buf()
                bstgC = [Buf(), Buf()]
                by1sl = Buf()
                bX2 = [Buf() for _ in range(NC)]
                bNX2 = [Buf() for _ in range(NC)]
                bAPL2 = Buf()
                bgsc = Buf()
                bgbit = Buf()
                bgtrow = Buf()

                psc = PsA(esB3, "C")
                pwoF = [psc.alloc(f"pwoF{i}", [128, 512]) for i in range(2)]
                bpwoF = [Buf(), Buf()]
                pg = psc.alloc("pg", [128, 8])
                bpg = Buf()
                pgt = psc.alloc("pgt", [8, 128], BF16)
                bpgt = Buf()
                kit2 = LNKit(esB3, "C", psc, pwoF, bpwoF)

                for c in range(NC if PHASES >= 3 else 0):
                    wr_ = WoF_d.rearrange("(c p) o -> p c o", p=128)
                    j = c % 2
                    td = ldma(stgC[j][:], wr_[:, c, :], bstgC[j].wr())
                    bstgC[j].setw(td)
                    tcw = dve_copy(woF[:, c * D:(c + 1) * D],
                                   stgC[j][:], [td] + (bwoF.wr() if c == 0
                                                       else [bwoF.w]))
                    bwoF.setw(tcw)
                    bstgC[j].addr(tcw)

                for tt in range(NB if PHASES >= 3 else 0):
                    T0 = tt * 512
                    tld = ldma(y1sl[:].rearrange("p (c t) -> p c t", c=NC),
                               y1S_r[:, :, T0:T0 + 512], by1sl.wr())
                    by1sl.setw(tld)
                    for dc in range(NC):
                        pv = pwoF[dc % 2]
                        bp = bpwoF[dc % 2]
                        tm = None
                        for ic in range(NC):
                            deps2 = (bp.wr() if ic == 0 else []) + bO[ic].rd()
                            tm = pe_mm(
                                pv[:],
                                woF[:, ic * D + dc * 128:ic * D + dc * 128 + 128],
                                O_sb[ic][:, T0:T0 + 512],
                                deps2 + (bwoF.rd() if ic == 0 else []),
                                start=(ic == 0), stop=(ic == NC - 1),
                                inc="pe" if ic == NC - 1 else None)
                        bp.setw(tm)
                        for ic in range(NC):
                            bO[ic].addr(tm)
                        bwoF.addr(tm)
                        tr_ = dve_stt(X2blk[:, dc * 512:dc * 512 + 512], pv[:],
                                      boFP[:, dc:dc + 1],
                                      y1sl[:, dc * 512:dc * 512 + 512],
                                      ALU.add, ALU.add,
                                      [tm, tld, b_small.w] + bX2[dc].wr())
                        bX2[dc].setw(tr_)
                        bp.addr(tr_)
                        by1sl.addr(tr_)
                    ts_ = sdma(x2S_r[:, :, T0:T0 + 512],
                               X2blk[:].rearrange("p (c t) -> p c t", c=NC),
                               [b.w for b in bX2])
                    for dc in range(NC):
                        bX2[dc].addr(ts_)
                    for dc in range(NC):
                        kit2.stats(X2blk[:, dc * 512:dc * 512 + 512], bX2[dc],
                                   dc, ones_c)
                    kit2.finish(ones_r)
                    for dc in range(NC):
                        t1 = dve_tt(ALU.mult, APL2[:],
                                    X2blk[:, dc * 512:dc * 512 + 512],
                                    kit2.pbc[0][:],
                                    bX2[dc].rd() + [kit2.bpbc[0].w] + bAPL2.wr())
                        t2 = dve_tt(ALU.subtract, APL2[:], APL2[:], kit2.pbc[1][:],
                                    [t1, kit2.bpbc[1].w])
                        bX2[dc].addr(t2)
                        kit2.bpbc[0].addr(t2)
                        kit2.bpbc[1].addr(t2)
                        t3 = dve_ts(NX2blk[:, dc * 512:dc * 512 + 512], APL2[:],
                                    ln2wP[:, dc:dc + 1],
                                    ln2bP[:, dc:dc + 1], ALU.mult, ALU.add,
                                    [t2, b_small.w] + bNX2[dc].wr())
                        bAPL2.setw(t3)
                        bNX2[dc].setw(t3)
                    ts_ = sdma(nx2S_r[:, :, T0:T0 + 512],
                               NX2blk[:].rearrange("p (c t) -> p c t", c=NC),
                               [b.w for b in bNX2])
                    for dc in range(NC):
                        bNX2[dc].addr(ts_)
                    for r in range(4):
                        t128 = tt * 4 + r
                        tm = None
                        deps0 = bpg.wr() + [b_small.w]
                        for kc2 in range(NC):
                            deps0 += bNX2[kc2].rd()
                        for kc in range(NC):
                            tm = pe_mm(pg[:], NX2blk[:, kc * 512 + r * 128:kc * 512 + r * 128 + 128],
                                       gateWb[:, kc * 8:kc * 8 + 8],
                                       deps0 if kc == 0 else [],
                                       start=(kc == 0), stop=(kc == NC - 1),
                                       inc="pe" if kc == NC - 1 else None)
                            bNX2[kc].addr(tm)
                        bpg.setw(tm)
                        G8 = gsc[:, 0:8]
                        m1 = gsc[:, 8:9]
                        nm1 = gsc[:, 9:10]
                        m2 = gsc[:, 10:11]
                        den = gsc[:, 11:12]
                        rden = gsc[:, 12:13]
                        msk = gsc[:, 16:24]
                        g2 = gsc[:, 24:32]
                        ee = gsc[:, 32:40]
                        d0 = dve_copy(G8, pg[:], [tm] + bgsc.wr())
                        bpg.addr(d0)
                        d1 = dve_red(m1, G8, ALU.max, [d0])
                        d1b = dve_red(nm1, G8, ALU.max, [d1], negate=True)
                        d2 = dve_ts(msk, G8, m1, None, ALU.is_ge, ALU.bypass, [d1b])
                        d3 = dve_stt(g2, msk, -1e30, G8, ALU.mult, ALU.add, [d2])
                        d4 = dve_red(m2, g2, ALU.max, [d3])
                        a1 = act(AF.Exp, ee, G8, [d4], bias=nm1)
                        d5 = dve_ts(msk, G8, m2, None, ALU.is_ge, ALU.bypass, [a1])
                        d6 = dve_tt(ALU.mult, ee, ee, msk, [d5])
                        d7 = dve_red(den, ee, ALU.add, [d6])
                        d8 = dve_recip(rden, den, [d7])
                        d9 = dve_ts(g_tm[:, t128 * 8:t128 * 8 + 8], ee, rden, None,
                                    ALU.mult, ALU.bypass, [d8] + b_gtm.wr())
                        b_gtm.setw(d9)
                        bgsc.setw(d9)
                        d10 = dve_copy(gbit[:], g_tm[:, t128 * 8:t128 * 8 + 8],
                                       [d9] + bgbit.wr())
                        bgbit.setw(d10)
                        b_gtm.addr(d10)
                        t12 = pe_tr(pgt[:], gbit[:], iden[:],
                                    [d10] + biden.rd() + bpgt.wr())
                        bpgt.setw(t12)
                        bgbit.addr(t12)
                        d11 = dve_copy(gtrow[:], pgt[:], [t12] + bgtrow.wr())
                        bgtrow.setw(d11)
                        bpgt.addr(d11)
                        tsg = sdma(gTS[:, t128 * 128:t128 * 128 + 128], gtrow[:],
                                   [d11])
                        bgtrow.addr(tsg)
                flush()
                p.barrier()

        esM = ExitStack()
        with esM:
            fc1T = SB(esM, "fc1T", [128, 32 * 1024], BF16)
            fc2T = SB(esM, "fc2T", [128, 32 * 1024], BF16)
            nx2sl = [SB(esM, f"nx2sl{i}", [128, NC * 512], BF16) for i in range(2)]
            stgM = [SB(esM, f"stgM{i}", [128, 2048]) for i in range(2)]
            tmpM = [SB(esM, f"tmpM{i}", [128, 512]) for i in range(4)]
            fc1bT = SB(esM, "fc1bT_b", [128, 256])
            esM2 = ExitStack()
            Hs = [SB(esM2, f"Hs{i}", [128, 512], BF16) for i in range(32)]
            bfc1 = [Buf() for _ in range(32)]
            bfc2 = [Buf() for _ in range(32)]
            bHs = [Buf() for _ in range(32)]
            bnx2sl = [Buf(), Buf()]
            bstgM = [Buf(), Buf()]
            btmpM = [Buf(), Buf(), Buf(), Buf()]
            baccS = Buf()
            bfc1bT = Buf()
            bfc2bB = Buf()
            bgTm = Buf()
            bstgm2 = Buf()

            psm = PsA(esM, "M")
            pf = [psm.alloc(f"pf{i}", [128, 512]) for i in range(2)]
            pD = [psm.alloc(f"pD{i}", [128, 512]) for i in range(4)]
            bpf = [Buf(), Buf()]
            bpD = [Buf() for _ in range(4)]

            bfc1bT.setw(ldma(fc1bT[:], fc1bT_d, []))

            def load_pair(e, k, which, deps_extra, queue=0):
                j = (2 * k + which) % 2
                if which == 0:
                    src = fc1ts_d[e, 2 * k:2 * k + 2].rearrange(
                        "f (c p) g -> p f c g", p=128)
                    dstg = stgM[j][:].rearrange("p (f c g) -> p f c g",
                                                f=2, c=NC)
                    dst = fc1T[:, 2 * k * 1024:(2 * k + 2) * 1024]
                    bsl = [bfc1[2 * k], bfc1[2 * k + 1]]
                else:
                    src = fc2ts_d[e, 2 * k:2 * k + 2].rearrange(
                        "f p d -> p f d")
                    dstg = stgM[j][:].rearrange("p (f d) -> p f d", f=2)
                    dst = fc2T[:, 2 * k * 1024:(2 * k + 2) * 1024]
                    bsl = [bfc2[2 * k], bfc2[2 * k + 1]]
                deps = bstgM[j].wr() + deps_extra
                for b in bsl:
                    deps += b.wr()
                if queue == 0:
                    td = ldma(dstg, src, deps)
                else:
                    td = pdma(dstg, src, deps)
                bstgM[j].setw(td)
                tcw = dve_copy(dst, stgM[j][:], [td])
                bstgM[j].addr(tcw)
                for b in bsl:
                    b.setw(tcw)

            for k in range(16 if PHASES >= 4 else 0):
                load_pair(0, k, 0, [], queue=k % 2)
                load_pair(0, k, 1, [], queue=(k + 1) % 2)

            for e in range(8 if PHASES >= 4 else 0):
                for sl in range(NB):
                    T0 = sl * 512
                    jn = sl % 2
                    if NB == 1 or (e == 0 and sl == 0):
                        tld = ldma(nx2sl[jn][:].rearrange("p (c t) -> p c t", c=NC),
                                   nx2S_r[:, :, T0:T0 + 512], bnx2sl[jn].wr())
                        bnx2sl[jn].setw(tld)
                    else:
                        tld = bnx2sl[jn].w
                    if NB > 1 and (sl < NB - 1 or e < 7):
                        npf = (sl + 1) % NB
                        jn2 = npf % 2
                        T0n = npf * 512
                        tldn = ldma(nx2sl[jn2][:].rearrange("p (c t) -> p c t", c=NC),
                                    nx2S_r[:, :, T0n:T0n + 512], bnx2sl[jn2].wr())
                        bnx2sl[jn2].setw(tldn)
                    def emit_fc1(fc):
                        ppf = pf[fc % 2]
                        bppf = bpf[fc % 2]
                        tm = None
                        for kc in range(NC):
                            tm = pe_mm(
                                ppf[:],
                                fc1T[:, fc * 1024 + kc * 128:fc * 1024 + kc * 128 + 128],
                                nx2sl[jn][:, kc * 512:kc * 512 + 512],
                                ([tld] + bppf.wr() + bfc1[fc].rd()) if kc == 0 else [],
                                start=(kc == 0), stop=(kc == NC - 1),
                                inc="pe" if kc == NC - 1 else None)
                        bppf.setw(tm)
                        bfc1[fc].addr(tm)
                        bnx2sl[jn].addr(tm)
                        if e < 7 and sl == NB - 1 and fc % 2 == 1:
                            load_pair(e + 1, fc // 2, 0,
                                      [tm] + bfc1[fc - 1].rd())
                        tg = act(AF.Gelu, Hs[fc][:], ppf[:],
                                 [tm] + bfc1bT.rd() + bHs[fc].wr(),
                                 bias=fc1bT[:, e * 32 + fc:e * 32 + fc + 1])
                        bHs[fc].setw(tg)
                        bppf.addr(tg)

                    def emit_fc2A(fc):
                        tg = bHs[fc].w
                        for r in range(4):
                            tm2 = pe_mm(pD[r][:], Hs[fc][:, r * 128:r * 128 + 128],
                                        fc2T[:, fc * 1024:fc * 1024 + 512],
                                        [tg] + bfc2[fc].rd() +
                                        (bpD[r].wr() if fc == 0 else []),
                                        start=(fc == 0), stop=(fc == 31),
                                        inc="pe" if (fc == 31 or r == 3) else None)
                            bpD[r].setw(tm2)
                            bHs[fc].addr(tm2)
                            bfc2[fc].addr(tm2)

                    emit_fc1(0)
                    for fc in range(1, 32):
                        emit_fc1(fc)
                        emit_fc2A(fc - 1)
                    emit_fc2A(31)
                    for r in range(4):
                        jt = r
                        t128 = sl * 4 + r
                        gcol = g_tm[:, t128 * 8 + e:t128 * 8 + e + 1]
                        tev = dve_ts(tmpM[jt][:], pD[r][:], gcol, None, ALU.mult,
                                     ALU.bypass,
                                     [bpD[r].w, b_gtm.w] + btmpM[jt].wr())
                        btmpM[jt].setw(tev)
                        bpD[r].addr(tev)
                        b_gtm.addr(tev)
                        tdm = pdma(accS[T0 + r * 128:T0 + r * 128 + 128, 0:512],
                                   tmpM[jt][:], [tev] + baccS.rd(), accum=(e > 0))
                        btmpM[jt].addr(tdm)
                        baccS.setw(tdm)
                    for r in range(4):
                        for fc in range(32):
                            tm2 = pe_mm(pD[r][:], Hs[fc][:, r * 128:r * 128 + 128],
                                        fc2T[:, fc * 1024 + 512:fc * 1024 + 1024],
                                        (bHs[fc].rd() + bfc2[fc].rd()) +
                                        (bpD[r].wr() if fc == 0 else []),
                                        start=(fc == 0), stop=(fc == 31),
                                        inc="pe" if (fc == 31 or r == 3) else None)
                            bpD[r].setw(tm2)
                            bHs[fc].addr(tm2)
                            bfc2[fc].addr(tm2)
                            if r == 3 and e < 7 and sl == NB - 1 and fc % 2 == 1:
                                load_pair(e + 1, fc // 2, 1,
                                          list(bfc2[fc].rs) +
                                          list(bfc2[fc - 1].rs))
                    for r in range(4):
                        jt = r
                        t128 = sl * 4 + r
                        gcol = g_tm[:, t128 * 8 + e:t128 * 8 + e + 1]
                        tev = dve_ts(tmpM[jt][:], pD[r][:], gcol, None, ALU.mult,
                                     ALU.bypass,
                                     [bpD[r].w, b_gtm.w] + btmpM[jt].wr())
                        btmpM[jt].setw(tev)
                        bpD[r].addr(tev)
                        b_gtm.addr(tev)
                        tdm = pdma(accS[T0 + r * 128:T0 + r * 128 + 128, 512:1024],
                                   tmpM[jt][:], [tev] + baccS.rd(), accum=(e > 0))
                        btmpM[jt].addr(tdm)
                        baccS.setw(tdm)
            flush()
            p.barrier()
            esM2.close()

        esF = ExitStack()
        with esF:
            accsl = SB(esF, "accsl", [128, 4 * 1024])
            idenF = SB(esF, "idenF_b", [128, 128])
            bidenF = Buf()
            bidenF.setw(ldma(idenF[:], idenF_d, []))
            fc2bB = SB(esF, "fc2bB", [8, D], BF16)
            gTm = SB(esF, "gTm", [8, NTOK], BF16)
            stgm2 = SB(esF, "stgm2", [8, D])
            bfc2bB = Buf()
            bgTm = Buf()
            bstgm2 = Buf()
            td = ldma(stgm2[:], fc2b_d, bstgm2.wr())
            bstgm2.setw(td)
            tcb = dve_copy(fc2bB[:], stgm2[:], [td])
            bstgm2.addr(tcb)
            bfc2bB.setw(tcb)
            bgTm.setw(ldma(gTm[:], gTS, []))
            y1sl2 = SB(esF, "y1sl2", [128, NC * 512])
            x2sl2 = SB(esF, "x2sl2", [128, NC * 512])
            R2 = [SB(esF, f"R2_{i}", [128, 512]) for i in range(NC)]
            OUTB = SB(esF, "OUTB", [128, NC * 512])
            baccsl = Buf()
            by1sl2 = Buf()
            bx2sl2 = Buf()
            bR2 = [Buf() for _ in range(NC)]
            bOUTB = Buf()
            psf = PsA(esF, "F")
            ptrF = [psf.alloc("ptrF0", [128, 128]), psf.alloc("ptrF1", [128, 128])]
            bptrF = [Buf(), Buf()]
            pbcF = [psf.alloc(f"pbcF{i}", [128, 512]) for i in range(2)]
            bpbcF = [Buf(), Buf()]
            kit3 = LNKit(esF, "F", psf, pbcF, bpbcF)

            accS_r = accS.rearrange("(s r p) d -> s p r d", p=128, r=4)

            for tt in range(NB if PHASES >= 5 else 0):
                T0 = tt * 512
                ta = ldma(accsl[:].rearrange("p (r d) -> p r d", r=4),
                          accS_r[tt], baccsl.wr())
                baccsl.setw(ta)
                t1_ = ldma(y1sl2[:].rearrange("p (c t) -> p c t", c=NC),
                           y1S_r[:, :, T0:T0 + 512], by1sl2.wr())
                by1sl2.setw(t1_)
                t2_ = ldma(x2sl2[:].rearrange("p (c t) -> p c t", c=NC),
                           x2S_r[:, :, T0:T0 + 512], bx2sl2.wr())
                bx2sl2.setw(t2_)
                for r in range(4):
                    t128 = tt * 4 + r
                    for dc in range(NC):
                        jf = (r * NC + dc) % 2
                        pF = ptrF[jf]
                        bpF = bptrF[jf]
                        ttr = pe_tr(pF[:],
                                    accsl[:, r * 1024 + dc * 128:
                                          r * 1024 + dc * 128 + 128],
                                    idenF[:],
                                    baccsl.rd() + bpF.wr() + bidenF.rd())
                        baccsl.addr(ttr)
                        ttr = pe_mm(pF[:], fc2bB[0:8, dc * 128:dc * 128 + 128],
                                    gTm[:, t128 * 128:t128 * 128 + 128],
                                    [ttr] + bfc2bB.rd() + bgTm.rd(),
                                    start=False, stop=True, inc="pe")
                        bpF.setw(ttr)
                        bfc2bB.addr(ttr)
                        bgTm.addr(ttr)
                        da = dve_tt(ALU.add, R2[dc][:, r * 128:r * 128 + 128],
                                    pF[:],
                                    y1sl2[:, dc * 512 + r * 128:
                                          dc * 512 + r * 128 + 128],
                                    [ttr, t1_] + (bR2[dc].wr() if r == 0
                                                  else [bR2[dc].w]))
                        db = dve_tt(ALU.add, R2[dc][:, r * 128:r * 128 + 128],
                                    R2[dc][:, r * 128:r * 128 + 128],
                                    x2sl2[:, dc * 512 + r * 128:
                                          dc * 512 + r * 128 + 128],
                                    [da, t2_])
                        bR2[dc].setw(db)
                        bpF.addr(db)
                        by1sl2.addr(db)
                        bx2sl2.addr(db)
                for dc in range(NC):
                    kit3.stats(R2[dc][:], bR2[dc], dc, ones_c)
                kit3.finish(ones_r)
                outws = []
                for dc in range(NC):
                    osl = OUTB[:, dc * 512:dc * 512 + 512]
                    t1 = dve_tt(ALU.mult, osl, R2[dc][:], kit3.pbc[0][:],
                                bR2[dc].rd() + [kit3.bpbc[0].w] + bOUTB.wr())
                    t2 = dve_tt(ALU.subtract, osl, osl, kit3.pbc[1][:],
                                [t1, kit3.bpbc[1].w])
                    bOUTB.setw(t2)
                    outws.append(t2)
                    bR2[dc].addr(t2)
                    kit3.bpbc[0].addr(t2)
                    kit3.bpbc[1].addr(t2)
                ts_ = sdma(out_r[:, :, T0:T0 + 512],
                           OUTB[:].rearrange("p (c t) -> p c t", c=NC), outws)
                bOUTB.addr(ts_)
            p.op("dve",
                 lambda e: e.memset(OUTB[0:1, 0:8], 0.0),
                 [p.tok("dl"), p.tok("dacc")])
            flush()

    nc.finalize()
    return nc


def pack_cols(v):
    return np.ascontiguousarray(np.asarray(v, np.float32).reshape(-1, 128).T)


def prep_inputs(inputs, NTOK=4096):
    f32 = lambda a: np.asarray(a, np.float32)
    x = f32(inputs["L0_attn_x"])[:, :NTOK, :]
    omega = f32(inputs["L0_attn_omega"])
    com = {
        "wqT": np.ascontiguousarray(f32(inputs["L0_attn_wq"]).T),
        "wkT": np.ascontiguousarray(f32(inputs["L0_attn_wk"]).T),
        "wvT": np.ascontiguousarray(f32(inputs["L0_attn_wv"]).T),
        "woT": np.ascontiguousarray(f32(inputs["L0_attn_wo"]).T),
        "omA": np.ascontiguousarray(
            np.concatenate([omega.T, np.full((64, 64), -0.5, np.float32)],
                           axis=0)),
        "boP": pack_cols(inputs["L0_attn_bo"]),
        "Wqkv": np.ascontiguousarray(f32(inputs["L0_ffn_Wqkv"])),
        "bqkvP": pack_cols(inputs["L0_ffn_bqkv"]),
        "bqkvV": np.ascontiguousarray(
            f32(inputs["L0_ffn_bqkv"])[2048:].reshape(1, 1024)),
        "WoF": np.ascontiguousarray(f32(inputs["L0_ffn_Wo"])),
        "boFP": pack_cols(inputs["L0_ffn_bo"]),
        "ln1wP": pack_cols(inputs["L0_ffn_ln1_w"]),
        "ln1bP": pack_cols(inputs["L0_ffn_ln1_b"]),
        "ln2wP": pack_cols(inputs["L0_ffn_ln2_w"]),
        "ln2bP": pack_cols(inputs["L0_ffn_ln2_b"]),
        "gateWP": np.ascontiguousarray(
            f32(inputs["L0_ffn_gate_W"]).reshape(8, 128, 8)
            .transpose(1, 0, 2).reshape(128, 64)),
        "fc1ts": np.ascontiguousarray(
            f32(inputs["L0_ffn_fc1_w"]).transpose(0, 2, 1)
            .reshape(8, 1024, 32, 128).transpose(0, 2, 1, 3)),
        "fc2ts": np.ascontiguousarray(
            f32(inputs["L0_ffn_fc2_w"]).transpose(0, 2, 1)
            .reshape(8, 32, 128, 1024)),
        "fc1bT": np.ascontiguousarray(
            f32(inputs["L0_ffn_fc1_b"]).reshape(8, 32, 128)
            .transpose(2, 0, 1).reshape(128, 256)),
        "fc2b": np.ascontiguousarray(f32(inputs["L0_ffn_fc2_b"])),
        "iden": np.eye(128, dtype=ml_dtypes.bfloat16),
        "idenF": np.eye(128, dtype=np.float32),
        "triu": np.triu(np.ones((128, 128))).astype(ml_dtypes.bfloat16),
        "fconst": np.ascontiguousarray(
            np.broadcast_to(np.array([[-np.log(8.0), 1e-5, 1e-6]], np.float32),
                            (128, 3))),
    }
    in_maps = []
    for b in range(x.shape[0]):
        m = dict(com)
        m["xT"] = np.ascontiguousarray(x[b].T)
        in_maps.append(m)
    return in_maps


from concourse.bass_utils import run_bass_kernel_spmd


def kernel(**inputs):
    in_maps = prep_inputs(inputs, 4096)
    nc = build(4096, debug=False)
    res = run_bass_kernel_spmd(nc, in_maps, core_ids=list(range(8)))
    out = np.stack([np.asarray(r["out"], np.float32).T for r in res.results])
    return np.ascontiguousarray(out)


# version 13
# speedup vs baseline: 1.1943x; 1.1943x over previous
import numpy as np
import ml_dtypes
from contextlib import ExitStack

import concourse.bass as bass
import concourse.mybir as mybir

F32 = mybir.dt.float32
BF16 = mybir.dt.bfloat16
AF = mybir.ActivationFunctionType
ALU = mybir.AluOpType
AX = mybir.AxisListType

PHASES = 9
APART = 9
D = 1024
NC = 8
H = 16
LN8 = float(np.log(8.0))


class Buf:
    __slots__ = ("w", "rs")

    def __init__(self):
        self.w = None
        self.rs = []

    def rd(self):
        return [self.w] if self.w else []

    def wr(self):
        d = list(self.rs)
        if self.w:
            d.append(self.w)
        return d

    def setw(self, tok):
        if tok:
            self.w = tok
            self.rs = []

    def addr(self, tok):
        if tok:
            self.rs.append(tok)


class Prog:
    ENG = ("sync", "pe", "act", "dve", "pool")

    def __init__(self):
        self.ops = {e: [] for e in self.ENG}
        self.cnt = {}
        self.seen = {e: {} for e in self.ENG}
        self.floor = []

    def barrier(self):
        self.floor = [(s, c) for s, c in self.cnt.items() if c > 0]

    def tok(self, s):
        return (s, self.cnt.get(s, 0))

    def op(self, eng, fn, deps=(), inc=None, amt=1, nofloor=False):
        need = {}
        dd = list(deps)
        if not nofloor:
            dd += self.floor
        for t in dd:
            if not t:
                continue
            s, v = t
            if v <= self.seen[eng].get(s, 0):
                continue
            if need.get(s, 0) < v:
                need[s] = v
        for s, v in need.items():
            self.seen[eng][s] = v
        tok = None
        incs = []
        if inc:
            self.cnt[inc] = self.cnt.get(inc, 0) + amt
            tok = (inc, self.cnt[inc])
            incs = [(inc, amt)]
        self.ops[eng].append((fn, sorted(need.items()), incs))
        return tok


def build(NTOK=4096, debug=False):
    NB = NTOK // 512
    NT128 = NTOK // 128
    nc = bass.Bass("TRN2", target_bir_lowering=False)
    p = Prog()

    def din(name, shape, dt=F32):
        return nc.dram_tensor(name, list(shape), dt, kind="ExternalInput").ap()

    xT_d = din("xT", [D, NTOK])
    wqT_d = din("wqT", [D, D])
    wkT_d = din("wkT", [D, D])
    wvT_d = din("wvT", [D, D])
    woT_d = din("woT", [D, D])
    omA_d = din("omA", [128, 64])
    boP_d = din("boP", [128, 8])
    Wqkv_d = din("Wqkv", [D, 3 * D])
    bqkvP_d = din("bqkvP", [128, 24])
    bqkvV_d = din("bqkvV", [1, D])
    WoF_d = din("WoF", [D, D])
    boFP_d = din("boFP", [128, 8])
    ln1wP_d = din("ln1wP", [128, 8])
    ln1bP_d = din("ln1bP", [128, 8])
    ln2wP_d = din("ln2wP", [128, 8])
    ln2bP_d = din("ln2bP", [128, 8])
    gateWP_d = din("gateWP", [128, 64])
    fc1ts_d = din("fc1ts", [8, 32, D, 128])
    fc2ts_d = din("fc2ts", [8, 32, 128, D])
    fc1bT_d = din("fc1bT", [128, 256])
    fc2b_d = din("fc2b", [8, D])
    iden_d = din("iden", [128, 128], BF16)
    idenF_d = din("idenF", [128, 128])
    triu_d = din("triu", [128, 128], BF16)
    fconst_d = din("fconst", [128, 3])

    out_d = nc.dram_tensor("out", [D, NTOK], F32, kind="ExternalOutput").ap()
    skind = "ExternalOutput" if debug else "Internal"
    y1S = nc.dram_tensor("y1S", [D, NTOK], F32, kind=skind).ap()
    x2S = nc.dram_tensor("x2S", [D, NTOK], F32, kind=skind).ap()
    nx2S = nc.dram_tensor("nx2S", [D, NTOK], BF16, kind=skind).ap()
    accS = nc.dram_tensor("accS", [NTOK, D], F32, kind=skind).ap()
    gTS = nc.dram_tensor("gTS", [8, NTOK], BF16, kind="Internal").ap()

    xT_r = xT_d.rearrange("(c p) t -> p c t", p=128)
    y1S_r = y1S.rearrange("(c p) t -> p c t", p=128)
    x2S_r = x2S.rearrange("(c p) t -> p c t", p=128)
    nx2S_r = nx2S.rearrange("(c p) t -> p c t", p=128)
    out_r = out_d.rearrange("(c p) t -> p c t", p=128)

    def pe_mm(o, a, b, deps, start=True, stop=True, inc=None, tp=None):
        def f(e, o=o, a=a, b=b, start=start, stop=stop, tp=tp):
            return e.matmul(o, a, b, start=start, stop=stop, tile_position=tp,
                            skip_group_check=True)
        return p.op("pe", f, deps, inc=inc)

    def pe_tr(o, a, ident, deps, inc="pe"):
        def f(e, o=o, a=a, i=ident):
            with nc.allow_low_precision(reason="bf16 transpose of bf16 data"):
                return e.transpose(o, a, i)
        return p.op("pe", f, deps, inc=inc)

    def act(func, o, i, deps, bias=0.0, scale=1.0, inc="act"):
        def f(e, o=o, i=i, func=func, bias=bias, scale=scale):
            return e.activation(o, i, func, bias=bias, scale=scale)
        return p.op("act", f, deps, inc=inc)

    def act_copy(o, i, deps, inc="act"):
        def f(e, o=o, i=i):
            return e.copy(o, i)
        return p.op("act", f, deps, inc=inc)

    def act_add(o, i, b, deps, inc="act"):
        def f(e, o=o, i=i, b=b):
            return e.add(o, i, b)
        return p.op("act", f, deps, inc=inc)

    def dve_copy(o, i, deps, inc="dve"):
        def f(e, o=o, i=i):
            return e.tensor_copy(o, i)
        return p.op("dve", f, deps, inc=inc)

    def dve_tt(op, o, a, b, deps, inc="dve"):
        def f(e, o=o, a=a, b=b, op=op):
            return e.tensor_tensor(o, a, b, op)
        return p.op("dve", f, deps, inc=inc)

    def dve_ts(o, i, s1, s2, op0, op1, deps, inc="dve"):
        def f(e, o=o, i=i, s1=s1, s2=s2, op0=op0, op1=op1):
            return e.tensor_scalar(o, i, s1, s2, op0, op1)
        return p.op("dve", f, deps, inc=inc)

    def dve_stt(o, i, s, t, op0, op1, deps, inc="dve"):
        def f(e, o=o, i=i, s=s, t=t, op0=op0, op1=op1):
            return e.scalar_tensor_tensor(o, i, s, t, op0, op1)
        return p.op("dve", f, deps, inc=inc)

    def dve_red(o, i, op, deps, negate=False, inc="dve"):
        def f(e, o=o, i=i, op=op, negate=negate):
            return e.tensor_reduce(o, i, AX.X, op, negate=negate)
        return p.op("dve", f, deps, inc=inc)

    def dve_recip(o, i, deps, inc="dve"):
        def f(e, o=o, i=i):
            with nc.allow_low_precision(reason="recip rounded to bf16 output"):
                return e.reciprocal(o, i)
        return p.op("dve", f, deps, inc=inc)

    def dve_memset(o, val, deps, inc="dve"):
        def f(e, o=o, val=val):
            return e.memset(o, val)
        return p.op("dve", f, deps, inc=inc)

    def ldma(dst, src, deps, ch="dl", nofloor=False):
        def f(e, dst=dst, src=src):
            return e.dma_start(dst, src)
        return p.op("sync", f, list(deps) + [p.tok(ch)], inc=ch, amt=16,
                    nofloor=nofloor)

    def sdma(dst, src, deps, ch="dacc"):
        def f(e, dst=dst, src=src):
            return e.dma_start(dst, src)
        return p.op("pool", f, list(deps) + [p.tok(ch)], inc=ch, amt=16)

    def pdma(dst, src, deps, accum=False, ch="dacc"):
        def f(e, dst=dst, src=src, accum=accum):
            if accum:
                return e.dma_start(dst, src, accum_op=ALU.add)
            return e.dma_start(dst, src)
        return p.op("pool", f, list(deps) + [p.tok(ch)], inc=ch, amt=16)

    es = ExitStack()
    sems = {}
    for s in ("pe", "act", "dve", "pool", "dl", "dacc"):
        sems[s] = es.enter_context(nc.semaphore("sem_" + s))

    def flush():
        with nc.Block() as block:
            for name, deco in (("sync", block.sync), ("pe", block.tensor),
                               ("act", block.scalar), ("dve", block.vector),
                               ("pool", block.gpsimd)):
                ops = p.ops[name]
                if not ops:
                    continue

                def make(ops=ops):
                    def run(eng):
                        for fn, waits, incs in ops:
                            for s, v in waits:
                                eng.wait_ge(sems[s], v)
                            ins = fn(eng)
                            for s, a in incs:
                                ins.then_inc(sems[s], a)
                    return run
                deco(make())
        p.ops = {e: [] for e in p.ENG}

    def SB(st, name, shape, dt=F32, side=None):
        if side:
            return st.enter_context(nc.sbuf_tensor(name, list(shape), dt, side=side))
        return st.enter_context(nc.sbuf_tensor(name, list(shape), dt))

    class PsA:
        def __init__(self, st, tag):
            self.st = st
            self.tag = tag
            self.n = 0

        def alloc(self, name, shape, dt=F32):
            self.n += 1
            assert self.n <= 8, f"psum overflow {self.tag}"
            return self.st.enter_context(nc.psum_tensor(name, list(shape), dt))

    class LNKit:
        def __init__(self, st, tag, ps, pbc, bpbc, ps12=None, bps12=None):
            self.Rb = SB(st, f"{tag}_Rb", [128, 512], BF16)
            self.Rsq = SB(st, f"{tag}_Rsq", [128, 512], BF16)
            self.rA = SB(st, f"{tag}_rA", [1, 512])
            self.rB = SB(st, f"{tag}_rB", [1, 512])
            self.rC = SB(st, f"{tag}_rC", [1, 512])
            self.rowb = SB(st, f"{tag}_rowb", [1, 1024], BF16)
            if ps12 is None:
                self.ps1 = ps.alloc(f"{tag}_ps1", [1, 512])[:]
                self.ps2 = ps.alloc(f"{tag}_ps2", [1, 512])[:]
                self.bps1 = Buf()
                self.bps2 = Buf()
            else:
                self.ps1, self.ps2 = ps12
                self.bps1, self.bps2 = bps12
            self.pbc = pbc
            self.bpbc = bpbc
            self.bRb = Buf()
            self.bRsq = Buf()
            self.brow = Buf()

        def stats(self, src_ap, bsrc, dc, ones_c):
            t1 = dve_copy(self.Rb[:], src_ap, bsrc.rd() + self.bRb.wr())
            self.bRb.setw(t1)
            bsrc.addr(t1)
            t2 = act(AF.Square, self.Rsq[:], src_ap, bsrc.rd() + self.bRsq.wr())
            self.bRsq.setw(t2)
            bsrc.addr(t2)
            m1 = pe_mm(self.ps1, self.ones_c[:], self.Rb[:],
                       [t1] + (self.bps1.wr() if dc == 0 else []),
                       start=(dc == 0), stop=(dc == NC - 1), inc="pe")
            self.bps1.setw(m1)
            self.bRb.addr(m1)
            m2 = pe_mm(self.ps2, self.ones_c[:], self.Rsq[:],
                       [t2] + (self.bps2.wr() if dc == 0 else []),
                       start=(dc == 0), stop=(dc == NC - 1), inc="pe")
            self.bps2.setw(m2)
            self.bRsq.addr(m2)

        def finish(self, ones_r, eps=1e-5):
            A_, B_, C_ = self.rA, self.rB, self.rC
            d1 = dve_ts(A_[:], self.ps1, 1.0 / D, None, ALU.mult, ALU.bypass,
                        [self.bps1.w] + self.brow.wr())
            d2 = dve_ts(B_[:], self.ps2, 1.0 / D, None, ALU.mult, ALU.bypass,
                        [self.bps2.w, d1])
            self.bps1.addr(d1)
            self.bps2.addr(d2)
            d3 = dve_tt(ALU.mult, C_[:], A_[:], A_[:], [d2])
            d4 = dve_tt(ALU.subtract, B_[:], B_[:], C_[:], [d3])
            a1 = act(AF.Ln, C_[:], B_[:], [d4],
                     bias=self.fconst[0:1, 1:2])
            d5 = act(AF.Exp, B_[:], C_[:], [a1], scale=-0.5)
            d6 = dve_tt(ALU.mult, C_[:], A_[:], B_[:], [d5])
            d7 = dve_copy(self.rowb[0:1, 0:512], B_[:], [d6])
            d8 = dve_copy(self.rowb[0:1, 512:1024], C_[:], [d7])
            self.brow.setw(d8)
            t1 = pe_mm(self.pbc[0][:], self.ones_r[0:1, 0:128],
                       self.rowb[0:1, 0:512],
                       [d8] + self.bpbc[0].wr(), inc="pe")
            t2 = pe_mm(self.pbc[1][:], self.ones_r[0:1, 0:128],
                       self.rowb[0:1, 512:1024],
                       [d8] + self.bpbc[1].wr(), inc="pe")
            self.bpbc[0].setw(t1)
            self.bpbc[1].setw(t2)
            self.brow.addr(t1)
            self.brow.addr(t2)

    with es:
        triu = SB(es, "triu_b", [128, 128], BF16)
        ones_r = SB(es, "ones_r", [1, 128], BF16)
        ones_c = SB(es, "ones_c", [128, 1], BF16)
        LNKit.ones_r = ones_r
        LNKit.ones_c = ones_c
        omA = SB(es, "omA_b", [128, 64], BF16)
        boP = SB(es, "boP_b", [128, 8])
        boFP = SB(es, "boFP_b", [128, 8])
        bqkvP = SB(es, "bqkvP_b", [128, 24])
        ln1wP = SB(es, "ln1wP_b", [128, 8])
        ln1bP = SB(es, "ln1bP_b", [128, 8])
        ln2wP = SB(es, "ln2wP_b", [128, 8])
        ln2bP = SB(es, "ln2bP_b", [128, 8])
        gateWb = SB(es, "gateWb", [128, 64], BF16)
        fconst = SB(es, "fconst_b", [128, 3])
        LNKit.fconst = None
        g_tm = SB(es, "g_tm", [128, NT128 * 8])
        b_small = Buf()
        b_gtm = Buf()

        stg0 = SB(es, "stg0", [128, 64])
        bstg0 = Buf()

        LNKit.fconst = fconst
        for dst, src in ((boP, boP_d), (boFP, boFP_d), (bqkvP, bqkvP_d),
                         (ln1wP, ln1wP_d), (ln1bP, ln1bP_d),
                         (ln2wP, ln2wP_d), (ln2bP, ln2bP_d), (triu, triu_d),
                         (fconst, fconst_d)):
            b_small.addr(ldma(dst[:], src, []))
        t1 = ldma(stg0[:, 0:64], omA_d, [])
        bstg0.setw(t1)
        tc = dve_copy(omA[:], stg0[:, 0:64], [t1])
        bstg0.addr(tc)
        b_small.addr(tc)
        t1 = ldma(stg0[:, 0:64], gateWP_d, bstg0.wr())
        bstg0.setw(t1)
        tc = dve_copy(gateWb[:], stg0[:, 0:64], [t1])
        bstg0.addr(tc)
        b_small.addr(tc)
        b_small.addr(dve_memset(ones_r[:], 1.0, []))
        b_small.addr(dve_memset(ones_c[:], 1.0, []))

        esO = ExitStack()
        with esO:
            es1 = ExitStack()
            with es1:
                NX1 = [SB(es1, f"NX1_{c}", [128, NTOK], BF16) for c in range(NC)]
                bNX1 = [Buf() for _ in range(NC)]

                esA = ExitStack()
                with esA:
                    wQ = SB(esA, "wQ", [128, NC * D], BF16)
                    wK = SB(esA, "wK", [128, NC * D], BF16)
                    wV = SB(esA, "wV", [128, NC * D], BF16)
                    wO = SB(esA, "wO", [128, NC * D], BF16)
                    xb = SB(esA, "xb", [128, NC * 512], BF16)
                    xstg = SB(esA, "xstg", [128, 2048])
                    combq = [[SB(esA, f"combq{g}{i}", [128, 512], BF16)
                              for i in range(2)] for g in range(2)]
                    combk = [[SB(esA, f"combk{g}{i}", [128, 512], BF16)
                              for i in range(2)] for g in range(2)]
                    fkT = [[SB(esA, f"fkT{g}{i}", [128, 128], BF16)
                            for i in range(4)] for g in range(2)]
                    fq = [[combq[g][jj][0:64, :] for jj in range(2)]
                          for g in range(2)]
                    fk = [[combk[g][jj][0:64, :] for jj in range(2)]
                          for g in range(2)]
                    amS = SB(esA, "amS", [128, 256], BF16)
                    v2 = [SB(esA, f"v2_{i}", [128, H * 65], BF16) for i in range(4)]
                    SRUN = SB(esA, "SRUN", [64, H * 65])
                    Ssb = [SB(esA, f"Ssb{g}", [64, 2 * 4 * 65], BF16)
                           for g in range(2)]
                    srun_bf = [SB(esA, f"srun_bf{g}", [64, 2 * 65], BF16)
                               for g in range(2)]
                    csc = [SB(esA, f"csc{i}", [128, 512], BF16) for i in range(NC)]
                    RES = SB(esA, "RES", [128, NC * 512])
                    APL = SB(esA, "APL", [128, 512])

                    psa = PsA(esA, "A")
                    pq = psa.alloc("pq", [128, 512])
                    pk = psa.alloc("pk", [128, 512])
                    Apack = psa.alloc("Apack", [128, 512])
                    pA_ = Apack[:, 0:128]
                    po = [psa.alloc("poT0", [65, 128])[:],
                          psa.alloc("poT1", [65, 128])[:]]
                    phiA = pq[0:64, 0:512]
                    phiB = pk[0:64, 0:512]
                    ptmA = pq[:, 0:64]
                    ptmB = pk[:, 0:64]
                    pwo = [psa.alloc(f"pwo{i}", [128, 512]) for i in range(2)]
                    pScT = psa.alloc("pScT", [64, 65])
                    pSc = [pScT[:], pwo[0][0:64, 0:65]]
                    pzb = pwo[1][0:64, 0:128]
                    bpwo = None

                    bxb, bAPL = (Buf() for _ in range(2))
                    bSsb = [[Buf(), Buf()], [Buf(), Buf()]]
                    bsrun_bf = [[Buf(), Buf()], [Buf(), Buf()]]
                    bxstg = [Buf(), Buf()]
                    bw = Buf()
                    bcombq = [[Buf(), Buf()], [Buf(), Buf()]]
                    bcombk = [[Buf(), Buf()], [Buf(), Buf()]]
                    bfq = bcombq
                    bfk = bcombk
                    bam = Buf()
                    bfkT = [[Buf() for _ in range(4)] for _ in range(2)]
                    bv2 = [Buf() for _ in range(4)]
                    bSRUN = [Buf() for _ in range(H)]
                    bcsc = [Buf() for _ in range(NC)]
                    bRES = [Buf() for _ in range(NC)]
                    bpq, bpk, bpA = (Buf() for _ in range(3))
                    bpo = [Buf(), Buf()]
                    bpwo = [Buf(), Buf()]
                    bpzb = bpwo[1]
                    bpSc = [Buf(), bpwo[0]]
                    kit = LNKit(esA, "A", psa, pwo, bpwo,
                                ps12=(pq[0:1, 0:512], pk[0:1, 0:512]),
                                bps12=(bpq, bpk))
                    bzrs = kit.brow
                    bzrecb = kit.brow
                    zrs = kit.rA[0:1, 0:128]
                    zrecb = kit.rowb.bitcast(BF16)[0:1, 0:128] if False else \
                        kit.rowb[0:1, 0:128]

                    wi = 0
                    for wsb, wd in ((wQ, wqT_d), (wK, wkT_d), (wV, wvT_d),
                                    (wO, woT_d)):
                        wr_ = wd.rearrange("(c p) i -> p c i", p=128)
                        for c in range(NC):
                            j = wi % 2
                            wi += 1
                            sl_ = xstg[:, j * 1024:j * 1024 + 1024]
                            td = ldma(sl_, wr_[:, c, :], bxstg[j].wr())
                            bxstg[j].setw(td)
                            tcw = dve_copy(wsb[:, c * D:(c + 1) * D], sl_, [td])
                            bxstg[j].addr(tcw)
                            bw.addr(tcw)

                    for i in range(4):
                        bv2[i].setw(dve_memset(v2[i][:], 1.0, []))
                    tz = dve_memset(SRUN[:], 0.0, [])
                    for h in range(H):
                        bSRUN[h].setw(tz)

                    for b in range(NB if PHASES >= 1 else 0):
                        T0 = b * 512
                        for hf in range(2):
                            sl_ = xstg[:].rearrange("p (c t) -> p c t", c=4)
                            td = ldma(sl_, xT_r[:, hf * 4:hf * 4 + 4, T0:T0 + 512],
                                      bxstg[0].wr() + bxstg[1].wr())
                            bxstg[0].setw(td)
                            bxstg[1].setw(td)
                            tcx = dve_copy(xb[:, hf * 2048:hf * 2048 + 2048],
                                           xstg[:],
                                           [td] + (bxb.wr() if hf == 0 else []))
                            bxb.setw(tcx)
                            bxstg[0].addr(tcx)
                            bxstg[1].addr(tcx)

                        for r in range(4 if APART >= 1 else 0):
                            for half in range(2):
                                pv = pwo[half]
                                bp = bpwo[half]
                                deps = bxb.rd() + bp.wr() + list(bw.rs)
                                tm = None
                                for kc in range(NC):
                                    tm = pe_mm(
                                        pv[:],
                                        xb[:, kc * 512 + r * 128:kc * 512 + r * 128 + 128],
                                        wV[:, kc * D + half * 512:kc * D + half * 512 + 512],
                                        deps if kc == 0 else [],
                                        start=(kc == 0), stop=(kc == NC - 1),
                                        inc="pe" if kc == NC - 1 else None)
                                bp.setw(tm)
                                bxb.addr(tm)
                                dst = v2[r][:].rearrange("p (h x) -> p h x", h=H)[
                                    :, half * 8:half * 8 + 8, 0:64]
                                src = pv[:].rearrange("p (h x) -> p h x", x=64)
                                te = dve_copy(dst, src, [tm] + bv2[r].wr())
                                bv2[r].setw(te)
                                bp.addr(te)

                        def prep(hp):
                            g = hp % 2
                            h0, h1 = 2 * hp, 2 * hp + 1
                            for pp_, bpp_, wsb in ((pq, bpq, wQ), (pk, bpk, wK)):
                                deps = bxb.rd() + bpp_.wr()
                                tm = None
                                for kc in range(NC):
                                    tm = pe_mm(
                                        pp_[:],
                                        wsb[:, kc * D + hp * 128:kc * D + hp * 128 + 128],
                                        xb[:, kc * 512:kc * 512 + 512],
                                        deps if kc == 0 else [],
                                        start=(kc == 0), stop=(kc == NC - 1),
                                        inc="pe" if kc == NC - 1 else None)
                                bpp_.setw(tm)
                                bxb.addr(tm)
                            for pp_, bpp_, comb, bcomb in (
                                    (pq, bpq, combq[g], bcombq[g]),
                                    (pk, bpk, combk[g], bcombk[g])):
                                for j in range(2):
                                    t1 = dve_copy(comb[j][0:64, :],
                                                  pp_[j * 64:j * 64 + 64, :],
                                                  [bpp_.w] + bcomb[j].wr())
                                    t2 = act(AF.Square, comb[j][64:128, :],
                                             pp_[j * 64:j * 64 + 64, :],
                                             [bpp_.w] + bcomb[j].wr() + [t1])
                                    bcomb[j].setw(t2)
                                    bcomb[j].addr(t1)
                                    bpp_.addr(t1)
                                    bpp_.addr(t2)
                            for comb, bcomb, f_, php, bphp in (
                                    (combq[g][0], bcombq[g][0], fq[g][0], phiA, bpq),
                                    (combq[g][1], bcombq[g][1], fq[g][1], phiB, bpk)):
                                tm = pe_mm(php, omA[:], comb[:],
                                           bcomb.rd() + list(bcomb.rs) +
                                           bphp.wr() + [b_small.w], inc="pe")
                                bphp.setw(tm)
                                bcomb.addr(tm)
                                te = act(AF.Exp, f_, php,
                                         [tm] + bcomb.wr(),
                                         bias=fconst[0:64, 0:1])
                                bcomb.setw(te)
                                bphp.addr(te)
                            for j in range(2):
                                for r in range(4):
                                    even = ((j * 4 + r) % 2 == 0)
                                    ptm = ptmA if even else ptmB
                                    bptm = bpq if even else bpk
                                    tt_ = pe_mm(ptm,
                                                combk[g][j][:, r * 128:r * 128 + 128],
                                                omA[:],
                                                bcombk[g][j].rd() +
                                                list(bcombk[g][j].rs) +
                                                bptm.wr() + [b_small.w], inc="pe")
                                    bptm.setw(tt_)
                                    bcombk[g][j].addr(tt_)
                                    tc2 = act(AF.Exp,
                                              fkT[g][r][:, j * 64:j * 64 + 64],
                                              ptm, [tt_] + bfkT[g][r].wr(),
                                              bias=fconst[:, 0:1])
                                    bfkT[g][r].setw(tc2)
                                    bptm.addr(tc2)
                            for comb, bcomb, f_, php, bphp in (
                                    (combk[g][0], bcombk[g][0], fk[g][0], phiA, bpq),
                                    (combk[g][1], bcombk[g][1], fk[g][1], phiB, bpk)):
                                tm = pe_mm(php, omA[:], comb[:],
                                           bcomb.rd() + list(bcomb.rs) +
                                           bphp.wr() + [b_small.w], inc="pe")
                                bphp.setw(tm)
                                bcomb.addr(tm)
                                te = act(AF.Exp, f_, php,
                                         [tm] + bcomb.wr(),
                                         bias=fconst[0:64, 0:1])
                                bcomb.setw(te)
                                bphp.addr(te)
                            for j, h in enumerate((h0, h1)):
                                srun_sl = SRUN[:, h * 65:h * 65 + 65]
                                sbsl = srun_bf[g][:, j * 65:j * 65 + 65]
                                t40 = dve_copy(sbsl, srun_sl,
                                               bSRUN[h].rd() + bsrun_bf[g][j].wr())
                                bsrun_bf[g][j].setw(t40)
                                bSRUN[h].addr(t40)
                                for c in range(4):
                                    pscb = pSc[c % 2]
                                    bpscb = bpSc[c % 2]
                                    tS = pe_mm(pscb,
                                               fkT[g][c][:, j * 64:j * 64 + 64],
                                               v2[c][:, h * 65:h * 65 + 65],
                                               bfkT[g][c].rd() + bv2[c].rd() +
                                               bpscb.wr(), inc="pe")
                                    bpscb.setw(tS)
                                    bfkT[g][c].addr(tS)
                                    bv2[c].addr(tS)
                                    ssl = Ssb[g][:, (j * 4 + c) * 65:
                                                (j * 4 + c) * 65 + 65]
                                    tE = dve_copy(ssl, pscb,
                                                  [tS] + bSsb[g][j].wr())
                                    bSsb[g][j].setw(tE)
                                    bpscb.addr(tE)
                                    tU = dve_tt(ALU.add, srun_sl, srun_sl, pscb,
                                                [tS, tE] + bSRUN[h].wr())
                                    bSRUN[h].setw(tU)
                                    bpscb.addr(tU)

                        def chunk_stage(hp):
                            g = hp % 2
                            h0, h1 = 2 * hp, 2 * hp + 1
                            for j, h in enumerate((h0, h1)):

                                def emit_A(c, j=j, h=h):
                                    ta = pe_mm(pA_,
                                               fk[g][j][:, c * 128:c * 128 + 128],
                                               fq[g][j][:, c * 128:c * 128 + 128],
                                               bfk[g][j].rd() + bfq[g][j].rd() +
                                               bpA.wr(), inc="pe")
                                    bpA.setw(ta)
                                    bfk[g][j].addr(ta)
                                    bfq[g][j].addr(ta)
                                    am = amS[:, (c % 2) * 128:
                                             (c % 2) * 128 + 128]
                                    tmsk = dve_tt(ALU.mult, am, pA_, triu[:],
                                                  [ta] + bam.wr())
                                    bam.setw(tmsk)
                                    bam.addr(tmsk)
                                    bpA.addr(tmsk)
                                    return am, tmsk

                                def emit_pv(c, am, tmsk, j=j, h=h):
                                    pob = po[c % 2]
                                    bpob = bpo[c % 2]
                                    vsl = v2[c][:, h * 65:h * 65 + 65]
                                    pe_mm(pob, vsl, am,
                                          [tmsk] + bv2[c].rd() + bpob.wr(),
                                          start=True, stop=False, inc=None)
                                    bam.addr(None)
                                    deps = bsrun_bf[g][j].rd() + bSsb[g][j].rd()
                                    t2_ = pe_mm(pob,
                                                srun_bf[g][:, j * 65:j * 65 + 65],
                                                fq[g][j][:, c * 128:c * 128 + 128],
                                                deps + bfq[g][j].rd(),
                                                start=False, stop=(c == 0),
                                                inc="pe" if c == 0 else None)
                                    for cp in range(c):
                                        t2_ = pe_mm(
                                            pob,
                                            Ssb[g][:, (j * 4 + cp) * 65:
                                                   (j * 4 + cp) * 65 + 65],
                                            fq[g][j][:, c * 128:c * 128 + 128],
                                            [], start=False, stop=(cp == c - 1),
                                            inc="pe" if cp == c - 1 else None)
                                    bpob.setw(t2_)
                                    bsrun_bf[g][j].addr(t2_)
                                    bSsb[g][j].addr(t2_)
                                    bfq[g][j].addr(t2_)
                                    bv2[c].addr(t2_)
                                    bam.addr(t2_)
                                    return t2_

                                def emit_z(c, t2_, j=j, h=h):
                                    pob = po[c % 2]
                                    bpob = bpo[c % 2]
                                    t5 = act(AF.Ln, zrs, pob[64:65, :],
                                             [t2_, b_small.w] + bzrs.wr(),
                                             bias=fconst[0:1, 2:3])
                                    bzrs.setw(t5)
                                    bpob.addr(t5)
                                    t7 = act(AF.Exp, zrecb, zrs,
                                             [t5] + bzrecb.wr(), scale=-1.0)
                                    bzrs.addr(t7)
                                    bzrecb.setw(t7)
                                    t8 = pe_mm(pzb, ones_r[0:1, 0:64], zrecb,
                                               [t7] + bpzb.wr(), inc="pe")
                                    bpzb.setw(t8)
                                    bzrecb.addr(t8)
                                    osb = combk[g][j][64:128, (c % 2) * 128:
                                                      (c % 2) * 128 + 128]
                                    t9 = dve_copy(osb, pob[0:64, :],
                                                  [t2_] + bcombk[g][j].wr())
                                    bcombk[g][j].setw(t9)
                                    bcombk[g][j].addr(t9)
                                    bpob.addr(t9)
                                    dst = csc[hp][j * 64:j * 64 + 64,
                                                  c * 128:c * 128 + 128]
                                    t10 = dve_tt(ALU.mult, dst, osb, pzb,
                                                 [t9, t8] + bcsc[hp].wr())
                                    bcsc[hp].setw(t10)
                                    bpzb.addr(t10)
                                    bcombk[g][j].addr(t10)

                                am0, tm0 = emit_A(0)
                                am1, tm1 = emit_A(1)
                                t2a = emit_pv(0, am0, tm0)
                                am2, tm2 = emit_A(2)
                                t2b = emit_pv(1, am1, tm1)
                                emit_z(0, t2a)
                                am3, tm3 = emit_A(3)
                                t2c = emit_pv(2, am2, tm2)
                                emit_z(1, t2b)
                                t2d = emit_pv(3, am3, tm3)
                                emit_z(2, t2c)
                                emit_z(3, t2d)

                        if APART >= 2:
                            prep(0)
                            for hp in range(8):
                                if hp + 1 < 8:
                                    prep(hp + 1)
                                if APART >= 5:
                                    chunk_stage(hp)
                        for dc in range(NC if APART >= 6 else 0):
                            j = dc % 2
                            if dc % 4 == 0:
                                hf = dc // 4
                                sl_ = xstg[:].rearrange("p (c t) -> p c t", c=4)
                                tdx4 = ldma(sl_,
                                            xT_r[:, hf * 4:hf * 4 + 4, T0:T0 + 512],
                                            bxstg[0].wr() + bxstg[1].wr())
                                bxstg[0].setw(tdx4)
                                bxstg[1].setw(tdx4)
                            tdx = tdx4
                            pv = pwo[j]
                            bp = bpwo[j]
                            tm = None
                            for ic in range(NC):
                                deps2 = ((bp.wr() + list(bw.rs)) if ic == 0 else []) \
                                    + bcsc[ic].rd()
                                tm = pe_mm(
                                    pv[:],
                                    wO[:, ic * D + dc * 128:ic * D + dc * 128 + 128],
                                    csc[ic][:, 0:512], deps2,
                                    start=(ic == 0), stop=(ic == NC - 1),
                                    inc="pe" if ic == NC - 1 else None)
                            bp.setw(tm)
                            for ic in range(NC):
                                bcsc[ic].addr(tm)
                            tr_ = dve_stt(RES[:, dc * 512:dc * 512 + 512], pv[:],
                                          boP[:, dc:dc + 1],
                                          xstg[:, (dc % 4) * 512:(dc % 4) * 512 + 512],
                                          ALU.add, ALU.add,
                                          [tm, tdx, b_small.w] + bRES[dc].wr())
                            bRES[dc].setw(tr_)
                            bp.addr(tr_)
                            bxstg[0].addr(tr_)
                            bxstg[1].addr(tr_)
                        for dc in range(NC if APART >= 6 else 0):
                            kit.stats(RES[:, dc * 512:dc * 512 + 512], bRES[dc], dc, ones_c)
                        if APART >= 6:
                            kit.finish(ones_r)
                        for dc in range(NC if APART >= 6 else 0):
                            rsl = RES[:, dc * 512:dc * 512 + 512]
                            t1 = dve_tt(ALU.mult, rsl, rsl,
                                        kit.pbc[0][:],
                                        bRES[dc].wr() + [kit.bpbc[0].w])
                            t2 = dve_tt(ALU.subtract, rsl, rsl,
                                        kit.pbc[1][:], [t1, kit.bpbc[1].w])
                            bRES[dc].setw(t2)
                            kit.bpbc[0].addr(t2)
                            kit.bpbc[1].addr(t2)
                        ts_ = sdma(y1S_r[:, :, T0:T0 + 512],
                                   RES[:].rearrange("p (c t) -> p c t", c=NC),
                                   [b.w for b in bRES])
                        for dc in range(NC):
                            bRES[dc].addr(ts_)
                        for dc in range(NC if APART >= 6 else 0):
                            kit.stats(RES[:, dc * 512:dc * 512 + 512], bRES[dc], dc, ones_c)
                        if APART >= 6:
                            kit.finish(ones_r)
                        for dc in range(NC if APART >= 6 else 0):
                            t1 = dve_tt(ALU.mult, APL[:],
                                        RES[:, dc * 512:dc * 512 + 512], kit.pbc[0][:],
                                        bRES[dc].rd() + [kit.bpbc[0].w] + bAPL.wr())
                            t2 = dve_tt(ALU.subtract, APL[:], APL[:], kit.pbc[1][:],
                                        [t1, kit.bpbc[1].w])
                            bRES[dc].addr(t2)
                            kit.bpbc[0].addr(t2)
                            kit.bpbc[1].addr(t2)
                            t3 = dve_ts(NX1[dc][:, T0:T0 + 512], APL[:],
                                        ln1wP[:, dc:dc + 1], ln1bP[:, dc:dc + 1],
                                        ALU.mult, ALU.add,
                                        [t2, b_small.w] + bNX1[dc].wr())
                            bAPL.setw(t3)
                            bNX1[dc].setw(t3)
                    flush()
                    p.barrier()

                O_sb = [SB(esO, f"O_{c}", [128, NTOK], BF16, side="right")
                        for c in range(NC)]
                bO = [Buf() for _ in range(NC)]
                esB2 = ExitStack()
                with esB2:
                    Qg = [SB(esB2, f"Qg{i}", [128, NTOK], BF16) for i in range(2)]
                    Kg = [SB(esB2, f"Kg{i}", [128, NTOK], BF16) for i in range(2)]
                    Vg = [SB(esB2, f"Vg{i}", [128, 4 * 65], BF16)
                          for i in range(NT128)]
                    wg = SB(esB2, "wg", [128, NC * 768], BF16)
                    bqkvV = SB(esB2, "bqkvV_b", [1, D], BF16)
                    stgB = [SB(esB2, f"stgB{i}", [128, 512]) for i in range(2)]
                    P_sb = [SB(esB2, f"P_{i}", [128, 512], BF16) for i in range(3)]
                    osb2 = SB(esB2, "osb2", [64, 512], BF16)
                    zrs2 = SB(esB2, "zrs2", [1, 512])
                    zrecb2 = SB(esB2, "zrecb2", [1, 512], BF16)
                    bQg = [Buf(), Buf()]
                    bKg = [Buf(), Buf()]
                    bVg = [Buf() for _ in range(NT128)]
                    bwg = Buf()
                    bbqkvV = Buf()
                    bstgB = [Buf(), Buf()]
                    bP = [Buf(), Buf(), Buf()]
                    bosb2, bzrs2, bzrecb2 = Buf(), Buf(), Buf()

                    psb = PsA(esB2, "B")
                    pqk = [psb.alloc(f"pqk{i}", [128, 512]) for i in range(3)]
                    po2 = psb.alloc("po2", [65, 512])
                    pzb2 = psb.alloc("pzb2", [64, 512])
                    pvtm = psb.alloc("pvtm", [128, 256])
                    bpqk = [Buf(), Buf(), Buf()]
                    bpo2, bpzb2, bpvtm = Buf(), Buf(), Buf()

                    td = ldma(stgB[0][0:1, 0:512], bqkvV_d[0:1, 0:512],
                              bstgB[0].wr())
                    bstgB[0].setw(td)
                    tc0 = dve_copy(bqkvV[0:1, 0:512], stgB[0][0:1, 0:512], [td])
                    bstgB[0].addr(tc0)
                    td = ldma(stgB[1][0:1, 0:512], bqkvV_d[0:1, 512:1024],
                              bstgB[1].wr())
                    bstgB[1].setw(td)
                    tc = dve_copy(bqkvV[0:1, 512:1024], stgB[1][0:1, 0:512],
                                  [td, tc0])
                    bstgB[0].addr(tc)
                    bbqkvV.setw(tc)

                    for i in range(NT128):
                        bVg[i].setw(dve_memset(Vg[i][:], 1.0, []))

                    Wq_r = Wqkv_d.rearrange("(c p) o -> p c o", p=128)
                    for g in range(4 if PHASES >= 2 else 0):
                        for part, coff in ((0, g * 256), (1, D + g * 256),
                                           (2, 2 * D + g * 256)):
                            for qc in range(4):
                                j = qc % 2
                                td = ldma(
                                    stgB[j][:].rearrange("p (c o) -> p c o", c=2),
                                    Wq_r[:, qc * 2:qc * 2 + 2, coff:coff + 256],
                                    bstgB[j].wr())
                                bstgB[j].setw(td)
                                dst = wg[:].rearrange("p (c o) -> p c o", o=768)[
                                    :, qc * 2:qc * 2 + 2,
                                    part * 256:part * 256 + 256]
                                tcw = dve_copy(
                                    dst,
                                    stgB[j][:].rearrange("p (c o) -> p c o", c=2),
                                    [td] + (bwg.wr() if (part == 0 and qc == 0)
                                            else [bwg.w]))
                                bwg.setw(tcw)
                                bstgB[j].addr(tcw)

                        for oc in range(2):
                            cq = g * 2 + oc
                            for dstT, bdstT, wcol, biascol in (
                                    (Qg[oc], bQg[oc], oc * 128, cq),
                                    (Kg[oc], bKg[oc], 256 + oc * 128, 8 + cq)):
                                for tt in range(NB):
                                    pp_ = pqk[tt % 2]
                                    bpp_ = bpqk[tt % 2]
                                    deps = bwg.rd() + bpp_.wr() + \
                                        (bdstT.wr() if tt == 0 else [])
                                    tm = None
                                    for kc in range(NC):
                                        tm = pe_mm(
                                            pp_[:],
                                            wg[:, kc * 768 + wcol:kc * 768 + wcol + 128],
                                            NX1[kc][:, tt * 512:tt * 512 + 512],
                                            (deps + bNX1[kc].rd()) if kc == 0 else [],
                                            start=(kc == 0), stop=(kc == NC - 1),
                                            inc="pe" if kc == NC - 1 else None)
                                        bNX1[kc].addr(tm)
                                    bpp_.setw(tm)
                                    bwg.addr(tm)
                                    te = act_add(
                                        dstT[:, tt * 512:tt * 512 + 512],
                                        pp_[:],
                                        bqkvP[:, biascol:biascol + 1],
                                        [tm, b_small.w])
                                    bdstT.setw(te)
                                    bpp_.addr(te)
                        for t128 in range(NT128):
                            deps = bwg.rd() + bpvtm.wr() + bbqkvV.rd()
                            for kc in range(NC):
                                pe_mm(pvtm[:],
                                      NX1[kc][:, t128 * 128:t128 * 128 + 128],
                                      wg[:, kc * 768 + 512:kc * 768 + 768],
                                      (deps + bNX1[kc].rd()) if kc == 0 else [],
                                      start=(kc == 0), stop=False, inc=None)
                            tm = pe_mm(pvtm[:], ones_r[0:1, 0:128],
                                       bqkvV[0:1, g * 256:g * 256 + 256],
                                       [], start=False, stop=True, inc="pe")
                            bpvtm.setw(tm)
                            bwg.addr(tm)
                            bbqkvV.addr(tm)
                            for kc in range(NC):
                                bNX1[kc].addr(tm)
                            dst = Vg[t128][:].rearrange("p (h x) -> p h x", h=4)[
                                :, :, 0:64]
                            src = pvtm[:].rearrange("p (h x) -> p h x", x=64)
                            te = dve_copy(dst, src, [tm] + bVg[t128].wr())
                            bVg[t128].setw(te)
                            bpvtm.addr(te)

                        for hh in range(4):
                            h = 4 * g + hh
                            oc = hh // 2
                            ro = (hh % 2) * 64
                            tp = (ro, 0)
                            for qt in range(NB):
                                kmax = 4 * qt + 3

                                def emit_score(jj):
                                    pp_ = pqk[jj % 3]
                                    bpp_ = bpqk[jj % 3]
                                    ts_ = pe_mm(
                                        pp_[:],
                                        Kg[oc][ro:ro + 64, jj * 128:jj * 128 + 128],
                                        Qg[oc][ro:ro + 64, qt * 512:qt * 512 + 512],
                                        bKg[oc].rd() + bQg[oc].rd() + bpp_.wr(),
                                        inc="pe", tp=tp)
                                    bpp_.setw(ts_)
                                    bKg[oc].addr(ts_)
                                    bQg[oc].addr(ts_)
                                    Pb = P_sb[jj % 3]
                                    bPb = bP[jj % 3]
                                    if jj < 4 * qt:
                                        te = act(AF.Exp, Pb[:], pp_[:],
                                                 [ts_] + bPb.wr(), scale=0.125)
                                        bPb.setw(te)
                                        bpp_.addr(te)
                                    else:
                                        r = jj - 4 * qt
                                        deps = [ts_] + bPb.wr()
                                        tz2 = None
                                        if r > 0:
                                            tz2 = dve_memset(Pb[:, 0:128 * r], 0.0,
                                                             deps)
                                        te0 = act(AF.Exp, Pb[:, 128 * r:512],
                                                  pp_[:, 128 * r:512], deps,
                                                  scale=0.125)
                                        te = dve_tt(ALU.mult,
                                                    Pb[:, 128 * r:128 * r + 128],
                                                    Pb[:, 128 * r:128 * r + 128],
                                                    triu[:],
                                                    [te0] + ([tz2] if tz2 else []))
                                        bPb.setw(te)
                                        bpp_.addr(te0)

                                def emit_pv(jj):
                                    Pb = P_sb[jj % 3]
                                    bPb = bP[jj % 3]
                                    tpv = pe_mm(po2[:],
                                                Vg[jj][:, hh * 65:hh * 65 + 65],
                                                Pb[:],
                                                bPb.rd() + bVg[jj].rd() +
                                                (bpo2.wr() if jj == 0 else []),
                                                start=(jj == 0), stop=(jj == kmax),
                                                inc="pe")
                                    bpo2.setw(tpv)
                                    bPb.addr(tpv)
                                    bVg[jj].addr(tpv)

                                emit_score(0)
                                if kmax >= 1:
                                    emit_score(1)
                                for jj in range(kmax + 1):
                                    if jj + 2 <= kmax:
                                        emit_score(jj + 2)
                                    emit_pv(jj)
                                tlast = bpo2.w
                                t5 = act(AF.Ln, zrs2[:], po2[64:65, :],
                                         [tlast] + bzrs2.wr())
                                bzrs2.setw(t5)
                                bpo2.addr(t5)
                                t7 = act(AF.Exp, zrecb2[:], zrs2[:],
                                         [t5] + bzrecb2.wr(), scale=-1.0)
                                bzrecb2.setw(t7)
                                bzrs2.addr(t7)
                                t8 = pe_mm(pzb2[:], ones_r[0:1, 0:64], zrecb2[:],
                                           [t7] + bpzb2.wr(), inc="pe")
                                bpzb2.setw(t8)
                                bzrecb2.addr(t8)
                                t9 = dve_copy(osb2[:], po2[0:64, :],
                                              [tlast] + bosb2.wr())
                                bosb2.setw(t9)
                                bpo2.addr(t9)
                                ch = h // 2
                                dst = O_sb[ch][(h % 2) * 64:(h % 2) * 64 + 64,
                                               qt * 512:qt * 512 + 512]
                                t10 = dve_tt(ALU.mult, dst, osb2[:], pzb2[:],
                                             [t9, t8] + bO[ch].wr())
                                bO[ch].setw(t10)
                                bpzb2.addr(t10)
                                bosb2.addr(t10)
                    flush()
                    p.barrier()

            esB3 = ExitStack()
            with esB3:
                woF = SB(esB3, "woF", [128, NC * D], BF16)
                stgC = [SB(esB3, f"stgC{i}", [128, 1024]) for i in range(2)]
                y1sl = SB(esB3, "y1sl", [128, NC * 512])
                X2blk = SB(esB3, "X2blk", [128, NC * 512])
                NX2blk = SB(esB3, "NX2blk", [128, NC * 512], BF16)
                APL2 = SB(esB3, "APL2", [128, 512])
                gsc = SB(esB3, "gsc", [128, 64])
                gbit = SB(esB3, "gbit", [128, 8], BF16)
                gtrow = SB(esB3, "gtrow", [8, 128], BF16)
                iden = SB(esB3, "iden_b", [128, 128], BF16)
                biden = Buf()
                biden.setw(ldma(iden[:], iden_d, []))
                bwoF = Buf()
                bstgC = [Buf(), Buf()]
                by1sl = Buf()
                bX2 = [Buf() for _ in range(NC)]
                bNX2 = [Buf() for _ in range(NC)]
                bAPL2 = Buf()
                bgsc = Buf()
                bgbit = Buf()
                bgtrow = Buf()

                psc = PsA(esB3, "C")
                pwoF = [psc.alloc(f"pwoF{i}", [128, 512]) for i in range(2)]
                bpwoF = [Buf(), Buf()]
                pg = psc.alloc("pg", [128, 8])
                bpg = Buf()
                pgt = psc.alloc("pgt", [8, 128], BF16)
                bpgt = Buf()
                kit2 = LNKit(esB3, "C", psc, pwoF, bpwoF)

                for c in range(NC if PHASES >= 3 else 0):
                    wr_ = WoF_d.rearrange("(c p) o -> p c o", p=128)
                    j = c % 2
                    td = ldma(stgC[j][:], wr_[:, c, :], bstgC[j].wr())
                    bstgC[j].setw(td)
                    tcw = dve_copy(woF[:, c * D:(c + 1) * D],
                                   stgC[j][:], [td] + (bwoF.wr() if c == 0
                                                       else [bwoF.w]))
                    bwoF.setw(tcw)
                    bstgC[j].addr(tcw)

                for tt in range(NB if PHASES >= 3 else 0):
                    T0 = tt * 512
                    tld = ldma(y1sl[:].rearrange("p (c t) -> p c t", c=NC),
                               y1S_r[:, :, T0:T0 + 512], by1sl.wr())
                    by1sl.setw(tld)
                    for dc in range(NC):
                        pv = pwoF[dc % 2]
                        bp = bpwoF[dc % 2]
                        tm = None
                        for ic in range(NC):
                            deps2 = (bp.wr() if ic == 0 else []) + bO[ic].rd()
                            tm = pe_mm(
                                pv[:],
                                woF[:, ic * D + dc * 128:ic * D + dc * 128 + 128],
                                O_sb[ic][:, T0:T0 + 512],
                                deps2 + (bwoF.rd() if ic == 0 else []),
                                start=(ic == 0), stop=(ic == NC - 1),
                                inc="pe" if ic == NC - 1 else None)
                        bp.setw(tm)
                        for ic in range(NC):
                            bO[ic].addr(tm)
                        bwoF.addr(tm)
                        tr_ = dve_stt(X2blk[:, dc * 512:dc * 512 + 512], pv[:],
                                      boFP[:, dc:dc + 1],
                                      y1sl[:, dc * 512:dc * 512 + 512],
                                      ALU.add, ALU.add,
                                      [tm, tld, b_small.w] + bX2[dc].wr())
                        bX2[dc].setw(tr_)
                        bp.addr(tr_)
                        by1sl.addr(tr_)
                    ts_ = sdma(x2S_r[:, :, T0:T0 + 512],
                               X2blk[:].rearrange("p (c t) -> p c t", c=NC),
                               [b.w for b in bX2])
                    for dc in range(NC):
                        bX2[dc].addr(ts_)
                    for dc in range(NC):
                        kit2.stats(X2blk[:, dc * 512:dc * 512 + 512], bX2[dc],
                                   dc, ones_c)
                    kit2.finish(ones_r)
                    for dc in range(NC):
                        t1 = dve_tt(ALU.mult, APL2[:],
                                    X2blk[:, dc * 512:dc * 512 + 512],
                                    kit2.pbc[0][:],
                                    bX2[dc].rd() + [kit2.bpbc[0].w] + bAPL2.wr())
                        t2 = dve_tt(ALU.subtract, APL2[:], APL2[:], kit2.pbc[1][:],
                                    [t1, kit2.bpbc[1].w])
                        bX2[dc].addr(t2)
                        kit2.bpbc[0].addr(t2)
                        kit2.bpbc[1].addr(t2)
                        t3 = dve_ts(NX2blk[:, dc * 512:dc * 512 + 512], APL2[:],
                                    ln2wP[:, dc:dc + 1],
                                    ln2bP[:, dc:dc + 1], ALU.mult, ALU.add,
                                    [t2, b_small.w] + bNX2[dc].wr())
                        bAPL2.setw(t3)
                        bNX2[dc].setw(t3)
                    ts_ = sdma(nx2S_r[:, :, T0:T0 + 512],
                               NX2blk[:].rearrange("p (c t) -> p c t", c=NC),
                               [b.w for b in bNX2])
                    for dc in range(NC):
                        bNX2[dc].addr(ts_)
                    for r in range(4):
                        t128 = tt * 4 + r
                        tm = None
                        deps0 = bpg.wr() + [b_small.w]
                        for kc2 in range(NC):
                            deps0 += bNX2[kc2].rd()
                        for kc in range(NC):
                            tm = pe_mm(pg[:], NX2blk[:, kc * 512 + r * 128:kc * 512 + r * 128 + 128],
                                       gateWb[:, kc * 8:kc * 8 + 8],
                                       deps0 if kc == 0 else [],
                                       start=(kc == 0), stop=(kc == NC - 1),
                                       inc="pe" if kc == NC - 1 else None)
                            bNX2[kc].addr(tm)
                        bpg.setw(tm)
                        G8 = gsc[:, 0:8]
                        m1 = gsc[:, 8:9]
                        nm1 = gsc[:, 9:10]
                        m2 = gsc[:, 10:11]
                        den = gsc[:, 11:12]
                        rden = gsc[:, 12:13]
                        msk = gsc[:, 16:24]
                        g2 = gsc[:, 24:32]
                        ee = gsc[:, 32:40]
                        d0 = dve_copy(G8, pg[:], [tm] + bgsc.wr())
                        bpg.addr(d0)
                        d1 = dve_red(m1, G8, ALU.max, [d0])
                        d1b = dve_red(nm1, G8, ALU.max, [d1], negate=True)
                        d2 = dve_ts(msk, G8, m1, None, ALU.is_ge, ALU.bypass, [d1b])
                        d3 = dve_stt(g2, msk, -1e30, G8, ALU.mult, ALU.add, [d2])
                        d4 = dve_red(m2, g2, ALU.max, [d3])
                        a1 = act(AF.Exp, ee, G8, [d4], bias=nm1)
                        d5 = dve_ts(msk, G8, m2, None, ALU.is_ge, ALU.bypass, [a1])
                        d6 = dve_tt(ALU.mult, ee, ee, msk, [d5])
                        d7 = dve_red(den, ee, ALU.add, [d6])
                        d8 = dve_recip(rden, den, [d7])
                        d9 = dve_ts(g_tm[:, t128 * 8:t128 * 8 + 8], ee, rden, None,
                                    ALU.mult, ALU.bypass, [d8] + b_gtm.wr())
                        b_gtm.setw(d9)
                        bgsc.setw(d9)
                        d10 = dve_copy(gbit[:], g_tm[:, t128 * 8:t128 * 8 + 8],
                                       [d9] + bgbit.wr())
                        bgbit.setw(d10)
                        b_gtm.addr(d10)
                        t12 = pe_tr(pgt[:], gbit[:], iden[:],
                                    [d10] + biden.rd() + bpgt.wr())
                        bpgt.setw(t12)
                        bgbit.addr(t12)
                        d11 = dve_copy(gtrow[:], pgt[:], [t12] + bgtrow.wr())
                        bgtrow.setw(d11)
                        bpgt.addr(d11)
                        tsg = sdma(gTS[:, t128 * 128:t128 * 128 + 128], gtrow[:],
                                   [d11])
                        bgtrow.addr(tsg)
                flush()
                p.barrier()

        esM = ExitStack()
        with esM:
            fc1T = SB(esM, "fc1T", [128, 32 * 1024], BF16)
            fc2T = SB(esM, "fc2T", [128, 32 * 1024], BF16)
            nx2sl = [SB(esM, f"nx2sl{i}", [128, NC * 512], BF16) for i in range(2)]
            stgM = [SB(esM, f"stgM{i}", [128, 2048]) for i in range(2)]
            tmpM = [SB(esM, f"tmpM{i}", [128, 512]) for i in range(4)]
            fc1bT = SB(esM, "fc1bT_b", [128, 256])
            esM2 = ExitStack()
            Hs = [SB(esM2, f"Hs{i}", [128, 512], BF16) for i in range(32)]
            bfc1 = [Buf() for _ in range(32)]
            bfc2 = [Buf() for _ in range(32)]
            bHs = [Buf() for _ in range(32)]
            bnx2sl = [Buf(), Buf()]
            bstgM = [Buf(), Buf()]
            btmpM = [Buf(), Buf(), Buf(), Buf()]
            baccS = Buf()
            bfc1bT = Buf()
            bfc2bB = Buf()
            bgTm = Buf()
            bstgm2 = Buf()

            psm = PsA(esM, "M")
            pf = [psm.alloc(f"pf{i}", [128, 512]) for i in range(2)]
            pD = [psm.alloc(f"pD{i}", [128, 512]) for i in range(4)]
            bpf = [Buf(), Buf()]
            bpD = [Buf() for _ in range(4)]

            bfc1bT.setw(ldma(fc1bT[:], fc1bT_d, []))

            def load_pair(e, k, which, deps_extra, queue=0):
                j = (2 * k + which) % 2
                if which == 0:
                    src = fc1ts_d[e, 2 * k:2 * k + 2].rearrange(
                        "f (c p) g -> p f c g", p=128)
                    dstg = stgM[j][:].rearrange("p (f c g) -> p f c g",
                                                f=2, c=NC)
                    dst = fc1T[:, 2 * k * 1024:(2 * k + 2) * 1024]
                    bsl = [bfc1[2 * k], bfc1[2 * k + 1]]
                else:
                    src = fc2ts_d[e, 2 * k:2 * k + 2].rearrange(
                        "f p d -> p f d")
                    dstg = stgM[j][:].rearrange("p (f d) -> p f d", f=2)
                    dst = fc2T[:, 2 * k * 1024:(2 * k + 2) * 1024]
                    bsl = [bfc2[2 * k], bfc2[2 * k + 1]]
                deps = bstgM[j].wr() + deps_extra
                for b in bsl:
                    deps += b.wr()
                if queue == 0:
                    td = ldma(dstg, src, deps)
                else:
                    td = pdma(dstg, src, deps)
                bstgM[j].setw(td)
                tcw = dve_copy(dst, stgM[j][:], [td])
                bstgM[j].addr(tcw)
                for b in bsl:
                    b.setw(tcw)

            for k in range(16 if PHASES >= 4 else 0):
                load_pair(0, k, 0, [], queue=k % 2)
                load_pair(0, k, 1, [], queue=(k + 1) % 2)

            for e in range(8 if PHASES >= 4 else 0):
                for sl in range(NB):
                    T0 = sl * 512
                    jn = sl % 2
                    if NB == 1 or (e == 0 and sl == 0):
                        tld = ldma(nx2sl[jn][:].rearrange("p (c t) -> p c t", c=NC),
                                   nx2S_r[:, :, T0:T0 + 512], bnx2sl[jn].wr())
                        bnx2sl[jn].setw(tld)
                    else:
                        tld = bnx2sl[jn].w
                    if NB > 1 and (sl < NB - 1 or e < 7):
                        npf = (sl + 1) % NB
                        jn2 = npf % 2
                        T0n = npf * 512
                        tldn = ldma(nx2sl[jn2][:].rearrange("p (c t) -> p c t", c=NC),
                                    nx2S_r[:, :, T0n:T0n + 512], bnx2sl[jn2].wr())
                        bnx2sl[jn2].setw(tldn)
                    def emit_fc1(fc):
                        ppf = pf[fc % 2]
                        bppf = bpf[fc % 2]
                        tm = None
                        for kc in range(NC):
                            tm = pe_mm(
                                ppf[:],
                                fc1T[:, fc * 1024 + kc * 128:fc * 1024 + kc * 128 + 128],
                                nx2sl[jn][:, kc * 512:kc * 512 + 512],
                                ([tld] + bppf.wr() + bfc1[fc].rd()) if kc == 0 else [],
                                start=(kc == 0), stop=(kc == NC - 1),
                                inc="pe" if kc == NC - 1 else None)
                        bppf.setw(tm)
                        bfc1[fc].addr(tm)
                        bnx2sl[jn].addr(tm)
                        if e < 7 and sl == NB - 1 and fc % 2 == 1:
                            load_pair(e + 1, fc // 2, 0,
                                      [tm] + bfc1[fc - 1].rd())
                        tg = act(AF.Gelu, Hs[fc][:], ppf[:],
                                 [tm] + bfc1bT.rd() + bHs[fc].wr(),
                                 bias=fc1bT[:, e * 32 + fc:e * 32 + fc + 1])
                        bHs[fc].setw(tg)
                        bppf.addr(tg)

                    def emit_fc2A(fc):
                        tg = bHs[fc].w
                        for r in range(4):
                            tm2 = pe_mm(pD[r][:], Hs[fc][:, r * 128:r * 128 + 128],
                                        fc2T[:, fc * 1024:fc * 1024 + 512],
                                        [tg] + bfc2[fc].rd() +
                                        (bpD[r].wr() if fc == 0 else []),
                                        start=(fc == 0), stop=(fc == 31),
                                        inc="pe" if (fc == 31 or r == 3) else None)
                            bpD[r].setw(tm2)
                            bHs[fc].addr(tm2)
                            bfc2[fc].addr(tm2)

                    emit_fc1(0)
                    for fc in range(1, 32):
                        emit_fc1(fc)
                        emit_fc2A(fc - 1)
                    emit_fc2A(31)
                    for r in range(4):
                        jt = r
                        t128 = sl * 4 + r
                        gcol = g_tm[:, t128 * 8 + e:t128 * 8 + e + 1]
                        tev = dve_ts(tmpM[jt][:], pD[r][:], gcol, None, ALU.mult,
                                     ALU.bypass,
                                     [bpD[r].w, b_gtm.w] + btmpM[jt].wr())
                        btmpM[jt].setw(tev)
                        bpD[r].addr(tev)
                        b_gtm.addr(tev)
                        tdm = pdma(accS[T0 + r * 128:T0 + r * 128 + 128, 0:512],
                                   tmpM[jt][:], [tev] + baccS.rd(), accum=(e > 0))
                        btmpM[jt].addr(tdm)
                        baccS.setw(tdm)
                    for r in range(4):
                        for fc in range(32):
                            tm2 = pe_mm(pD[r][:], Hs[fc][:, r * 128:r * 128 + 128],
                                        fc2T[:, fc * 1024 + 512:fc * 1024 + 1024],
                                        (bHs[fc].rd() + bfc2[fc].rd()) +
                                        (bpD[r].wr() if fc == 0 else []),
                                        start=(fc == 0), stop=(fc == 31),
                                        inc="pe" if (fc == 31 or r == 3) else None)
                            bpD[r].setw(tm2)
                            bHs[fc].addr(tm2)
                            bfc2[fc].addr(tm2)
                            if r == 3 and e < 7 and sl == NB - 1 and fc % 2 == 1:
                                load_pair(e + 1, fc // 2, 1,
                                          list(bfc2[fc].rs) +
                                          list(bfc2[fc - 1].rs))
                    for r in range(4):
                        jt = r
                        t128 = sl * 4 + r
                        gcol = g_tm[:, t128 * 8 + e:t128 * 8 + e + 1]
                        tev = dve_ts(tmpM[jt][:], pD[r][:], gcol, None, ALU.mult,
                                     ALU.bypass,
                                     [bpD[r].w, b_gtm.w] + btmpM[jt].wr())
                        btmpM[jt].setw(tev)
                        bpD[r].addr(tev)
                        b_gtm.addr(tev)
                        tdm = pdma(accS[T0 + r * 128:T0 + r * 128 + 128, 512:1024],
                                   tmpM[jt][:], [tev] + baccS.rd(), accum=(e > 0))
                        btmpM[jt].addr(tdm)
                        baccS.setw(tdm)
            flush()
            p.barrier()
            esM2.close()

        esF = ExitStack()
        with esF:
            accsl = SB(esF, "accsl", [128, 4 * 1024])
            idenF = SB(esF, "idenF_b", [128, 128])
            bidenF = Buf()
            bidenF.setw(ldma(idenF[:], idenF_d, []))
            fc2bB = SB(esF, "fc2bB", [8, D], BF16)
            gTm = SB(esF, "gTm", [8, NTOK], BF16)
            stgm2 = SB(esF, "stgm2", [8, D])
            bfc2bB = Buf()
            bgTm = Buf()
            bstgm2 = Buf()
            td = ldma(stgm2[:], fc2b_d, bstgm2.wr())
            bstgm2.setw(td)
            tcb = dve_copy(fc2bB[:], stgm2[:], [td])
            bstgm2.addr(tcb)
            bfc2bB.setw(tcb)
            bgTm.setw(ldma(gTm[:], gTS, []))
            y1sl2 = SB(esF, "y1sl2", [128, NC * 512])
            x2sl2 = SB(esF, "x2sl2", [128, NC * 512])
            R2 = [SB(esF, f"R2_{i}", [128, 512]) for i in range(NC)]
            OUTB = SB(esF, "OUTB", [128, NC * 512])
            baccsl = Buf()
            by1sl2 = Buf()
            bx2sl2 = Buf()
            bR2 = [Buf() for _ in range(NC)]
            bOUTB = Buf()
            psf = PsA(esF, "F")
            ptrF = [psf.alloc("ptrF0", [128, 128]), psf.alloc("ptrF1", [128, 128])]
            bptrF = [Buf(), Buf()]
            pbcF = [psf.alloc(f"pbcF{i}", [128, 512]) for i in range(2)]
            bpbcF = [Buf(), Buf()]
            kit3 = LNKit(esF, "F", psf, pbcF, bpbcF)

            accS_r = accS.rearrange("(s r p) d -> s p r d", p=128, r=4)

            for tt in range(NB if PHASES >= 5 else 0):
                T0 = tt * 512
                ta = ldma(accsl[:].rearrange("p (r d) -> p r d", r=4),
                          accS_r[tt], baccsl.wr())
                baccsl.setw(ta)
                t1_ = ldma(y1sl2[:].rearrange("p (c t) -> p c t", c=NC),
                           y1S_r[:, :, T0:T0 + 512], by1sl2.wr())
                by1sl2.setw(t1_)
                t2_ = ldma(x2sl2[:].rearrange("p (c t) -> p c t", c=NC),
                           x2S_r[:, :, T0:T0 + 512], bx2sl2.wr())
                bx2sl2.setw(t2_)
                for r in range(4):
                    t128 = tt * 4 + r
                    for dc in range(NC):
                        jf = (r * NC + dc) % 2
                        pF = ptrF[jf]
                        bpF = bptrF[jf]
                        ttr = pe_tr(pF[:],
                                    accsl[:, r * 1024 + dc * 128:
                                          r * 1024 + dc * 128 + 128],
                                    idenF[:],
                                    baccsl.rd() + bpF.wr() + bidenF.rd())
                        baccsl.addr(ttr)
                        ttr = pe_mm(pF[:], fc2bB[0:8, dc * 128:dc * 128 + 128],
                                    gTm[:, t128 * 128:t128 * 128 + 128],
                                    [ttr] + bfc2bB.rd() + bgTm.rd(),
                                    start=False, stop=True, inc="pe")
                        bpF.setw(ttr)
                        bfc2bB.addr(ttr)
                        bgTm.addr(ttr)
                        da = dve_tt(ALU.add, R2[dc][:, r * 128:r * 128 + 128],
                                    pF[:],
                                    y1sl2[:, dc * 512 + r * 128:
                                          dc * 512 + r * 128 + 128],
                                    [ttr, t1_] + (bR2[dc].wr() if r == 0
                                                  else [bR2[dc].w]))
                        db = dve_tt(ALU.add, R2[dc][:, r * 128:r * 128 + 128],
                                    R2[dc][:, r * 128:r * 128 + 128],
                                    x2sl2[:, dc * 512 + r * 128:
                                          dc * 512 + r * 128 + 128],
                                    [da, t2_])
                        bR2[dc].setw(db)
                        bpF.addr(db)
                        by1sl2.addr(db)
                        bx2sl2.addr(db)
                for dc in range(NC):
                    kit3.stats(R2[dc][:], bR2[dc], dc, ones_c)
                kit3.finish(ones_r)
                outws = []
                for dc in range(NC):
                    osl = OUTB[:, dc * 512:dc * 512 + 512]
                    t1 = dve_tt(ALU.mult, osl, R2[dc][:], kit3.pbc[0][:],
                                bR2[dc].rd() + [kit3.bpbc[0].w] + bOUTB.wr())
                    t2 = dve_tt(ALU.subtract, osl, osl, kit3.pbc[1][:],
                                [t1, kit3.bpbc[1].w])
                    bOUTB.setw(t2)
                    outws.append(t2)
                    bR2[dc].addr(t2)
                    kit3.bpbc[0].addr(t2)
                    kit3.bpbc[1].addr(t2)
                ts_ = sdma(out_r[:, :, T0:T0 + 512],
                           OUTB[:].rearrange("p (c t) -> p c t", c=NC), outws)
                bOUTB.addr(ts_)
            p.op("dve",
                 lambda e: e.memset(OUTB[0:1, 0:8], 0.0),
                 [p.tok("dl"), p.tok("dacc")])
            flush()

    nc.finalize()
    return nc


def pack_cols(v):
    return np.ascontiguousarray(np.asarray(v, np.float32).reshape(-1, 128).T)


def prep_inputs(inputs, NTOK=4096):
    f32 = lambda a: np.asarray(a, np.float32)
    x = f32(inputs["L0_attn_x"])[:, :NTOK, :]
    omega = f32(inputs["L0_attn_omega"])
    com = {
        "wqT": np.ascontiguousarray(f32(inputs["L0_attn_wq"]).T),
        "wkT": np.ascontiguousarray(f32(inputs["L0_attn_wk"]).T),
        "wvT": np.ascontiguousarray(f32(inputs["L0_attn_wv"]).T),
        "woT": np.ascontiguousarray(f32(inputs["L0_attn_wo"]).T),
        "omA": np.ascontiguousarray(
            np.concatenate([omega.T, np.full((64, 64), -0.5, np.float32)],
                           axis=0)),
        "boP": pack_cols(inputs["L0_attn_bo"]),
        "Wqkv": np.ascontiguousarray(f32(inputs["L0_ffn_Wqkv"])),
        "bqkvP": pack_cols(inputs["L0_ffn_bqkv"]),
        "bqkvV": np.ascontiguousarray(
            f32(inputs["L0_ffn_bqkv"])[2048:].reshape(1, 1024)),
        "WoF": np.ascontiguousarray(f32(inputs["L0_ffn_Wo"])),
        "boFP": pack_cols(inputs["L0_ffn_bo"]),
        "ln1wP": pack_cols(inputs["L0_ffn_ln1_w"]),
        "ln1bP": pack_cols(inputs["L0_ffn_ln1_b"]),
        "ln2wP": pack_cols(inputs["L0_ffn_ln2_w"]),
        "ln2bP": pack_cols(inputs["L0_ffn_ln2_b"]),
        "gateWP": np.ascontiguousarray(
            f32(inputs["L0_ffn_gate_W"]).reshape(8, 128, 8)
            .transpose(1, 0, 2).reshape(128, 64)),
        "fc1ts": np.ascontiguousarray(
            f32(inputs["L0_ffn_fc1_w"]).transpose(0, 2, 1)
            .reshape(8, 1024, 32, 128).transpose(0, 2, 1, 3)),
        "fc2ts": np.ascontiguousarray(
            f32(inputs["L0_ffn_fc2_w"]).transpose(0, 2, 1)
            .reshape(8, 32, 128, 1024)),
        "fc1bT": np.ascontiguousarray(
            f32(inputs["L0_ffn_fc1_b"]).reshape(8, 32, 128)
            .transpose(2, 0, 1).reshape(128, 256)),
        "fc2b": np.ascontiguousarray(f32(inputs["L0_ffn_fc2_b"])),
        "iden": np.eye(128, dtype=ml_dtypes.bfloat16),
        "idenF": np.eye(128, dtype=np.float32),
        "triu": np.triu(np.ones((128, 128))).astype(ml_dtypes.bfloat16),
        "fconst": np.ascontiguousarray(
            np.broadcast_to(np.array([[-np.log(8.0), 1e-5, 1e-6]], np.float32),
                            (128, 3))),
    }
    in_maps = []
    for b in range(x.shape[0]):
        m = dict(com)
        m["xT"] = np.ascontiguousarray(x[b].T)
        in_maps.append(m)
    return in_maps


from concourse.bass_utils import run_bass_kernel_spmd


def kernel(**inputs):
    in_maps = prep_inputs(inputs, 4096)
    nc = build(4096, debug=False)
    res = run_bass_kernel_spmd(nc, in_maps, core_ids=list(range(8)))
    out = np.stack([np.asarray(r["out"], np.float32).T for r in res.results])
    return np.ascontiguousarray(out)


# version 14
# speedup vs baseline: 1.2279x; 1.0282x over previous
import numpy as np
import ml_dtypes
from contextlib import ExitStack

import concourse.bass as bass
import concourse.mybir as mybir

F32 = mybir.dt.float32
BF16 = mybir.dt.bfloat16
AF = mybir.ActivationFunctionType
ALU = mybir.AluOpType
AX = mybir.AxisListType

PHASES = 9
APART = 9
D = 1024
NC = 8
H = 16
LN8 = float(np.log(8.0))


class Buf:
    __slots__ = ("w", "rs")

    def __init__(self):
        self.w = None
        self.rs = []

    def rd(self):
        return [self.w] if self.w else []

    def wr(self):
        d = list(self.rs)
        if self.w:
            d.append(self.w)
        return d

    def setw(self, tok):
        if tok:
            self.w = tok
            self.rs = []

    def addr(self, tok):
        if tok:
            self.rs.append(tok)


class Prog:
    ENG = ("sync", "pe", "act", "dve", "pool")

    def __init__(self):
        self.ops = {e: [] for e in self.ENG}
        self.cnt = {}
        self.seen = {e: {} for e in self.ENG}
        self.floor = []

    def barrier(self):
        self.floor = [(s, c) for s, c in self.cnt.items() if c > 0]

    def tok(self, s):
        return (s, self.cnt.get(s, 0))

    def op(self, eng, fn, deps=(), inc=None, amt=1, nofloor=False):
        need = {}
        dd = list(deps)
        if not nofloor:
            dd += self.floor
        for t in dd:
            if not t:
                continue
            s, v = t
            if v <= self.seen[eng].get(s, 0):
                continue
            if need.get(s, 0) < v:
                need[s] = v
        for s, v in need.items():
            self.seen[eng][s] = v
        tok = None
        incs = []
        if inc:
            self.cnt[inc] = self.cnt.get(inc, 0) + amt
            tok = (inc, self.cnt[inc])
            incs = [(inc, amt)]
        self.ops[eng].append((fn, sorted(need.items()), incs))
        return tok


def build(NTOK=4096, debug=False):
    NB = NTOK // 512
    NT128 = NTOK // 128
    nc = bass.Bass("TRN2", target_bir_lowering=False)
    p = Prog()

    def din(name, shape, dt=F32):
        return nc.dram_tensor(name, list(shape), dt, kind="ExternalInput").ap()

    xT_d = din("xT", [D, NTOK])
    wqT_d = din("wqT", [D, D])
    wkT_d = din("wkT", [D, D])
    wvT_d = din("wvT", [D, D])
    woT_d = din("woT", [D, D])
    omA_d = din("omA", [128, 64])
    boP_d = din("boP", [128, 8])
    Wqkv_d = din("Wqkv", [D, 3 * D])
    bqkvP_d = din("bqkvP", [128, 24])
    bqkvV_d = din("bqkvV", [1, D])
    WoF_d = din("WoF", [D, D])
    boFP_d = din("boFP", [128, 8])
    ln1wP_d = din("ln1wP", [128, 8])
    ln1bP_d = din("ln1bP", [128, 8])
    ln2wP_d = din("ln2wP", [128, 8])
    ln2bP_d = din("ln2bP", [128, 8])
    gateWP_d = din("gateWP", [128, 64])
    fc1ts_d = din("fc1ts", [8, 32, D, 128])
    fc2ts_d = din("fc2ts", [8, 32, 128, D])
    fc1bT_d = din("fc1bT", [128, 256])
    fc2b_d = din("fc2b", [8, D])
    iden_d = din("iden", [128, 128], BF16)
    idenF_d = din("idenF", [128, 128])
    triu_d = din("triu", [128, 128], BF16)
    fconst_d = din("fconst", [128, 3])

    out_d = nc.dram_tensor("out", [D, NTOK], F32, kind="ExternalOutput").ap()
    skind = "ExternalOutput" if debug else "Internal"
    y1S = nc.dram_tensor("y1S", [D, NTOK], F32, kind=skind).ap()
    x2S = nc.dram_tensor("x2S", [D, NTOK], F32, kind=skind).ap()
    nx2S = nc.dram_tensor("nx2S", [D, NTOK], BF16, kind=skind).ap()
    accS = nc.dram_tensor("accS", [NTOK, D], F32, kind=skind).ap()
    gTS = nc.dram_tensor("gTS", [8, NTOK], BF16, kind="Internal").ap()

    xT_r = xT_d.rearrange("(c p) t -> p c t", p=128)
    y1S_r = y1S.rearrange("(c p) t -> p c t", p=128)
    x2S_r = x2S.rearrange("(c p) t -> p c t", p=128)
    nx2S_r = nx2S.rearrange("(c p) t -> p c t", p=128)
    out_r = out_d.rearrange("(c p) t -> p c t", p=128)

    def pe_mm(o, a, b, deps, start=True, stop=True, inc=None, tp=None):
        def f(e, o=o, a=a, b=b, start=start, stop=stop, tp=tp):
            return e.matmul(o, a, b, start=start, stop=stop, tile_position=tp,
                            skip_group_check=True)
        return p.op("pe", f, deps, inc=inc)

    def pe_tr(o, a, ident, deps, inc="pe"):
        def f(e, o=o, a=a, i=ident):
            with nc.allow_low_precision(reason="bf16 transpose of bf16 data"):
                return e.transpose(o, a, i)
        return p.op("pe", f, deps, inc=inc)

    def act(func, o, i, deps, bias=0.0, scale=1.0, inc="act"):
        def f(e, o=o, i=i, func=func, bias=bias, scale=scale):
            return e.activation(o, i, func, bias=bias, scale=scale)
        return p.op("act", f, deps, inc=inc)

    def act_copy(o, i, deps, inc="act"):
        def f(e, o=o, i=i):
            return e.copy(o, i)
        return p.op("act", f, deps, inc=inc)

    def act_add(o, i, b, deps, inc="act"):
        def f(e, o=o, i=i, b=b):
            return e.add(o, i, b)
        return p.op("act", f, deps, inc=inc)

    def dve_copy(o, i, deps, inc="dve"):
        def f(e, o=o, i=i):
            return e.tensor_copy(o, i)
        return p.op("dve", f, deps, inc=inc)

    def dve_tt(op, o, a, b, deps, inc="dve"):
        def f(e, o=o, a=a, b=b, op=op):
            return e.tensor_tensor(o, a, b, op)
        return p.op("dve", f, deps, inc=inc)

    def dve_ts(o, i, s1, s2, op0, op1, deps, inc="dve"):
        def f(e, o=o, i=i, s1=s1, s2=s2, op0=op0, op1=op1):
            return e.tensor_scalar(o, i, s1, s2, op0, op1)
        return p.op("dve", f, deps, inc=inc)

    def dve_stt(o, i, s, t, op0, op1, deps, inc="dve"):
        def f(e, o=o, i=i, s=s, t=t, op0=op0, op1=op1):
            return e.scalar_tensor_tensor(o, i, s, t, op0, op1)
        return p.op("dve", f, deps, inc=inc)

    def dve_red(o, i, op, deps, negate=False, inc="dve"):
        def f(e, o=o, i=i, op=op, negate=negate):
            return e.tensor_reduce(o, i, AX.X, op, negate=negate)
        return p.op("dve", f, deps, inc=inc)

    def dve_recip(o, i, deps, inc="dve"):
        def f(e, o=o, i=i):
            with nc.allow_low_precision(reason="recip rounded to bf16 output"):
                return e.reciprocal(o, i)
        return p.op("dve", f, deps, inc=inc)

    def dve_memset(o, val, deps, inc="dve"):
        def f(e, o=o, val=val):
            return e.memset(o, val)
        return p.op("dve", f, deps, inc=inc)

    def ldma(dst, src, deps, ch="dl", nofloor=False):
        def f(e, dst=dst, src=src):
            return e.dma_start(dst, src)
        return p.op("sync", f, list(deps) + [p.tok(ch)], inc=ch, amt=16,
                    nofloor=nofloor)

    def sdma(dst, src, deps, ch="dacc"):
        def f(e, dst=dst, src=src):
            return e.dma_start(dst, src)
        return p.op("pool", f, list(deps) + [p.tok(ch)], inc=ch, amt=16)

    def pdma(dst, src, deps, accum=False, ch="dacc"):
        def f(e, dst=dst, src=src, accum=accum):
            if accum:
                return e.dma_start(dst, src, accum_op=ALU.add)
            return e.dma_start(dst, src)
        return p.op("pool", f, list(deps) + [p.tok(ch)], inc=ch, amt=16)

    es = ExitStack()
    sems = {}
    for s in ("pe", "act", "dve", "pool", "dl", "dacc"):
        sems[s] = es.enter_context(nc.semaphore("sem_" + s))

    def flush():
        with nc.Block() as block:
            for name, deco in (("sync", block.sync), ("pe", block.tensor),
                               ("act", block.scalar), ("dve", block.vector),
                               ("pool", block.gpsimd)):
                ops = p.ops[name]
                if not ops:
                    continue

                def make(ops=ops):
                    def run(eng):
                        for fn, waits, incs in ops:
                            for s, v in waits:
                                eng.wait_ge(sems[s], v)
                            ins = fn(eng)
                            for s, a in incs:
                                ins.then_inc(sems[s], a)
                    return run
                deco(make())
        p.ops = {e: [] for e in p.ENG}

    def SB(st, name, shape, dt=F32, side=None):
        if side:
            return st.enter_context(nc.sbuf_tensor(name, list(shape), dt, side=side))
        return st.enter_context(nc.sbuf_tensor(name, list(shape), dt))

    class PsA:
        def __init__(self, st, tag):
            self.st = st
            self.tag = tag
            self.n = 0

        def alloc(self, name, shape, dt=F32):
            self.n += 1
            assert self.n <= 8, f"psum overflow {self.tag}"
            return self.st.enter_context(nc.psum_tensor(name, list(shape), dt))

    class LNKit:
        def __init__(self, st, tag, ps, pbc, bpbc, ps12=None, bps12=None):
            self.Rb = SB(st, f"{tag}_Rb", [128, 512], BF16)
            self.Rsq = SB(st, f"{tag}_Rsq", [128, 512], BF16)
            self.rA = SB(st, f"{tag}_rA", [1, 512])
            self.rB = SB(st, f"{tag}_rB", [1, 512])
            self.rC = SB(st, f"{tag}_rC", [1, 512])
            self.rowb = SB(st, f"{tag}_rowb", [1, 1024], BF16)
            if ps12 is None:
                self.ps1 = ps.alloc(f"{tag}_ps1", [1, 512])[:]
                self.ps2 = ps.alloc(f"{tag}_ps2", [1, 512])[:]
                self.bps1 = Buf()
                self.bps2 = Buf()
            else:
                self.ps1, self.ps2 = ps12
                self.bps1, self.bps2 = bps12
            self.pbc = pbc
            self.bpbc = bpbc
            self.bRb = Buf()
            self.bRsq = Buf()
            self.brow = Buf()

        def stats(self, src_ap, bsrc, dc, ones_c):
            t1 = dve_copy(self.Rb[:], src_ap, bsrc.rd() + self.bRb.wr())
            self.bRb.setw(t1)
            bsrc.addr(t1)
            t2 = act(AF.Square, self.Rsq[:], src_ap, bsrc.rd() + self.bRsq.wr())
            self.bRsq.setw(t2)
            bsrc.addr(t2)
            m1 = pe_mm(self.ps1, self.ones_c[:], self.Rb[:],
                       [t1] + (self.bps1.wr() if dc == 0 else []),
                       start=(dc == 0), stop=(dc == NC - 1), inc="pe")
            self.bps1.setw(m1)
            self.bRb.addr(m1)
            m2 = pe_mm(self.ps2, self.ones_c[:], self.Rsq[:],
                       [t2] + (self.bps2.wr() if dc == 0 else []),
                       start=(dc == 0), stop=(dc == NC - 1), inc="pe")
            self.bps2.setw(m2)
            self.bRsq.addr(m2)

        def finish(self, ones_r, eps=1e-5):
            A_, B_, C_ = self.rA, self.rB, self.rC
            d1 = dve_ts(A_[:], self.ps1, 1.0 / D, None, ALU.mult, ALU.bypass,
                        [self.bps1.w] + self.brow.wr())
            d2 = dve_ts(B_[:], self.ps2, 1.0 / D, None, ALU.mult, ALU.bypass,
                        [self.bps2.w, d1])
            self.bps1.addr(d1)
            self.bps2.addr(d2)
            d3 = dve_tt(ALU.mult, C_[:], A_[:], A_[:], [d2])
            d4 = dve_tt(ALU.subtract, B_[:], B_[:], C_[:], [d3])
            a1 = act(AF.Ln, C_[:], B_[:], [d4],
                     bias=self.fconst[0:1, 1:2])
            d5 = act(AF.Exp, B_[:], C_[:], [a1], scale=-0.5)
            d6 = dve_tt(ALU.mult, C_[:], A_[:], B_[:], [d5])
            d7 = dve_copy(self.rowb[0:1, 0:512], B_[:], [d6])
            d8 = dve_copy(self.rowb[0:1, 512:1024], C_[:], [d7])
            self.brow.setw(d8)
            t1 = pe_mm(self.pbc[0][:], self.ones_r[0:1, 0:128],
                       self.rowb[0:1, 0:512],
                       [d8] + self.bpbc[0].wr(), inc="pe")
            t2 = pe_mm(self.pbc[1][:], self.ones_r[0:1, 0:128],
                       self.rowb[0:1, 512:1024],
                       [d8] + self.bpbc[1].wr(), inc="pe")
            self.bpbc[0].setw(t1)
            self.bpbc[1].setw(t2)
            self.brow.addr(t1)
            self.brow.addr(t2)

    with es:
        triu = SB(es, "triu_b", [128, 128], BF16)
        ones_r = SB(es, "ones_r", [1, 128], BF16)
        ones_c = SB(es, "ones_c", [128, 1], BF16)
        LNKit.ones_r = ones_r
        LNKit.ones_c = ones_c
        omA = SB(es, "omA_b", [128, 64], BF16)
        boP = SB(es, "boP_b", [128, 8])
        boFP = SB(es, "boFP_b", [128, 8])
        bqkvP = SB(es, "bqkvP_b", [128, 24])
        ln1wP = SB(es, "ln1wP_b", [128, 8])
        ln1bP = SB(es, "ln1bP_b", [128, 8])
        ln2wP = SB(es, "ln2wP_b", [128, 8])
        ln2bP = SB(es, "ln2bP_b", [128, 8])
        gateWb = SB(es, "gateWb", [128, 64], BF16)
        fconst = SB(es, "fconst_b", [128, 3])
        LNKit.fconst = None
        g_tm = SB(es, "g_tm", [128, NT128 * 8])
        b_small = Buf()
        b_gtm = Buf()

        stg0 = SB(es, "stg0", [128, 64])
        bstg0 = Buf()

        LNKit.fconst = fconst
        for dst, src in ((boP, boP_d), (boFP, boFP_d), (bqkvP, bqkvP_d),
                         (ln1wP, ln1wP_d), (ln1bP, ln1bP_d),
                         (ln2wP, ln2wP_d), (ln2bP, ln2bP_d), (triu, triu_d),
                         (fconst, fconst_d)):
            b_small.addr(ldma(dst[:], src, []))
        t1 = ldma(stg0[:, 0:64], omA_d, [])
        bstg0.setw(t1)
        tc = dve_copy(omA[:], stg0[:, 0:64], [t1])
        bstg0.addr(tc)
        b_small.addr(tc)
        t1 = ldma(stg0[:, 0:64], gateWP_d, bstg0.wr())
        bstg0.setw(t1)
        tc = dve_copy(gateWb[:], stg0[:, 0:64], [t1])
        bstg0.addr(tc)
        b_small.addr(tc)
        b_small.addr(dve_memset(ones_r[:], 1.0, []))
        b_small.addr(dve_memset(ones_c[:], 1.0, []))

        esO = ExitStack()
        with esO:
            es1 = ExitStack()
            with es1:
                NX1 = [SB(es1, f"NX1_{c}", [128, NTOK], BF16) for c in range(NC)]
                bNX1 = [Buf() for _ in range(NC)]

                esA = ExitStack()
                with esA:
                    wQ = SB(esA, "wQ", [128, NC * D], BF16)
                    wK = SB(esA, "wK", [128, NC * D], BF16)
                    wV = SB(esA, "wV", [128, NC * D], BF16)
                    wO = SB(esA, "wO", [128, NC * D], BF16)
                    xb = SB(esA, "xb", [128, NC * 512], BF16)
                    xstg = SB(esA, "xstg", [128, 2048])
                    combq = [[SB(esA, f"combq{g}{i}", [128, 512], BF16)
                              for i in range(2)] for g in range(2)]
                    combk = [[SB(esA, f"combk{g}{i}", [128, 512], BF16)
                              for i in range(2)] for g in range(2)]
                    fkT = [[SB(esA, f"fkT{g}{i}", [128, 128], BF16)
                            for i in range(4)] for g in range(2)]
                    fq = [[combq[g][jj][0:64, :] for jj in range(2)]
                          for g in range(2)]
                    fk = [[combk[g][jj][0:64, :] for jj in range(2)]
                          for g in range(2)]
                    amS = SB(esA, "amS", [128, 256], BF16)
                    v2 = [SB(esA, f"v2_{i}", [128, H * 65], BF16) for i in range(4)]
                    SRUN = SB(esA, "SRUN", [64, H * 65])
                    Ssb = [SB(esA, f"Ssb{g}", [64, 2 * 4 * 65], BF16)
                           for g in range(2)]
                    srun_bf = [SB(esA, f"srun_bf{g}", [64, 2 * 65], BF16)
                               for g in range(2)]
                    csc = [SB(esA, f"csc{i}", [128, 512], BF16) for i in range(NC)]
                    RES = SB(esA, "RES", [128, NC * 512])
                    APL = SB(esA, "APL", [128, 512])

                    psa = PsA(esA, "A")
                    pq = psa.alloc("pq", [128, 512])
                    pk = psa.alloc("pk", [128, 512])
                    Apack = psa.alloc("Apack", [128, 512])
                    pA_ = Apack[:, 0:128]
                    po = [psa.alloc("poT0", [65, 128])[:],
                          psa.alloc("poT1", [65, 128])[:]]
                    phiA = pq[0:64, 0:512]
                    phiB = pk[0:64, 0:512]
                    ptmA = pq[:, 0:64]
                    ptmB = pk[:, 0:64]
                    pwo = [psa.alloc(f"pwo{i}", [128, 512]) for i in range(2)]
                    pScT = psa.alloc("pScT", [64, 65])
                    pSc = [pScT[:], pwo[0][0:64, 0:65]]
                    pzb = pwo[1][0:64, 0:128]
                    bpwo = None

                    bxb, bAPL = (Buf() for _ in range(2))
                    bSsb = [[Buf(), Buf()], [Buf(), Buf()]]
                    bsrun_bf = [[Buf(), Buf()], [Buf(), Buf()]]
                    bxstg = [Buf(), Buf()]
                    bw = Buf()
                    bcombq = [[Buf(), Buf()], [Buf(), Buf()]]
                    bcombk = [[Buf(), Buf()], [Buf(), Buf()]]
                    bfq = bcombq
                    bfk = bcombk
                    bam = Buf()
                    bfkT = [[Buf() for _ in range(4)] for _ in range(2)]
                    bv2 = [Buf() for _ in range(4)]
                    bSRUN = [Buf() for _ in range(H)]
                    bcsc = [Buf() for _ in range(NC)]
                    bRES = [Buf() for _ in range(NC)]
                    bpq, bpk, bpA = (Buf() for _ in range(3))
                    bpo = [Buf(), Buf()]
                    bpwo = [Buf(), Buf()]
                    bpzb = bpwo[1]
                    bpSc = [Buf(), bpwo[0]]
                    kit = LNKit(esA, "A", psa, pwo, bpwo,
                                ps12=(pq[0:1, 0:512], pk[0:1, 0:512]),
                                bps12=(bpq, bpk))
                    bzrs = kit.brow
                    bzrecb = kit.brow
                    zrs = kit.rA[0:1, 0:128]
                    zrecb = kit.rowb.bitcast(BF16)[0:1, 0:128] if False else \
                        kit.rowb[0:1, 0:128]

                    wi = 0
                    for wsb, wd in ((wQ, wqT_d), (wK, wkT_d), (wV, wvT_d),
                                    (wO, woT_d)):
                        wr_ = wd.rearrange("(c p) i -> p c i", p=128)
                        for c in range(NC):
                            j = wi % 2
                            wi += 1
                            sl_ = xstg[:, j * 1024:j * 1024 + 1024]
                            td = ldma(sl_, wr_[:, c, :], bxstg[j].wr())
                            bxstg[j].setw(td)
                            tcw = dve_copy(wsb[:, c * D:(c + 1) * D], sl_, [td])
                            bxstg[j].addr(tcw)
                            bw.addr(tcw)

                    for i in range(4):
                        bv2[i].setw(dve_memset(v2[i][:], 1.0, []))
                    tz = dve_memset(SRUN[:], 0.0, [])
                    for h in range(H):
                        bSRUN[h].setw(tz)

                    for b in range(NB if PHASES >= 1 else 0):
                        T0 = b * 512
                        for hf in range(2):
                            sl_ = xstg[:].rearrange("p (c t) -> p c t", c=4)
                            td = ldma(sl_, xT_r[:, hf * 4:hf * 4 + 4, T0:T0 + 512],
                                      bxstg[0].wr() + bxstg[1].wr())
                            bxstg[0].setw(td)
                            bxstg[1].setw(td)
                            tcx = dve_copy(xb[:, hf * 2048:hf * 2048 + 2048],
                                           xstg[:],
                                           [td] + (bxb.wr() if hf == 0 else []))
                            bxb.setw(tcx)
                            bxstg[0].addr(tcx)
                            bxstg[1].addr(tcx)

                        for r in range(4 if APART >= 1 else 0):
                            for half in range(2):
                                pv = pwo[half]
                                bp = bpwo[half]
                                deps = bxb.rd() + bp.wr() + list(bw.rs)
                                tm = None
                                for kc in range(NC):
                                    tm = pe_mm(
                                        pv[:],
                                        xb[:, kc * 512 + r * 128:kc * 512 + r * 128 + 128],
                                        wV[:, kc * D + half * 512:kc * D + half * 512 + 512],
                                        deps if kc == 0 else [],
                                        start=(kc == 0), stop=(kc == NC - 1),
                                        inc="pe" if kc == NC - 1 else None)
                                bp.setw(tm)
                                bxb.addr(tm)
                                dst = v2[r][:].rearrange("p (h x) -> p h x", h=H)[
                                    :, half * 8:half * 8 + 8, 0:64]
                                src = pv[:].rearrange("p (h x) -> p h x", x=64)
                                te = dve_copy(dst, src, [tm] + bv2[r].wr())
                                bv2[r].setw(te)
                                bp.addr(te)

                        def prep(hp):
                            g = hp % 2
                            h0, h1 = 2 * hp, 2 * hp + 1
                            for pp_, bpp_, wsb in ((pq, bpq, wQ), (pk, bpk, wK)):
                                deps = bxb.rd() + bpp_.wr()
                                tm = None
                                for kc in range(NC):
                                    tm = pe_mm(
                                        pp_[:],
                                        wsb[:, kc * D + hp * 128:kc * D + hp * 128 + 128],
                                        xb[:, kc * 512:kc * 512 + 512],
                                        deps if kc == 0 else [],
                                        start=(kc == 0), stop=(kc == NC - 1),
                                        inc="pe" if kc == NC - 1 else None)
                                bpp_.setw(tm)
                                bxb.addr(tm)
                            for pp_, bpp_, comb, bcomb in (
                                    (pq, bpq, combq[g], bcombq[g]),
                                    (pk, bpk, combk[g], bcombk[g])):
                                for j in range(2):
                                    t1 = dve_copy(comb[j][0:64, :],
                                                  pp_[j * 64:j * 64 + 64, :],
                                                  [bpp_.w] + bcomb[j].wr())
                                    t2 = act(AF.Square, comb[j][64:128, :],
                                             pp_[j * 64:j * 64 + 64, :],
                                             [bpp_.w] + bcomb[j].wr() + [t1])
                                    bcomb[j].setw(t2)
                                    bcomb[j].addr(t1)
                                    bpp_.addr(t1)
                                    bpp_.addr(t2)
                            for comb, bcomb, f_, php, bphp in (
                                    (combq[g][0], bcombq[g][0], fq[g][0], phiA, bpq),
                                    (combq[g][1], bcombq[g][1], fq[g][1], phiB, bpk)):
                                tm = pe_mm(php, omA[:], comb[:],
                                           bcomb.rd() + list(bcomb.rs) +
                                           bphp.wr() + [b_small.w], inc="pe")
                                bphp.setw(tm)
                                bcomb.addr(tm)
                                te = act(AF.Exp, f_, php,
                                         [tm] + bcomb.wr(),
                                         bias=fconst[0:64, 0:1])
                                bcomb.setw(te)
                                bphp.addr(te)
                            for j in range(2):
                                for r in range(4):
                                    even = ((j * 4 + r) % 2 == 0)
                                    ptm = ptmA if even else ptmB
                                    bptm = bpq if even else bpk
                                    tt_ = pe_mm(ptm,
                                                combk[g][j][:, r * 128:r * 128 + 128],
                                                omA[:],
                                                bcombk[g][j].rd() +
                                                list(bcombk[g][j].rs) +
                                                bptm.wr() + [b_small.w], inc="pe")
                                    bptm.setw(tt_)
                                    bcombk[g][j].addr(tt_)
                                    tc2 = act(AF.Exp,
                                              fkT[g][r][:, j * 64:j * 64 + 64],
                                              ptm, [tt_] + bfkT[g][r].wr(),
                                              bias=fconst[:, 0:1])
                                    bfkT[g][r].setw(tc2)
                                    bptm.addr(tc2)
                            for comb, bcomb, f_, php, bphp in (
                                    (combk[g][0], bcombk[g][0], fk[g][0], phiA, bpq),
                                    (combk[g][1], bcombk[g][1], fk[g][1], phiB, bpk)):
                                tm = pe_mm(php, omA[:], comb[:],
                                           bcomb.rd() + list(bcomb.rs) +
                                           bphp.wr() + [b_small.w], inc="pe")
                                bphp.setw(tm)
                                bcomb.addr(tm)
                                te = act(AF.Exp, f_, php,
                                         [tm] + bcomb.wr(),
                                         bias=fconst[0:64, 0:1])
                                bcomb.setw(te)
                                bphp.addr(te)
                            for j, h in enumerate((h0, h1)):
                                srun_sl = SRUN[:, h * 65:h * 65 + 65]
                                sbsl = srun_bf[g][:, j * 65:j * 65 + 65]
                                t40 = dve_copy(sbsl, srun_sl,
                                               bSRUN[h].rd() + bsrun_bf[g][j].wr())
                                bsrun_bf[g][j].setw(t40)
                                bSRUN[h].addr(t40)
                                for c in range(4):
                                    pscb = pSc[c % 2]
                                    bpscb = bpSc[c % 2]
                                    tS = pe_mm(pscb,
                                               fkT[g][c][:, j * 64:j * 64 + 64],
                                               v2[c][:, h * 65:h * 65 + 65],
                                               bfkT[g][c].rd() + bv2[c].rd() +
                                               bpscb.wr(), inc="pe")
                                    bpscb.setw(tS)
                                    bfkT[g][c].addr(tS)
                                    bv2[c].addr(tS)
                                    ssl = Ssb[g][:, (j * 4 + c) * 65:
                                                (j * 4 + c) * 65 + 65]
                                    tE = dve_copy(ssl, pscb,
                                                  [tS] + bSsb[g][j].wr())
                                    bSsb[g][j].setw(tE)
                                    bpscb.addr(tE)
                                    tU = dve_tt(ALU.add, srun_sl, srun_sl, pscb,
                                                [tS, tE] + bSRUN[h].wr())
                                    bSRUN[h].setw(tU)
                                    bpscb.addr(tU)

                        def chunk_stage(hp):
                            g = hp % 2
                            h0, h1 = 2 * hp, 2 * hp + 1
                            for j, h in enumerate((h0, h1)):

                                def emit_A(c, j=j, h=h):
                                    ta = pe_mm(pA_,
                                               fk[g][j][:, c * 128:c * 128 + 128],
                                               fq[g][j][:, c * 128:c * 128 + 128],
                                               bfk[g][j].rd() + bfq[g][j].rd() +
                                               bpA.wr(), inc="pe")
                                    bpA.setw(ta)
                                    bfk[g][j].addr(ta)
                                    bfq[g][j].addr(ta)
                                    am = amS[:, (c % 2) * 128:
                                             (c % 2) * 128 + 128]
                                    tmsk = dve_tt(ALU.mult, am, pA_, triu[:],
                                                  [ta] + bam.wr())
                                    bam.setw(tmsk)
                                    bam.addr(tmsk)
                                    bpA.addr(tmsk)
                                    return am, tmsk

                                def emit_pv(c, am, tmsk, j=j, h=h):
                                    pob = po[c % 2]
                                    bpob = bpo[c % 2]
                                    vsl = v2[c][:, h * 65:h * 65 + 65]
                                    pe_mm(pob, vsl, am,
                                          [tmsk] + bv2[c].rd() + bpob.wr(),
                                          start=True, stop=False, inc=None)
                                    bam.addr(None)
                                    deps = bsrun_bf[g][j].rd() + bSsb[g][j].rd()
                                    t2_ = pe_mm(pob,
                                                srun_bf[g][:, j * 65:j * 65 + 65],
                                                fq[g][j][:, c * 128:c * 128 + 128],
                                                deps + bfq[g][j].rd(),
                                                start=False, stop=(c == 0),
                                                inc="pe" if c == 0 else None)
                                    for cp in range(c):
                                        t2_ = pe_mm(
                                            pob,
                                            Ssb[g][:, (j * 4 + cp) * 65:
                                                   (j * 4 + cp) * 65 + 65],
                                            fq[g][j][:, c * 128:c * 128 + 128],
                                            [], start=False, stop=(cp == c - 1),
                                            inc="pe" if cp == c - 1 else None)
                                    bpob.setw(t2_)
                                    bsrun_bf[g][j].addr(t2_)
                                    bSsb[g][j].addr(t2_)
                                    bfq[g][j].addr(t2_)
                                    bv2[c].addr(t2_)
                                    bam.addr(t2_)
                                    return t2_

                                def emit_z(c, t2_, j=j, h=h):
                                    pob = po[c % 2]
                                    bpob = bpo[c % 2]
                                    t5 = act(AF.Ln, zrs, pob[64:65, :],
                                             [t2_, b_small.w] + bzrs.wr(),
                                             bias=fconst[0:1, 2:3])
                                    bzrs.setw(t5)
                                    bpob.addr(t5)
                                    t7 = act(AF.Exp, zrecb, zrs,
                                             [t5] + bzrecb.wr(), scale=-1.0)
                                    bzrs.addr(t7)
                                    bzrecb.setw(t7)
                                    t8 = pe_mm(pzb, ones_r[0:1, 0:64], zrecb,
                                               [t7] + bpzb.wr(), inc="pe")
                                    bpzb.setw(t8)
                                    bzrecb.addr(t8)
                                    osb = combk[g][j][64:128, (c % 2) * 128:
                                                      (c % 2) * 128 + 128]
                                    t9 = dve_copy(osb, pob[0:64, :],
                                                  [t2_] + bcombk[g][j].wr())
                                    bcombk[g][j].setw(t9)
                                    bcombk[g][j].addr(t9)
                                    bpob.addr(t9)
                                    dst = csc[hp][j * 64:j * 64 + 64,
                                                  c * 128:c * 128 + 128]
                                    t10 = dve_tt(ALU.mult, dst, osb, pzb,
                                                 [t9, t8] + bcsc[hp].wr())
                                    bcsc[hp].setw(t10)
                                    bpzb.addr(t10)
                                    bcombk[g][j].addr(t10)

                                am0, tm0 = emit_A(0)
                                am1, tm1 = emit_A(1)
                                t2a = emit_pv(0, am0, tm0)
                                am2, tm2 = emit_A(2)
                                t2b = emit_pv(1, am1, tm1)
                                emit_z(0, t2a)
                                am3, tm3 = emit_A(3)
                                t2c = emit_pv(2, am2, tm2)
                                emit_z(1, t2b)
                                t2d = emit_pv(3, am3, tm3)
                                emit_z(2, t2c)
                                emit_z(3, t2d)

                        if APART >= 2:
                            prep(0)
                            for hp in range(8):
                                if hp + 1 < 8:
                                    prep(hp + 1)
                                if APART >= 5:
                                    chunk_stage(hp)
                        for dc in range(NC if APART >= 6 else 0):
                            j = dc % 2
                            if dc % 4 == 0:
                                hf = dc // 4
                                sl_ = xstg[:].rearrange("p (c t) -> p c t", c=4)
                                tdx4 = ldma(sl_,
                                            xT_r[:, hf * 4:hf * 4 + 4, T0:T0 + 512],
                                            bxstg[0].wr() + bxstg[1].wr())
                                bxstg[0].setw(tdx4)
                                bxstg[1].setw(tdx4)
                            tdx = tdx4
                            pv = pwo[j]
                            bp = bpwo[j]
                            tm = None
                            for ic in range(NC):
                                deps2 = ((bp.wr() + list(bw.rs)) if ic == 0 else []) \
                                    + bcsc[ic].rd()
                                tm = pe_mm(
                                    pv[:],
                                    wO[:, ic * D + dc * 128:ic * D + dc * 128 + 128],
                                    csc[ic][:, 0:512], deps2,
                                    start=(ic == 0), stop=(ic == NC - 1),
                                    inc="pe" if ic == NC - 1 else None)
                            bp.setw(tm)
                            for ic in range(NC):
                                bcsc[ic].addr(tm)
                            tr_ = dve_stt(RES[:, dc * 512:dc * 512 + 512], pv[:],
                                          boP[:, dc:dc + 1],
                                          xstg[:, (dc % 4) * 512:(dc % 4) * 512 + 512],
                                          ALU.add, ALU.add,
                                          [tm, tdx, b_small.w] + bRES[dc].wr())
                            bRES[dc].setw(tr_)
                            bp.addr(tr_)
                            bxstg[0].addr(tr_)
                            bxstg[1].addr(tr_)
                        for dc in range(NC if APART >= 6 else 0):
                            kit.stats(RES[:, dc * 512:dc * 512 + 512], bRES[dc], dc, ones_c)
                        if APART >= 6:
                            kit.finish(ones_r)
                        for dc in range(NC if APART >= 6 else 0):
                            rsl = RES[:, dc * 512:dc * 512 + 512]
                            t1 = dve_tt(ALU.mult, rsl, rsl,
                                        kit.pbc[0][:],
                                        bRES[dc].wr() + [kit.bpbc[0].w])
                            t2 = dve_tt(ALU.subtract, rsl, rsl,
                                        kit.pbc[1][:], [t1, kit.bpbc[1].w])
                            bRES[dc].setw(t2)
                            kit.bpbc[0].addr(t2)
                            kit.bpbc[1].addr(t2)
                        ts_ = sdma(y1S_r[:, :, T0:T0 + 512],
                                   RES[:].rearrange("p (c t) -> p c t", c=NC),
                                   [b.w for b in bRES])
                        for dc in range(NC):
                            bRES[dc].addr(ts_)
                        for dc in range(NC if APART >= 6 else 0):
                            kit.stats(RES[:, dc * 512:dc * 512 + 512], bRES[dc], dc, ones_c)
                        if APART >= 6:
                            kit.finish(ones_r)
                        for dc in range(NC if APART >= 6 else 0):
                            t1 = dve_tt(ALU.mult, APL[:],
                                        RES[:, dc * 512:dc * 512 + 512], kit.pbc[0][:],
                                        bRES[dc].rd() + [kit.bpbc[0].w] + bAPL.wr())
                            t2 = dve_tt(ALU.subtract, APL[:], APL[:], kit.pbc[1][:],
                                        [t1, kit.bpbc[1].w])
                            bRES[dc].addr(t2)
                            kit.bpbc[0].addr(t2)
                            kit.bpbc[1].addr(t2)
                            t3 = dve_ts(NX1[dc][:, T0:T0 + 512], APL[:],
                                        ln1wP[:, dc:dc + 1], ln1bP[:, dc:dc + 1],
                                        ALU.mult, ALU.add,
                                        [t2, b_small.w] + bNX1[dc].wr())
                            bAPL.setw(t3)
                            bNX1[dc].setw(t3)
                    flush()
                    p.barrier()

                O_sb = [SB(esO, f"O_{c}", [128, NTOK], BF16, side="right")
                        for c in range(NC)]
                bO = [Buf() for _ in range(NC)]
                esB2 = ExitStack()
                with esB2:
                    Qg = [SB(esB2, f"Qg{i}", [128, NTOK], BF16) for i in range(2)]
                    Kg = [SB(esB2, f"Kg{i}", [128, NTOK], BF16) for i in range(2)]
                    Vg = [SB(esB2, f"Vg{i}", [128, 4 * 65], BF16)
                          for i in range(NT128)]
                    wg = SB(esB2, "wg", [128, NC * 768], BF16)
                    bqkvV = SB(esB2, "bqkvV_b", [1, D], BF16)
                    stgB = [SB(esB2, f"stgB{i}", [128, 512]) for i in range(2)]
                    P_sb = [[SB(esB2, f"P_{s}{i}", [128, 512], BF16)
                             for i in range(2)] for s in range(2)]
                    osb2 = SB(esB2, "osb2", [64, 512], BF16)
                    zrs2 = SB(esB2, "zrs2", [1, 512])
                    zrecb2 = SB(esB2, "zrecb2", [1, 512], BF16)
                    bQg = [Buf(), Buf()]
                    bKg = [Buf(), Buf()]
                    bVg = [Buf() for _ in range(NT128)]
                    bwg = Buf()
                    bbqkvV = Buf()
                    bstgB = [Buf(), Buf()]
                    bP = [[Buf(), Buf()], [Buf(), Buf()]]
                    bosb2, bzrs2, bzrecb2 = Buf(), Buf(), Buf()

                    psb = PsA(esB2, "B")
                    pqk = [[psb.alloc(f"pqk{s}{i}", [128, 512])
                            for i in range(2)] for s in range(2)]
                    po2 = [psb.alloc("po2A", [65, 512]),
                           psb.alloc("po2B", [65, 512])]
                    pzb2 = psb.alloc("pzb2", [64, 512])
                    pvtm = psb.alloc("pvtm", [128, 256])
                    bpqk = [[Buf(), Buf()], [Buf(), Buf()]]
                    bpo2 = [Buf(), Buf()]
                    bpzb2, bpvtm = Buf(), Buf()

                    td = ldma(stgB[0][0:1, 0:512], bqkvV_d[0:1, 0:512],
                              bstgB[0].wr())
                    bstgB[0].setw(td)
                    tc0 = dve_copy(bqkvV[0:1, 0:512], stgB[0][0:1, 0:512], [td])
                    bstgB[0].addr(tc0)
                    td = ldma(stgB[1][0:1, 0:512], bqkvV_d[0:1, 512:1024],
                              bstgB[1].wr())
                    bstgB[1].setw(td)
                    tc = dve_copy(bqkvV[0:1, 512:1024], stgB[1][0:1, 0:512],
                                  [td, tc0])
                    bstgB[0].addr(tc)
                    bbqkvV.setw(tc)

                    for i in range(NT128):
                        bVg[i].setw(dve_memset(Vg[i][:], 1.0, []))

                    Wq_r = Wqkv_d.rearrange("(c p) o -> p c o", p=128)
                    for g in range(4 if PHASES >= 2 else 0):
                        for part, coff in ((0, g * 256), (1, D + g * 256),
                                           (2, 2 * D + g * 256)):
                            for qc in range(4):
                                j = qc % 2
                                td = ldma(
                                    stgB[j][:].rearrange("p (c o) -> p c o", c=2),
                                    Wq_r[:, qc * 2:qc * 2 + 2, coff:coff + 256],
                                    bstgB[j].wr())
                                bstgB[j].setw(td)
                                dst = wg[:].rearrange("p (c o) -> p c o", o=768)[
                                    :, qc * 2:qc * 2 + 2,
                                    part * 256:part * 256 + 256]
                                tcw = dve_copy(
                                    dst,
                                    stgB[j][:].rearrange("p (c o) -> p c o", c=2),
                                    [td] + (bwg.wr() if (part == 0 and qc == 0)
                                            else [bwg.w]))
                                bwg.setw(tcw)
                                bstgB[j].addr(tcw)

                        for oc in range(2):
                            cq = g * 2 + oc
                            for dstT, bdstT, wcol, biascol in (
                                    (Qg[oc], bQg[oc], oc * 128, cq),
                                    (Kg[oc], bKg[oc], 256 + oc * 128, 8 + cq)):
                                for tt in range(NB):
                                    pp_ = pqk[0][tt % 2]
                                    bpp_ = bpqk[0][tt % 2]
                                    deps = bwg.rd() + bpp_.wr() + \
                                        (bdstT.wr() if tt == 0 else [])
                                    tm = None
                                    for kc in range(NC):
                                        tm = pe_mm(
                                            pp_[:],
                                            wg[:, kc * 768 + wcol:kc * 768 + wcol + 128],
                                            NX1[kc][:, tt * 512:tt * 512 + 512],
                                            (deps + bNX1[kc].rd()) if kc == 0 else [],
                                            start=(kc == 0), stop=(kc == NC - 1),
                                            inc="pe" if kc == NC - 1 else None)
                                        bNX1[kc].addr(tm)
                                    bpp_.setw(tm)
                                    bwg.addr(tm)
                                    te = act_add(
                                        dstT[:, tt * 512:tt * 512 + 512],
                                        pp_[:],
                                        bqkvP[:, biascol:biascol + 1],
                                        [tm, b_small.w])
                                    bdstT.setw(te)
                                    bpp_.addr(te)
                        for t128 in range(NT128):
                            deps = bwg.rd() + bpvtm.wr() + bbqkvV.rd()
                            for kc in range(NC):
                                pe_mm(pvtm[:],
                                      NX1[kc][:, t128 * 128:t128 * 128 + 128],
                                      wg[:, kc * 768 + 512:kc * 768 + 768],
                                      (deps + bNX1[kc].rd()) if kc == 0 else [],
                                      start=(kc == 0), stop=False, inc=None)
                            tm = pe_mm(pvtm[:], ones_r[0:1, 0:128],
                                       bqkvV[0:1, g * 256:g * 256 + 256],
                                       [], start=False, stop=True, inc="pe")
                            bpvtm.setw(tm)
                            bwg.addr(tm)
                            bbqkvV.addr(tm)
                            for kc in range(NC):
                                bNX1[kc].addr(tm)
                            dst = Vg[t128][:].rearrange("p (h x) -> p h x", h=4)[
                                :, :, 0:64]
                            src = pvtm[:].rearrange("p (h x) -> p h x", x=64)
                            te = dve_copy(dst, src, [tm] + bVg[t128].wr())
                            bVg[t128].setw(te)
                            bpvtm.addr(te)

                        for oc in range(2):
                            for qt in range(NB):
                                kmax = 4 * qt + 3

                                def emit_score(s, jj, oc=oc, qt=qt):
                                    ro = s * 64
                                    pp_ = pqk[s][jj % 2]
                                    bpp_ = bpqk[s][jj % 2]
                                    ts_ = pe_mm(
                                        pp_[:],
                                        Kg[oc][ro:ro + 64, jj * 128:jj * 128 + 128],
                                        Qg[oc][ro:ro + 64, qt * 512:qt * 512 + 512],
                                        bKg[oc].rd() + bQg[oc].rd() + bpp_.wr(),
                                        inc="pe", tp=(ro, 0))
                                    bpp_.setw(ts_)
                                    bKg[oc].addr(ts_)
                                    bQg[oc].addr(ts_)
                                    Pb = P_sb[s][jj % 2]
                                    bPb = bP[s][jj % 2]
                                    if jj < 4 * qt:
                                        te = act(AF.Exp, Pb[:], pp_[:],
                                                 [ts_] + bPb.wr(), scale=0.125)
                                        bPb.setw(te)
                                        bpp_.addr(te)
                                    else:
                                        r = jj - 4 * qt
                                        deps = [ts_] + bPb.wr()
                                        tz2 = None
                                        if r > 0:
                                            tz2 = dve_memset(Pb[:, 0:128 * r],
                                                             0.0, deps)
                                        te0 = act(AF.Exp, Pb[:, 128 * r:512],
                                                  pp_[:, 128 * r:512], deps,
                                                  scale=0.125)
                                        te = dve_tt(ALU.mult,
                                                    Pb[:, 128 * r:128 * r + 128],
                                                    Pb[:, 128 * r:128 * r + 128],
                                                    triu[:],
                                                    [te0] + ([tz2] if tz2 else []))
                                        bPb.setw(te)
                                        bpp_.addr(te0)

                                def emit_pv(s, jj, oc=oc, qt=qt):
                                    hh = 2 * oc + s
                                    kmax = 4 * qt + 3
                                    Pb = P_sb[s][jj % 2]
                                    bPb = bP[s][jj % 2]
                                    tpv = pe_mm(po2[s][:],
                                                Vg[jj][:, hh * 65:hh * 65 + 65],
                                                Pb[:],
                                                bPb.rd() + bVg[jj].rd() +
                                                (bpo2[s].wr() if jj == 0 else []),
                                                start=(jj == 0), stop=(jj == kmax),
                                                inc="pe")
                                    bpo2[s].setw(tpv)
                                    bPb.addr(tpv)
                                    bVg[jj].addr(tpv)

                                emit_score(0, 0)
                                emit_score(1, 0)
                                for jj in range(kmax + 1):
                                    if jj + 1 <= kmax:
                                        emit_score(0, jj + 1)
                                        emit_score(1, jj + 1)
                                    emit_pv(0, jj)
                                    emit_pv(1, jj)

                                for s in range(2):
                                    h = 4 * g + 2 * oc + s
                                    tlast = bpo2[s].w
                                    t5 = act(AF.Ln, zrs2[:], po2[s][64:65, :],
                                             [tlast] + bzrs2.wr())
                                    bzrs2.setw(t5)
                                    bpo2[s].addr(t5)
                                    t7 = act(AF.Exp, zrecb2[:], zrs2[:],
                                             [t5] + bzrecb2.wr(), scale=-1.0)
                                    bzrecb2.setw(t7)
                                    bzrs2.addr(t7)
                                    t8 = pe_mm(pzb2[:], ones_r[0:1, 0:64],
                                               zrecb2[:],
                                               [t7] + bpzb2.wr(), inc="pe")
                                    bpzb2.setw(t8)
                                    bzrecb2.addr(t8)
                                    t9 = dve_copy(osb2[:], po2[s][0:64, :],
                                                  [tlast] + bosb2.wr())
                                    bosb2.setw(t9)
                                    bpo2[s].addr(t9)
                                    ch = h // 2
                                    dst = O_sb[ch][(h % 2) * 64:(h % 2) * 64 + 64,
                                                   qt * 512:qt * 512 + 512]
                                    t10 = dve_tt(ALU.mult, dst, osb2[:], pzb2[:],
                                                 [t9, t8] + bO[ch].wr())
                                    bO[ch].setw(t10)
                                    bpzb2.addr(t10)
                                    bosb2.addr(t10)
                    flush()
                    p.barrier()

            esB3 = ExitStack()
            with esB3:
                woF = SB(esB3, "woF", [128, NC * D], BF16)
                stgC = [SB(esB3, f"stgC{i}", [128, 1024]) for i in range(2)]
                y1sl = SB(esB3, "y1sl", [128, NC * 512])
                X2blk = SB(esB3, "X2blk", [128, NC * 512])
                NX2blk = SB(esB3, "NX2blk", [128, NC * 512], BF16)
                APL2 = SB(esB3, "APL2", [128, 512])
                gsc = SB(esB3, "gsc", [128, 64])
                gbit = SB(esB3, "gbit", [128, 8], BF16)
                gtrow = SB(esB3, "gtrow", [8, 128], BF16)
                iden = SB(esB3, "iden_b", [128, 128], BF16)
                biden = Buf()
                biden.setw(ldma(iden[:], iden_d, []))
                bwoF = Buf()
                bstgC = [Buf(), Buf()]
                by1sl = Buf()
                bX2 = [Buf() for _ in range(NC)]
                bNX2 = [Buf() for _ in range(NC)]
                bAPL2 = Buf()
                bgsc = Buf()
                bgbit = Buf()
                bgtrow = Buf()

                psc = PsA(esB3, "C")
                pwoF = [psc.alloc(f"pwoF{i}", [128, 512]) for i in range(2)]
                bpwoF = [Buf(), Buf()]
                pg = psc.alloc("pg", [128, 8])
                bpg = Buf()
                pgt = psc.alloc("pgt", [8, 128], BF16)
                bpgt = Buf()
                kit2 = LNKit(esB3, "C", psc, pwoF, bpwoF)

                for c in range(NC if PHASES >= 3 else 0):
                    wr_ = WoF_d.rearrange("(c p) o -> p c o", p=128)
                    j = c % 2
                    td = ldma(stgC[j][:], wr_[:, c, :], bstgC[j].wr())
                    bstgC[j].setw(td)
                    tcw = dve_copy(woF[:, c * D:(c + 1) * D],
                                   stgC[j][:], [td] + (bwoF.wr() if c == 0
                                                       else [bwoF.w]))
                    bwoF.setw(tcw)
                    bstgC[j].addr(tcw)

                for tt in range(NB if PHASES >= 3 else 0):
                    T0 = tt * 512
                    tld = ldma(y1sl[:].rearrange("p (c t) -> p c t", c=NC),
                               y1S_r[:, :, T0:T0 + 512], by1sl.wr())
                    by1sl.setw(tld)
                    for dc in range(NC):
                        pv = pwoF[dc % 2]
                        bp = bpwoF[dc % 2]
                        tm = None
                        for ic in range(NC):
                            deps2 = (bp.wr() if ic == 0 else []) + bO[ic].rd()
                            tm = pe_mm(
                                pv[:],
                                woF[:, ic * D + dc * 128:ic * D + dc * 128 + 128],
                                O_sb[ic][:, T0:T0 + 512],
                                deps2 + (bwoF.rd() if ic == 0 else []),
                                start=(ic == 0), stop=(ic == NC - 1),
                                inc="pe" if ic == NC - 1 else None)
                        bp.setw(tm)
                        for ic in range(NC):
                            bO[ic].addr(tm)
                        bwoF.addr(tm)
                        tr_ = dve_stt(X2blk[:, dc * 512:dc * 512 + 512], pv[:],
                                      boFP[:, dc:dc + 1],
                                      y1sl[:, dc * 512:dc * 512 + 512],
                                      ALU.add, ALU.add,
                                      [tm, tld, b_small.w] + bX2[dc].wr())
                        bX2[dc].setw(tr_)
                        bp.addr(tr_)
                        by1sl.addr(tr_)
                    ts_ = sdma(x2S_r[:, :, T0:T0 + 512],
                               X2blk[:].rearrange("p (c t) -> p c t", c=NC),
                               [b.w for b in bX2])
                    for dc in range(NC):
                        bX2[dc].addr(ts_)
                    for dc in range(NC):
                        kit2.stats(X2blk[:, dc * 512:dc * 512 + 512], bX2[dc],
                                   dc, ones_c)
                    kit2.finish(ones_r)
                    for dc in range(NC):
                        t1 = dve_tt(ALU.mult, APL2[:],
                                    X2blk[:, dc * 512:dc * 512 + 512],
                                    kit2.pbc[0][:],
                                    bX2[dc].rd() + [kit2.bpbc[0].w] + bAPL2.wr())
                        t2 = dve_tt(ALU.subtract, APL2[:], APL2[:], kit2.pbc[1][:],
                                    [t1, kit2.bpbc[1].w])
                        bX2[dc].addr(t2)
                        kit2.bpbc[0].addr(t2)
                        kit2.bpbc[1].addr(t2)
                        t3 = dve_ts(NX2blk[:, dc * 512:dc * 512 + 512], APL2[:],
                                    ln2wP[:, dc:dc + 1],
                                    ln2bP[:, dc:dc + 1], ALU.mult, ALU.add,
                                    [t2, b_small.w] + bNX2[dc].wr())
                        bAPL2.setw(t3)
                        bNX2[dc].setw(t3)
                    ts_ = sdma(nx2S_r[:, :, T0:T0 + 512],
                               NX2blk[:].rearrange("p (c t) -> p c t", c=NC),
                               [b.w for b in bNX2])
                    for dc in range(NC):
                        bNX2[dc].addr(ts_)
                    for r in range(4):
                        t128 = tt * 4 + r
                        tm = None
                        deps0 = bpg.wr() + [b_small.w]
                        for kc2 in range(NC):
                            deps0 += bNX2[kc2].rd()
                        for kc in range(NC):
                            tm = pe_mm(pg[:], NX2blk[:, kc * 512 + r * 128:kc * 512 + r * 128 + 128],
                                       gateWb[:, kc * 8:kc * 8 + 8],
                                       deps0 if kc == 0 else [],
                                       start=(kc == 0), stop=(kc == NC - 1),
                                       inc="pe" if kc == NC - 1 else None)
                            bNX2[kc].addr(tm)
                        bpg.setw(tm)
                        G8 = gsc[:, 0:8]
                        m1 = gsc[:, 8:9]
                        nm1 = gsc[:, 9:10]
                        m2 = gsc[:, 10:11]
                        den = gsc[:, 11:12]
                        rden = gsc[:, 12:13]
                        msk = gsc[:, 16:24]
                        g2 = gsc[:, 24:32]
                        ee = gsc[:, 32:40]
                        d0 = dve_copy(G8, pg[:], [tm] + bgsc.wr())
                        bpg.addr(d0)
                        d1 = dve_red(m1, G8, ALU.max, [d0])
                        d1b = dve_red(nm1, G8, ALU.max, [d1], negate=True)
                        d2 = dve_ts(msk, G8, m1, None, ALU.is_ge, ALU.bypass, [d1b])
                        d3 = dve_stt(g2, msk, -1e30, G8, ALU.mult, ALU.add, [d2])
                        d4 = dve_red(m2, g2, ALU.max, [d3])
                        a1 = act(AF.Exp, ee, G8, [d4], bias=nm1)
                        d5 = dve_ts(msk, G8, m2, None, ALU.is_ge, ALU.bypass, [a1])
                        d6 = dve_tt(ALU.mult, ee, ee, msk, [d5])
                        d7 = dve_red(den, ee, ALU.add, [d6])
                        d8 = dve_recip(rden, den, [d7])
                        d9 = dve_ts(g_tm[:, t128 * 8:t128 * 8 + 8], ee, rden, None,
                                    ALU.mult, ALU.bypass, [d8] + b_gtm.wr())
                        b_gtm.setw(d9)
                        bgsc.setw(d9)
                        d10 = dve_copy(gbit[:], g_tm[:, t128 * 8:t128 * 8 + 8],
                                       [d9] + bgbit.wr())
                        bgbit.setw(d10)
                        b_gtm.addr(d10)
                        t12 = pe_tr(pgt[:], gbit[:], iden[:],
                                    [d10] + biden.rd() + bpgt.wr())
                        bpgt.setw(t12)
                        bgbit.addr(t12)
                        d11 = dve_copy(gtrow[:], pgt[:], [t12] + bgtrow.wr())
                        bgtrow.setw(d11)
                        bpgt.addr(d11)
                        tsg = sdma(gTS[:, t128 * 128:t128 * 128 + 128], gtrow[:],
                                   [d11])
                        bgtrow.addr(tsg)
                flush()
                p.barrier()

        esM = ExitStack()
        with esM:
            fc1T = SB(esM, "fc1T", [128, 32 * 1024], BF16)
            fc2T = SB(esM, "fc2T", [128, 32 * 1024], BF16)
            nx2sl = [SB(esM, f"nx2sl{i}", [128, NC * 512], BF16) for i in range(2)]
            stgM = [SB(esM, f"stgM{i}", [128, 2048]) for i in range(2)]
            tmpM = [SB(esM, f"tmpM{i}", [128, 512]) for i in range(4)]
            fc1bT = SB(esM, "fc1bT_b", [128, 256])
            esM2 = ExitStack()
            Hs = [SB(esM2, f"Hs{i}", [128, 512], BF16) for i in range(32)]
            bfc1 = [Buf() for _ in range(32)]
            bfc2 = [Buf() for _ in range(32)]
            bHs = [Buf() for _ in range(32)]
            bnx2sl = [Buf(), Buf()]
            bstgM = [Buf(), Buf()]
            btmpM = [Buf(), Buf(), Buf(), Buf()]
            baccS = Buf()
            bfc1bT = Buf()
            bfc2bB = Buf()
            bgTm = Buf()
            bstgm2 = Buf()

            psm = PsA(esM, "M")
            pf = [psm.alloc(f"pf{i}", [128, 512]) for i in range(2)]
            pD = [psm.alloc(f"pD{i}", [128, 512]) for i in range(4)]
            bpf = [Buf(), Buf()]
            bpD = [Buf() for _ in range(4)]

            bfc1bT.setw(ldma(fc1bT[:], fc1bT_d, []))

            def load_pair(e, k, which, deps_extra, queue=0):
                j = (2 * k + which) % 2
                if which == 0:
                    src = fc1ts_d[e, 2 * k:2 * k + 2].rearrange(
                        "f (c p) g -> p f c g", p=128)
                    dstg = stgM[j][:].rearrange("p (f c g) -> p f c g",
                                                f=2, c=NC)
                    dst = fc1T[:, 2 * k * 1024:(2 * k + 2) * 1024]
                    bsl = [bfc1[2 * k], bfc1[2 * k + 1]]
                else:
                    src = fc2ts_d[e, 2 * k:2 * k + 2].rearrange(
                        "f p d -> p f d")
                    dstg = stgM[j][:].rearrange("p (f d) -> p f d", f=2)
                    dst = fc2T[:, 2 * k * 1024:(2 * k + 2) * 1024]
                    bsl = [bfc2[2 * k], bfc2[2 * k + 1]]
                deps = bstgM[j].wr() + deps_extra
                for b in bsl:
                    deps += b.wr()
                if queue == 0:
                    td = ldma(dstg, src, deps)
                else:
                    td = pdma(dstg, src, deps)
                bstgM[j].setw(td)
                tcw = dve_copy(dst, stgM[j][:], [td])
                bstgM[j].addr(tcw)
                for b in bsl:
                    b.setw(tcw)

            for k in range(16 if PHASES >= 4 else 0):
                load_pair(0, k, 0, [], queue=k % 2)
                load_pair(0, k, 1, [], queue=(k + 1) % 2)

            for e in range(8 if PHASES >= 4 else 0):
                for sl in range(NB):
                    T0 = sl * 512
                    jn = sl % 2
                    if NB == 1 or (e == 0 and sl == 0):
                        tld = ldma(nx2sl[jn][:].rearrange("p (c t) -> p c t", c=NC),
                                   nx2S_r[:, :, T0:T0 + 512], bnx2sl[jn].wr())
                        bnx2sl[jn].setw(tld)
                    else:
                        tld = bnx2sl[jn].w
                    if NB > 1 and (sl < NB - 1 or e < 7):
                        npf = (sl + 1) % NB
                        jn2 = npf % 2
                        T0n = npf * 512
                        tldn = ldma(nx2sl[jn2][:].rearrange("p (c t) -> p c t", c=NC),
                                    nx2S_r[:, :, T0n:T0n + 512], bnx2sl[jn2].wr())
                        bnx2sl[jn2].setw(tldn)
                    def emit_fc1(fc):
                        ppf = pf[fc % 2]
                        bppf = bpf[fc % 2]
                        tm = None
                        for kc in range(NC):
                            tm = pe_mm(
                                ppf[:],
                                fc1T[:, fc * 1024 + kc * 128:fc * 1024 + kc * 128 + 128],
                                nx2sl[jn][:, kc * 512:kc * 512 + 512],
                                ([tld] + bppf.wr() + bfc1[fc].rd()) if kc == 0 else [],
                                start=(kc == 0), stop=(kc == NC - 1),
                                inc="pe" if kc == NC - 1 else None)
                        bppf.setw(tm)
                        bfc1[fc].addr(tm)
                        bnx2sl[jn].addr(tm)
                        if e < 7 and sl == NB - 1 and fc % 2 == 1:
                            load_pair(e + 1, fc // 2, 0,
                                      [tm] + bfc1[fc - 1].rd())
                        tg = act(AF.Gelu, Hs[fc][:], ppf[:],
                                 [tm] + bfc1bT.rd() + bHs[fc].wr(),
                                 bias=fc1bT[:, e * 32 + fc:e * 32 + fc + 1])
                        bHs[fc].setw(tg)
                        bppf.addr(tg)

                    def emit_fc2A(fc):
                        tg = bHs[fc].w
                        for r in range(4):
                            tm2 = pe_mm(pD[r][:], Hs[fc][:, r * 128:r * 128 + 128],
                                        fc2T[:, fc * 1024:fc * 1024 + 512],
                                        [tg] + bfc2[fc].rd() +
                                        (bpD[r].wr() if fc == 0 else []),
                                        start=(fc == 0), stop=(fc == 31),
                                        inc="pe" if (fc == 31 or r == 3) else None)
                            bpD[r].setw(tm2)
                            bHs[fc].addr(tm2)
                            bfc2[fc].addr(tm2)

                    emit_fc1(0)
                    for fc in range(1, 32):
                        emit_fc1(fc)
                        emit_fc2A(fc - 1)
                    emit_fc2A(31)
                    for r in range(4):
                        jt = r
                        t128 = sl * 4 + r
                        gcol = g_tm[:, t128 * 8 + e:t128 * 8 + e + 1]
                        tev = dve_ts(tmpM[jt][:], pD[r][:], gcol, None, ALU.mult,
                                     ALU.bypass,
                                     [bpD[r].w, b_gtm.w] + btmpM[jt].wr())
                        btmpM[jt].setw(tev)
                        bpD[r].addr(tev)
                        b_gtm.addr(tev)
                        tdm = pdma(accS[T0 + r * 128:T0 + r * 128 + 128, 0:512],
                                   tmpM[jt][:], [tev] + baccS.rd(), accum=(e > 0))
                        btmpM[jt].addr(tdm)
                        baccS.setw(tdm)
                    for r in range(4):
                        for fc in range(32):
                            tm2 = pe_mm(pD[r][:], Hs[fc][:, r * 128:r * 128 + 128],
                                        fc2T[:, fc * 1024 + 512:fc * 1024 + 1024],
                                        (bHs[fc].rd() + bfc2[fc].rd()) +
                                        (bpD[r].wr() if fc == 0 else []),
                                        start=(fc == 0), stop=(fc == 31),
                                        inc="pe" if (fc == 31 or r == 3) else None)
                            bpD[r].setw(tm2)
                            bHs[fc].addr(tm2)
                            bfc2[fc].addr(tm2)
                            if r == 3 and e < 7 and sl == NB - 1 and fc % 2 == 1:
                                load_pair(e + 1, fc // 2, 1,
                                          list(bfc2[fc].rs) +
                                          list(bfc2[fc - 1].rs))
                    for r in range(4):
                        jt = r
                        t128 = sl * 4 + r
                        gcol = g_tm[:, t128 * 8 + e:t128 * 8 + e + 1]
                        tev = dve_ts(tmpM[jt][:], pD[r][:], gcol, None, ALU.mult,
                                     ALU.bypass,
                                     [bpD[r].w, b_gtm.w] + btmpM[jt].wr())
                        btmpM[jt].setw(tev)
                        bpD[r].addr(tev)
                        b_gtm.addr(tev)
                        tdm = pdma(accS[T0 + r * 128:T0 + r * 128 + 128, 512:1024],
                                   tmpM[jt][:], [tev] + baccS.rd(), accum=(e > 0))
                        btmpM[jt].addr(tdm)
                        baccS.setw(tdm)
            flush()
            p.barrier()
            esM2.close()

        esF = ExitStack()
        with esF:
            accsl = SB(esF, "accsl", [128, 4 * 1024])
            idenF = SB(esF, "idenF_b", [128, 128])
            bidenF = Buf()
            bidenF.setw(ldma(idenF[:], idenF_d, []))
            fc2bB = SB(esF, "fc2bB", [8, D], BF16)
            gTm = SB(esF, "gTm", [8, NTOK], BF16)
            stgm2 = SB(esF, "stgm2", [8, D])
            bfc2bB = Buf()
            bgTm = Buf()
            bstgm2 = Buf()
            td = ldma(stgm2[:], fc2b_d, bstgm2.wr())
            bstgm2.setw(td)
            tcb = dve_copy(fc2bB[:], stgm2[:], [td])
            bstgm2.addr(tcb)
            bfc2bB.setw(tcb)
            bgTm.setw(ldma(gTm[:], gTS, []))
            y1sl2 = SB(esF, "y1sl2", [128, NC * 512])
            x2sl2 = SB(esF, "x2sl2", [128, NC * 512])
            R2 = [SB(esF, f"R2_{i}", [128, 512]) for i in range(NC)]
            OUTB = SB(esF, "OUTB", [128, NC * 512])
            baccsl = Buf()
            by1sl2 = Buf()
            bx2sl2 = Buf()
            bR2 = [Buf() for _ in range(NC)]
            bOUTB = Buf()
            psf = PsA(esF, "F")
            ptrF = [psf.alloc("ptrF0", [128, 128]), psf.alloc("ptrF1", [128, 128])]
            bptrF = [Buf(), Buf()]
            pbcF = [psf.alloc(f"pbcF{i}", [128, 512]) for i in range(2)]
            bpbcF = [Buf(), Buf()]
            kit3 = LNKit(esF, "F", psf, pbcF, bpbcF)

            accS_r = accS.rearrange("(s r p) d -> s p r d", p=128, r=4)

            for tt in range(NB if PHASES >= 5 else 0):
                T0 = tt * 512
                ta = ldma(accsl[:].rearrange("p (r d) -> p r d", r=4),
                          accS_r[tt], baccsl.wr())
                baccsl.setw(ta)
                t1_ = ldma(y1sl2[:].rearrange("p (c t) -> p c t", c=NC),
                           y1S_r[:, :, T0:T0 + 512], by1sl2.wr())
                by1sl2.setw(t1_)
                t2_ = ldma(x2sl2[:].rearrange("p (c t) -> p c t", c=NC),
                           x2S_r[:, :, T0:T0 + 512], bx2sl2.wr())
                bx2sl2.setw(t2_)
                for r in range(4):
                    t128 = tt * 4 + r
                    for dc in range(NC):
                        jf = (r * NC + dc) % 2
                        pF = ptrF[jf]
                        bpF = bptrF[jf]
                        ttr = pe_tr(pF[:],
                                    accsl[:, r * 1024 + dc * 128:
                                          r * 1024 + dc * 128 + 128],
                                    idenF[:],
                                    baccsl.rd() + bpF.wr() + bidenF.rd())
                        baccsl.addr(ttr)
                        ttr = pe_mm(pF[:], fc2bB[0:8, dc * 128:dc * 128 + 128],
                                    gTm[:, t128 * 128:t128 * 128 + 128],
                                    [ttr] + bfc2bB.rd() + bgTm.rd(),
                                    start=False, stop=True, inc="pe")
                        bpF.setw(ttr)
                        bfc2bB.addr(ttr)
                        bgTm.addr(ttr)
                        da = dve_tt(ALU.add, R2[dc][:, r * 128:r * 128 + 128],
                                    pF[:],
                                    y1sl2[:, dc * 512 + r * 128:
                                          dc * 512 + r * 128 + 128],
                                    [ttr, t1_] + (bR2[dc].wr() if r == 0
                                                  else [bR2[dc].w]))
                        db = dve_tt(ALU.add, R2[dc][:, r * 128:r * 128 + 128],
                                    R2[dc][:, r * 128:r * 128 + 128],
                                    x2sl2[:, dc * 512 + r * 128:
                                          dc * 512 + r * 128 + 128],
                                    [da, t2_])
                        bR2[dc].setw(db)
                        bpF.addr(db)
                        by1sl2.addr(db)
                        bx2sl2.addr(db)
                for dc in range(NC):
                    kit3.stats(R2[dc][:], bR2[dc], dc, ones_c)
                kit3.finish(ones_r)
                outws = []
                for dc in range(NC):
                    osl = OUTB[:, dc * 512:dc * 512 + 512]
                    t1 = dve_tt(ALU.mult, osl, R2[dc][:], kit3.pbc[0][:],
                                bR2[dc].rd() + [kit3.bpbc[0].w] + bOUTB.wr())
                    t2 = dve_tt(ALU.subtract, osl, osl, kit3.pbc[1][:],
                                [t1, kit3.bpbc[1].w])
                    bOUTB.setw(t2)
                    outws.append(t2)
                    bR2[dc].addr(t2)
                    kit3.bpbc[0].addr(t2)
                    kit3.bpbc[1].addr(t2)
                ts_ = sdma(out_r[:, :, T0:T0 + 512],
                           OUTB[:].rearrange("p (c t) -> p c t", c=NC), outws)
                bOUTB.addr(ts_)
            p.op("dve",
                 lambda e: e.memset(OUTB[0:1, 0:8], 0.0),
                 [p.tok("dl"), p.tok("dacc")])
            flush()

    nc.finalize()
    return nc


def pack_cols(v):
    return np.ascontiguousarray(np.asarray(v, np.float32).reshape(-1, 128).T)


def prep_inputs(inputs, NTOK=4096):
    f32 = lambda a: np.asarray(a, np.float32)
    x = f32(inputs["L0_attn_x"])[:, :NTOK, :]
    omega = f32(inputs["L0_attn_omega"])
    com = {
        "wqT": np.ascontiguousarray(f32(inputs["L0_attn_wq"]).T),
        "wkT": np.ascontiguousarray(f32(inputs["L0_attn_wk"]).T),
        "wvT": np.ascontiguousarray(f32(inputs["L0_attn_wv"]).T),
        "woT": np.ascontiguousarray(f32(inputs["L0_attn_wo"]).T),
        "omA": np.ascontiguousarray(
            np.concatenate([omega.T, np.full((64, 64), -0.5, np.float32)],
                           axis=0)),
        "boP": pack_cols(inputs["L0_attn_bo"]),
        "Wqkv": np.ascontiguousarray(f32(inputs["L0_ffn_Wqkv"])),
        "bqkvP": pack_cols(inputs["L0_ffn_bqkv"]),
        "bqkvV": np.ascontiguousarray(
            f32(inputs["L0_ffn_bqkv"])[2048:].reshape(1, 1024)),
        "WoF": np.ascontiguousarray(f32(inputs["L0_ffn_Wo"])),
        "boFP": pack_cols(inputs["L0_ffn_bo"]),
        "ln1wP": pack_cols(inputs["L0_ffn_ln1_w"]),
        "ln1bP": pack_cols(inputs["L0_ffn_ln1_b"]),
        "ln2wP": pack_cols(inputs["L0_ffn_ln2_w"]),
        "ln2bP": pack_cols(inputs["L0_ffn_ln2_b"]),
        "gateWP": np.ascontiguousarray(
            f32(inputs["L0_ffn_gate_W"]).reshape(8, 128, 8)
            .transpose(1, 0, 2).reshape(128, 64)),
        "fc1ts": np.ascontiguousarray(
            f32(inputs["L0_ffn_fc1_w"]).transpose(0, 2, 1)
            .reshape(8, 1024, 32, 128).transpose(0, 2, 1, 3)),
        "fc2ts": np.ascontiguousarray(
            f32(inputs["L0_ffn_fc2_w"]).transpose(0, 2, 1)
            .reshape(8, 32, 128, 1024)),
        "fc1bT": np.ascontiguousarray(
            f32(inputs["L0_ffn_fc1_b"]).reshape(8, 32, 128)
            .transpose(2, 0, 1).reshape(128, 256)),
        "fc2b": np.ascontiguousarray(f32(inputs["L0_ffn_fc2_b"])),
        "iden": np.eye(128, dtype=ml_dtypes.bfloat16),
        "idenF": np.eye(128, dtype=np.float32),
        "triu": np.triu(np.ones((128, 128))).astype(ml_dtypes.bfloat16),
        "fconst": np.ascontiguousarray(
            np.broadcast_to(np.array([[-np.log(8.0), 1e-5, 1e-6]], np.float32),
                            (128, 3))),
    }
    in_maps = []
    for b in range(x.shape[0]):
        m = dict(com)
        m["xT"] = np.ascontiguousarray(x[b].T)
        in_maps.append(m)
    return in_maps


from concourse.bass_utils import run_bass_kernel_spmd


def kernel(**inputs):
    in_maps = prep_inputs(inputs, 4096)
    nc = build(4096, debug=False)
    res = run_bass_kernel_spmd(nc, in_maps, core_ids=list(range(8)))
    out = np.stack([np.asarray(r["out"], np.float32).T for r in res.results])
    return np.ascontiguousarray(out)


# version 15
# speedup vs baseline: 1.2847x; 1.0462x over previous
import numpy as np
import ml_dtypes
from contextlib import ExitStack

import concourse.bass as bass
import concourse.mybir as mybir

F32 = mybir.dt.float32
BF16 = mybir.dt.bfloat16
AF = mybir.ActivationFunctionType
ALU = mybir.AluOpType
AX = mybir.AxisListType

PHASES = 9
APART = 9
D = 1024
NC = 8
H = 16
LN8 = float(np.log(8.0))


class Buf:
    __slots__ = ("w", "rs")

    def __init__(self):
        self.w = None
        self.rs = []

    def rd(self):
        return [self.w] if self.w else []

    def wr(self):
        d = list(self.rs)
        if self.w:
            d.append(self.w)
        return d

    def setw(self, tok):
        if tok:
            self.w = tok
            self.rs = []

    def addr(self, tok):
        if tok:
            self.rs.append(tok)


class Prog:
    ENG = ("sync", "pe", "act", "dve", "pool")

    def __init__(self):
        self.ops = {e: [] for e in self.ENG}
        self.cnt = {}
        self.seen = {e: {} for e in self.ENG}
        self.floor = []

    def barrier(self):
        self.floor = [(s, c) for s, c in self.cnt.items() if c > 0]

    def tok(self, s):
        return (s, self.cnt.get(s, 0))

    def op(self, eng, fn, deps=(), inc=None, amt=1, nofloor=False):
        need = {}
        dd = list(deps)
        if not nofloor:
            dd += self.floor
        for t in dd:
            if not t:
                continue
            s, v = t
            if v <= self.seen[eng].get(s, 0):
                continue
            if need.get(s, 0) < v:
                need[s] = v
        for s, v in need.items():
            self.seen[eng][s] = v
        tok = None
        incs = []
        if inc:
            self.cnt[inc] = self.cnt.get(inc, 0) + amt
            tok = (inc, self.cnt[inc])
            incs = [(inc, amt)]
        self.ops[eng].append((fn, sorted(need.items()), incs))
        return tok


def build(NTOK=4096, debug=False):
    NB = NTOK // 512
    NT128 = NTOK // 128
    nc = bass.Bass("TRN2", target_bir_lowering=False)
    p = Prog()

    def din(name, shape, dt=F32):
        return nc.dram_tensor(name, list(shape), dt, kind="ExternalInput").ap()

    xT_d = din("xT", [D, NTOK])
    wqT_d = din("wqT", [D, D])
    wkT_d = din("wkT", [D, D])
    wvT_d = din("wvT", [D, D])
    woT_d = din("woT", [D, D])
    omA_d = din("omA", [128, 64])
    boP_d = din("boP", [128, 8])
    Wqkv_d = din("Wqkv", [D, 3 * D])
    bqkvP_d = din("bqkvP", [128, 24])
    bqkvV_d = din("bqkvV", [1, D])
    WoF_d = din("WoF", [D, D])
    boFP_d = din("boFP", [128, 8])
    ln1wP_d = din("ln1wP", [128, 8])
    ln1bP_d = din("ln1bP", [128, 8])
    ln2wP_d = din("ln2wP", [128, 8])
    ln2bP_d = din("ln2bP", [128, 8])
    gateWP_d = din("gateWP", [128, 64])
    fc1ts_d = din("fc1ts", [8, 32, D, 128])
    fc2ts_d = din("fc2ts", [8, 32, 128, D])
    fc1bT_d = din("fc1bT", [128, 256])
    fc2b_d = din("fc2b", [8, D])
    iden_d = din("iden", [128, 128], BF16)
    idenF_d = din("idenF", [128, 128])
    triu_d = din("triu", [128, 128], BF16)
    fconst_d = din("fconst", [128, 3])

    out_d = nc.dram_tensor("out", [D, NTOK], F32, kind="ExternalOutput").ap()
    skind = "ExternalOutput" if debug else "Internal"
    y1S = nc.dram_tensor("y1S", [D, NTOK], F32, kind=skind).ap()
    x2S = nc.dram_tensor("x2S", [D, NTOK], F32, kind=skind).ap()
    nx2S = nc.dram_tensor("nx2S", [D, NTOK], BF16, kind=skind).ap()
    accS = nc.dram_tensor("accS", [NTOK, D], F32, kind=skind).ap()
    gTS = nc.dram_tensor("gTS", [8, NTOK], BF16, kind="Internal").ap()

    xT_r = xT_d.rearrange("(c p) t -> p c t", p=128)
    y1S_r = y1S.rearrange("(c p) t -> p c t", p=128)
    x2S_r = x2S.rearrange("(c p) t -> p c t", p=128)
    nx2S_r = nx2S.rearrange("(c p) t -> p c t", p=128)
    out_r = out_d.rearrange("(c p) t -> p c t", p=128)

    def pe_mm(o, a, b, deps, start=True, stop=True, inc=None, tp=None):
        def f(e, o=o, a=a, b=b, start=start, stop=stop, tp=tp):
            return e.matmul(o, a, b, start=start, stop=stop, tile_position=tp,
                            skip_group_check=True)
        return p.op("pe", f, deps, inc=inc)

    def pe_tr(o, a, ident, deps, inc="pe"):
        def f(e, o=o, a=a, i=ident):
            with nc.allow_low_precision(reason="bf16 transpose of bf16 data"):
                return e.transpose(o, a, i)
        return p.op("pe", f, deps, inc=inc)

    def act(func, o, i, deps, bias=0.0, scale=1.0, inc="act"):
        def f(e, o=o, i=i, func=func, bias=bias, scale=scale):
            return e.activation(o, i, func, bias=bias, scale=scale)
        return p.op("act", f, deps, inc=inc)

    def act_copy(o, i, deps, inc="act"):
        def f(e, o=o, i=i):
            return e.copy(o, i)
        return p.op("act", f, deps, inc=inc)

    def act_add(o, i, b, deps, inc="act"):
        def f(e, o=o, i=i, b=b):
            return e.add(o, i, b)
        return p.op("act", f, deps, inc=inc)

    def dve_copy(o, i, deps, inc="dve"):
        def f(e, o=o, i=i):
            return e.tensor_copy(o, i)
        return p.op("dve", f, deps, inc=inc)

    def dve_tt(op, o, a, b, deps, inc="dve"):
        def f(e, o=o, a=a, b=b, op=op):
            return e.tensor_tensor(o, a, b, op)
        return p.op("dve", f, deps, inc=inc)

    def dve_ts(o, i, s1, s2, op0, op1, deps, inc="dve"):
        def f(e, o=o, i=i, s1=s1, s2=s2, op0=op0, op1=op1):
            return e.tensor_scalar(o, i, s1, s2, op0, op1)
        return p.op("dve", f, deps, inc=inc)

    def dve_stt(o, i, s, t, op0, op1, deps, inc="dve"):
        def f(e, o=o, i=i, s=s, t=t, op0=op0, op1=op1):
            return e.scalar_tensor_tensor(o, i, s, t, op0, op1)
        return p.op("dve", f, deps, inc=inc)

    def dve_red(o, i, op, deps, negate=False, inc="dve"):
        def f(e, o=o, i=i, op=op, negate=negate):
            return e.tensor_reduce(o, i, AX.X, op, negate=negate)
        return p.op("dve", f, deps, inc=inc)

    def dve_recip(o, i, deps, inc="dve"):
        def f(e, o=o, i=i):
            with nc.allow_low_precision(reason="recip rounded to bf16 output"):
                return e.reciprocal(o, i)
        return p.op("dve", f, deps, inc=inc)

    def dve_memset(o, val, deps, inc="dve"):
        def f(e, o=o, val=val):
            return e.memset(o, val)
        return p.op("dve", f, deps, inc=inc)

    def ldma(dst, src, deps, ch="dl", nofloor=False):
        def f(e, dst=dst, src=src):
            return e.dma_start(dst, src)
        return p.op("sync", f, list(deps) + [p.tok(ch)], inc=ch, amt=16,
                    nofloor=nofloor)

    def sdma(dst, src, deps, ch="dacc"):
        def f(e, dst=dst, src=src):
            return e.dma_start(dst, src)
        return p.op("pool", f, list(deps) + [p.tok(ch)], inc=ch, amt=16)

    def pdma(dst, src, deps, accum=False, ch="dacc"):
        def f(e, dst=dst, src=src, accum=accum):
            if accum:
                return e.dma_start(dst, src, accum_op=ALU.add)
            return e.dma_start(dst, src)
        return p.op("pool", f, list(deps) + [p.tok(ch)], inc=ch, amt=16)

    es = ExitStack()
    sems = {}
    for s in ("pe", "act", "dve", "pool", "dl", "dacc"):
        sems[s] = es.enter_context(nc.semaphore("sem_" + s))

    def flush():
        with nc.Block() as block:
            for name, deco in (("sync", block.sync), ("pe", block.tensor),
                               ("act", block.scalar), ("dve", block.vector),
                               ("pool", block.gpsimd)):
                ops = p.ops[name]
                if not ops:
                    continue

                def make(ops=ops):
                    def run(eng):
                        for fn, waits, incs in ops:
                            for s, v in waits:
                                eng.wait_ge(sems[s], v)
                            ins = fn(eng)
                            for s, a in incs:
                                ins.then_inc(sems[s], a)
                    return run
                deco(make())
        p.ops = {e: [] for e in p.ENG}

    def SB(st, name, shape, dt=F32, side=None):
        if side:
            return st.enter_context(nc.sbuf_tensor(name, list(shape), dt, side=side))
        return st.enter_context(nc.sbuf_tensor(name, list(shape), dt))

    class PsA:
        def __init__(self, st, tag):
            self.st = st
            self.tag = tag
            self.n = 0

        def alloc(self, name, shape, dt=F32):
            self.n += 1
            assert self.n <= 8, f"psum overflow {self.tag}"
            return self.st.enter_context(nc.psum_tensor(name, list(shape), dt))

    class LNKit:
        def __init__(self, st, tag, ps, pbc, bpbc, ps12=None, bps12=None):
            self.Rb = SB(st, f"{tag}_Rb", [128, 512], BF16)
            self.Rsq = SB(st, f"{tag}_Rsq", [128, 512], BF16)
            self.rA = SB(st, f"{tag}_rA", [1, 512])
            self.rB = SB(st, f"{tag}_rB", [1, 512])
            self.rC = SB(st, f"{tag}_rC", [1, 512])
            self.rowb = SB(st, f"{tag}_rowb", [1, 1024], BF16)
            if ps12 is None:
                self.ps1 = ps.alloc(f"{tag}_ps1", [1, 512])[:]
                self.ps2 = ps.alloc(f"{tag}_ps2", [1, 512])[:]
                self.bps1 = Buf()
                self.bps2 = Buf()
            else:
                self.ps1, self.ps2 = ps12
                self.bps1, self.bps2 = bps12
            self.pbc = pbc
            self.bpbc = bpbc
            self.bRb = Buf()
            self.bRsq = Buf()
            self.brow = Buf()

        def stats(self, src_ap, bsrc, dc, ones_c):
            t1 = dve_copy(self.Rb[:], src_ap, bsrc.rd() + self.bRb.wr())
            self.bRb.setw(t1)
            bsrc.addr(t1)
            t2 = act(AF.Square, self.Rsq[:], src_ap, bsrc.rd() + self.bRsq.wr())
            self.bRsq.setw(t2)
            bsrc.addr(t2)
            m1 = pe_mm(self.ps1, self.ones_c[:], self.Rb[:],
                       [t1] + (self.bps1.wr() if dc == 0 else []),
                       start=(dc == 0), stop=(dc == NC - 1), inc="pe")
            self.bps1.setw(m1)
            self.bRb.addr(m1)
            m2 = pe_mm(self.ps2, self.ones_c[:], self.Rsq[:],
                       [t2] + (self.bps2.wr() if dc == 0 else []),
                       start=(dc == 0), stop=(dc == NC - 1), inc="pe")
            self.bps2.setw(m2)
            self.bRsq.addr(m2)

        def finish(self, ones_r, eps=1e-5):
            A_, B_, C_ = self.rA, self.rB, self.rC
            d1 = dve_ts(A_[:], self.ps1, 1.0 / D, None, ALU.mult, ALU.bypass,
                        [self.bps1.w] + self.brow.wr())
            d2 = dve_ts(B_[:], self.ps2, 1.0 / D, None, ALU.mult, ALU.bypass,
                        [self.bps2.w, d1])
            self.bps1.addr(d1)
            self.bps2.addr(d2)
            d3 = dve_tt(ALU.mult, C_[:], A_[:], A_[:], [d2])
            d4 = dve_tt(ALU.subtract, B_[:], B_[:], C_[:], [d3])
            a1 = act(AF.Ln, C_[:], B_[:], [d4],
                     bias=self.fconst[0:1, 1:2])
            d5 = act(AF.Exp, B_[:], C_[:], [a1], scale=-0.5)
            d6 = dve_tt(ALU.mult, C_[:], A_[:], B_[:], [d5])
            d7 = dve_copy(self.rowb[0:1, 0:512], B_[:], [d6])
            d8 = dve_copy(self.rowb[0:1, 512:1024], C_[:], [d7])
            self.brow.setw(d8)
            t1 = pe_mm(self.pbc[0][:], self.ones_r[0:1, 0:128],
                       self.rowb[0:1, 0:512],
                       [d8] + self.bpbc[0].wr(), inc="pe")
            t2 = pe_mm(self.pbc[1][:], self.ones_r[0:1, 0:128],
                       self.rowb[0:1, 512:1024],
                       [d8] + self.bpbc[1].wr(), inc="pe")
            self.bpbc[0].setw(t1)
            self.bpbc[1].setw(t2)
            self.brow.addr(t1)
            self.brow.addr(t2)

    with es:
        triu = SB(es, "triu_b", [128, 128], BF16)
        ones_r = SB(es, "ones_r", [1, 128], BF16)
        ones_c = SB(es, "ones_c", [128, 1], BF16)
        LNKit.ones_r = ones_r
        LNKit.ones_c = ones_c
        omA = SB(es, "omA_b", [128, 64], BF16)
        boP = SB(es, "boP_b", [128, 8])
        boFP = SB(es, "boFP_b", [128, 8])
        bqkvP = SB(es, "bqkvP_b", [128, 24])
        ln1wP = SB(es, "ln1wP_b", [128, 8])
        ln1bP = SB(es, "ln1bP_b", [128, 8])
        ln2wP = SB(es, "ln2wP_b", [128, 8])
        ln2bP = SB(es, "ln2bP_b", [128, 8])
        gateWb = SB(es, "gateWb", [128, 64], BF16)
        fconst = SB(es, "fconst_b", [128, 3])
        LNKit.fconst = None
        g_tm = SB(es, "g_tm", [128, NT128 * 8])
        b_small = Buf()
        b_gtm = Buf()

        stg0 = SB(es, "stg0", [128, 64])
        bstg0 = Buf()

        LNKit.fconst = fconst
        for dst, src in ((boP, boP_d), (boFP, boFP_d), (bqkvP, bqkvP_d),
                         (ln1wP, ln1wP_d), (ln1bP, ln1bP_d),
                         (ln2wP, ln2wP_d), (ln2bP, ln2bP_d), (triu, triu_d),
                         (fconst, fconst_d)):
            b_small.addr(ldma(dst[:], src, []))
        t1 = ldma(stg0[:, 0:64], omA_d, [])
        bstg0.setw(t1)
        tc = dve_copy(omA[:], stg0[:, 0:64], [t1])
        bstg0.addr(tc)
        b_small.addr(tc)
        t1 = ldma(stg0[:, 0:64], gateWP_d, bstg0.wr())
        bstg0.setw(t1)
        tc = dve_copy(gateWb[:], stg0[:, 0:64], [t1])
        bstg0.addr(tc)
        b_small.addr(tc)
        b_small.addr(dve_memset(ones_r[:], 1.0, []))
        b_small.addr(dve_memset(ones_c[:], 1.0, []))

        esO = ExitStack()
        with esO:
            es1 = ExitStack()
            with es1:
                NX1 = [SB(es1, f"NX1_{c}", [128, NTOK], BF16) for c in range(NC)]
                bNX1 = [Buf() for _ in range(NC)]

                esA = ExitStack()
                with esA:
                    wQ = SB(esA, "wQ", [128, NC * D], BF16)
                    wK = SB(esA, "wK", [128, NC * D], BF16)
                    wV = SB(esA, "wV", [128, NC * D], BF16)
                    wO = SB(esA, "wO", [128, NC * D], BF16)
                    xb = SB(esA, "xb", [128, NC * 512], BF16)
                    xstg = SB(esA, "xstg", [128, 2048])
                    combq = [[SB(esA, f"combq{g}{i}", [128, 512], BF16)
                              for i in range(2)] for g in range(2)]
                    combk = [[SB(esA, f"combk{g}{i}", [128, 512], BF16)
                              for i in range(2)] for g in range(2)]
                    fkT = [[SB(esA, f"fkT{g}{i}", [128, 128], BF16)
                            for i in range(4)] for g in range(2)]
                    fq = [[combq[g][jj][0:64, :] for jj in range(2)]
                          for g in range(2)]
                    fk = [[combk[g][jj][0:64, :] for jj in range(2)]
                          for g in range(2)]
                    amS = SB(esA, "amS", [128, 256], BF16)
                    v2 = [SB(esA, f"v2_{i}", [128, H * 65], BF16) for i in range(4)]
                    SRUN = SB(esA, "SRUN", [64, H * 65])
                    Ssb = [SB(esA, f"Ssb{g}", [64, 2 * 4 * 65], BF16)
                           for g in range(2)]
                    srun_bf = [SB(esA, f"srun_bf{g}", [64, 2 * 65], BF16)
                               for g in range(2)]
                    csc = [SB(esA, f"csc{i}", [128, 512], BF16) for i in range(NC)]
                    RES = SB(esA, "RES", [128, NC * 512])
                    APL = SB(esA, "APL", [128, 512])

                    psa = PsA(esA, "A")
                    pq = psa.alloc("pq", [128, 512])
                    pk = psa.alloc("pk", [128, 512])
                    Apack = psa.alloc("Apack", [128, 512])
                    pA_ = Apack[:, 0:128]
                    po = [psa.alloc("poT0", [65, 128])[:],
                          psa.alloc("poT1", [65, 128])[:]]
                    phiA = pq[0:64, 0:512]
                    phiB = pk[0:64, 0:512]
                    ptmA = pq[:, 0:64]
                    ptmB = pk[:, 0:64]
                    pwo = [psa.alloc(f"pwo{i}", [128, 512]) for i in range(2)]
                    pScT = psa.alloc("pScT", [64, 65])
                    pSc = [pScT[:], pwo[0][0:64, 0:65]]
                    pzb = pwo[1][0:64, 0:128]
                    bpwo = None

                    bxb, bAPL = (Buf() for _ in range(2))
                    bSsb = [[Buf(), Buf()], [Buf(), Buf()]]
                    bsrun_bf = [[Buf(), Buf()], [Buf(), Buf()]]
                    bxstg = [Buf(), Buf()]
                    bw = Buf()
                    bcombq = [[Buf(), Buf()], [Buf(), Buf()]]
                    bcombk = [[Buf(), Buf()], [Buf(), Buf()]]
                    bfq = bcombq
                    bfk = bcombk
                    bam = Buf()
                    bfkT = [[Buf() for _ in range(4)] for _ in range(2)]
                    bv2 = [Buf() for _ in range(4)]
                    bSRUN = [Buf() for _ in range(H)]
                    bcsc = [Buf() for _ in range(NC)]
                    bRES = [Buf() for _ in range(NC)]
                    bpq, bpk, bpA = (Buf() for _ in range(3))
                    bpo = [Buf(), Buf()]
                    bpwo = [Buf(), Buf()]
                    bpzb = bpwo[1]
                    bpSc = [Buf(), bpwo[0]]
                    kit = LNKit(esA, "A", psa, pwo, bpwo,
                                ps12=(pq[0:1, 0:512], pk[0:1, 0:512]),
                                bps12=(bpq, bpk))
                    bzrs = kit.brow
                    bzrecb = kit.brow
                    zrs = kit.rA[0:1, 0:128]
                    zrecb = kit.rowb.bitcast(BF16)[0:1, 0:128] if False else \
                        kit.rowb[0:1, 0:128]

                    wi = 0
                    for wsb, wd in ((wQ, wqT_d), (wK, wkT_d), (wV, wvT_d),
                                    (wO, woT_d)):
                        wr_ = wd.rearrange("(c p) i -> p c i", p=128)
                        for c in range(NC):
                            j = wi % 2
                            wi += 1
                            sl_ = xstg[:, j * 1024:j * 1024 + 1024]
                            td = ldma(sl_, wr_[:, c, :], bxstg[j].wr())
                            bxstg[j].setw(td)
                            tcw = dve_copy(wsb[:, c * D:(c + 1) * D], sl_, [td])
                            bxstg[j].addr(tcw)
                            bw.addr(tcw)

                    for i in range(4):
                        bv2[i].setw(dve_memset(v2[i][:], 1.0, []))
                    tz = dve_memset(SRUN[:], 0.0, [])
                    for h in range(H):
                        bSRUN[h].setw(tz)

                    for b in range(NB if PHASES >= 1 else 0):
                        T0 = b * 512
                        for hf in range(2):
                            sl_ = xstg[:].rearrange("p (c t) -> p c t", c=4)
                            td = ldma(sl_, xT_r[:, hf * 4:hf * 4 + 4, T0:T0 + 512],
                                      bxstg[0].wr() + bxstg[1].wr())
                            bxstg[0].setw(td)
                            bxstg[1].setw(td)
                            tcx = dve_copy(xb[:, hf * 2048:hf * 2048 + 2048],
                                           xstg[:],
                                           [td] + (bxb.wr() if hf == 0 else []))
                            bxb.setw(tcx)
                            bxstg[0].addr(tcx)
                            bxstg[1].addr(tcx)

                        for r in range(4 if APART >= 1 else 0):
                            for half in range(2):
                                pv = pwo[half]
                                bp = bpwo[half]
                                deps = bxb.rd() + bp.wr() + list(bw.rs)
                                tm = None
                                for kc in range(NC):
                                    tm = pe_mm(
                                        pv[:],
                                        xb[:, kc * 512 + r * 128:kc * 512 + r * 128 + 128],
                                        wV[:, kc * D + half * 512:kc * D + half * 512 + 512],
                                        deps if kc == 0 else [],
                                        start=(kc == 0), stop=(kc == NC - 1),
                                        inc="pe" if kc == NC - 1 else None)
                                bp.setw(tm)
                                bxb.addr(tm)
                                dst = v2[r][:].rearrange("p (h x) -> p h x", h=H)[
                                    :, half * 8:half * 8 + 8, 0:64]
                                src = pv[:].rearrange("p (h x) -> p h x", x=64)
                                te = dve_copy(dst, src, [tm] + bv2[r].wr())
                                bv2[r].setw(te)
                                bp.addr(te)

                        def prep(hp):
                            g = hp % 2
                            h0, h1 = 2 * hp, 2 * hp + 1
                            for pp_, bpp_, wsb in ((pq, bpq, wQ), (pk, bpk, wK)):
                                deps = bxb.rd() + bpp_.wr()
                                tm = None
                                for kc in range(NC):
                                    tm = pe_mm(
                                        pp_[:],
                                        wsb[:, kc * D + hp * 128:kc * D + hp * 128 + 128],
                                        xb[:, kc * 512:kc * 512 + 512],
                                        deps if kc == 0 else [],
                                        start=(kc == 0), stop=(kc == NC - 1),
                                        inc="pe" if kc == NC - 1 else None)
                                bpp_.setw(tm)
                                bxb.addr(tm)
                            for pp_, bpp_, comb, bcomb in (
                                    (pq, bpq, combq[g], bcombq[g]),
                                    (pk, bpk, combk[g], bcombk[g])):
                                for j in range(2):
                                    t1 = dve_copy(comb[j][0:64, :],
                                                  pp_[j * 64:j * 64 + 64, :],
                                                  [bpp_.w] + bcomb[j].wr())
                                    t2 = act(AF.Square, comb[j][64:128, :],
                                             pp_[j * 64:j * 64 + 64, :],
                                             [bpp_.w] + bcomb[j].wr() + [t1])
                                    bcomb[j].setw(t2)
                                    bcomb[j].addr(t1)
                                    bpp_.addr(t1)
                                    bpp_.addr(t2)
                            for comb, bcomb, f_, php, bphp in (
                                    (combq[g][0], bcombq[g][0], fq[g][0], phiA, bpq),
                                    (combq[g][1], bcombq[g][1], fq[g][1], phiB, bpk)):
                                tm = pe_mm(php, omA[:], comb[:],
                                           bcomb.rd() + list(bcomb.rs) +
                                           bphp.wr() + [b_small.w], inc="pe")
                                bphp.setw(tm)
                                bcomb.addr(tm)
                                te = act(AF.Exp, f_, php,
                                         [tm] + bcomb.wr(),
                                         bias=fconst[0:64, 0:1])
                                bcomb.setw(te)
                                bphp.addr(te)
                            for j in range(2):
                                for r in range(4):
                                    even = ((j * 4 + r) % 2 == 0)
                                    ptm = ptmA if even else ptmB
                                    bptm = bpq if even else bpk
                                    tt_ = pe_mm(ptm,
                                                combk[g][j][:, r * 128:r * 128 + 128],
                                                omA[:],
                                                bcombk[g][j].rd() +
                                                list(bcombk[g][j].rs) +
                                                bptm.wr() + [b_small.w], inc="pe")
                                    bptm.setw(tt_)
                                    bcombk[g][j].addr(tt_)
                                    tc2 = act(AF.Exp,
                                              fkT[g][r][:, j * 64:j * 64 + 64],
                                              ptm, [tt_] + bfkT[g][r].wr(),
                                              bias=fconst[:, 0:1])
                                    bfkT[g][r].setw(tc2)
                                    bptm.addr(tc2)
                            for comb, bcomb, f_, php, bphp in (
                                    (combk[g][0], bcombk[g][0], fk[g][0], phiA, bpq),
                                    (combk[g][1], bcombk[g][1], fk[g][1], phiB, bpk)):
                                tm = pe_mm(php, omA[:], comb[:],
                                           bcomb.rd() + list(bcomb.rs) +
                                           bphp.wr() + [b_small.w], inc="pe")
                                bphp.setw(tm)
                                bcomb.addr(tm)
                                te = act(AF.Exp, f_, php,
                                         [tm] + bcomb.wr(),
                                         bias=fconst[0:64, 0:1])
                                bcomb.setw(te)
                                bphp.addr(te)
                            for j, h in enumerate((h0, h1)):
                                srun_sl = SRUN[:, h * 65:h * 65 + 65]
                                sbsl = srun_bf[g][:, j * 65:j * 65 + 65]
                                t40 = dve_copy(sbsl, srun_sl,
                                               bSRUN[h].rd() + bsrun_bf[g][j].wr())
                                bsrun_bf[g][j].setw(t40)
                                bSRUN[h].addr(t40)
                                for c in range(4):
                                    pscb = pSc[c % 2]
                                    bpscb = bpSc[c % 2]
                                    tS = pe_mm(pscb,
                                               fkT[g][c][:, j * 64:j * 64 + 64],
                                               v2[c][:, h * 65:h * 65 + 65],
                                               bfkT[g][c].rd() + bv2[c].rd() +
                                               bpscb.wr(), inc="pe")
                                    bpscb.setw(tS)
                                    bfkT[g][c].addr(tS)
                                    bv2[c].addr(tS)
                                    ssl = Ssb[g][:, (j * 4 + c) * 65:
                                                (j * 4 + c) * 65 + 65]
                                    tE = dve_copy(ssl, pscb,
                                                  [tS] + bSsb[g][j].wr())
                                    bSsb[g][j].setw(tE)
                                    bpscb.addr(tE)
                                    tU = dve_tt(ALU.add, srun_sl, srun_sl, pscb,
                                                [tS, tE] + bSRUN[h].wr())
                                    bSRUN[h].setw(tU)
                                    bpscb.addr(tU)

                        def chunk_stage(hp):
                            g = hp % 2
                            h0, h1 = 2 * hp, 2 * hp + 1
                            for j, h in enumerate((h0, h1)):

                                def emit_A(c, j=j, h=h):
                                    ta = pe_mm(pA_,
                                               fk[g][j][:, c * 128:c * 128 + 128],
                                               fq[g][j][:, c * 128:c * 128 + 128],
                                               bfk[g][j].rd() + bfq[g][j].rd() +
                                               bpA.wr(), inc="pe")
                                    bpA.setw(ta)
                                    bfk[g][j].addr(ta)
                                    bfq[g][j].addr(ta)
                                    am = amS[:, (c % 2) * 128:
                                             (c % 2) * 128 + 128]
                                    tmsk = dve_tt(ALU.mult, am, pA_, triu[:],
                                                  [ta] + bam.wr())
                                    bam.setw(tmsk)
                                    bam.addr(tmsk)
                                    bpA.addr(tmsk)
                                    return am, tmsk

                                def emit_pv(c, am, tmsk, j=j, h=h):
                                    pob = po[c % 2]
                                    bpob = bpo[c % 2]
                                    vsl = v2[c][:, h * 65:h * 65 + 65]
                                    pe_mm(pob, vsl, am,
                                          [tmsk] + bv2[c].rd() + bpob.wr(),
                                          start=True, stop=False, inc=None)
                                    bam.addr(None)
                                    deps = bsrun_bf[g][j].rd() + bSsb[g][j].rd()
                                    t2_ = pe_mm(pob,
                                                srun_bf[g][:, j * 65:j * 65 + 65],
                                                fq[g][j][:, c * 128:c * 128 + 128],
                                                deps + bfq[g][j].rd(),
                                                start=False, stop=(c == 0),
                                                inc="pe" if c == 0 else None)
                                    for cp in range(c):
                                        t2_ = pe_mm(
                                            pob,
                                            Ssb[g][:, (j * 4 + cp) * 65:
                                                   (j * 4 + cp) * 65 + 65],
                                            fq[g][j][:, c * 128:c * 128 + 128],
                                            [], start=False, stop=(cp == c - 1),
                                            inc="pe" if cp == c - 1 else None)
                                    bpob.setw(t2_)
                                    bsrun_bf[g][j].addr(t2_)
                                    bSsb[g][j].addr(t2_)
                                    bfq[g][j].addr(t2_)
                                    bv2[c].addr(t2_)
                                    bam.addr(t2_)
                                    return t2_

                                def emit_z(c, t2_, j=j, h=h):
                                    pob = po[c % 2]
                                    bpob = bpo[c % 2]
                                    t5 = act(AF.Ln, zrs, pob[64:65, :],
                                             [t2_, b_small.w] + bzrs.wr(),
                                             bias=fconst[0:1, 2:3])
                                    bzrs.setw(t5)
                                    bpob.addr(t5)
                                    t7 = act(AF.Exp, zrecb, zrs,
                                             [t5] + bzrecb.wr(), scale=-1.0)
                                    bzrs.addr(t7)
                                    bzrecb.setw(t7)
                                    t8 = pe_mm(pzb, ones_r[0:1, 0:64], zrecb,
                                               [t7] + bpzb.wr(), inc="pe")
                                    bpzb.setw(t8)
                                    bzrecb.addr(t8)
                                    osb = combk[g][j][64:128, (c % 2) * 128:
                                                      (c % 2) * 128 + 128]
                                    t9 = dve_copy(osb, pob[0:64, :],
                                                  [t2_] + bcombk[g][j].wr())
                                    bcombk[g][j].setw(t9)
                                    bcombk[g][j].addr(t9)
                                    bpob.addr(t9)
                                    dst = csc[hp][j * 64:j * 64 + 64,
                                                  c * 128:c * 128 + 128]
                                    t10 = dve_tt(ALU.mult, dst, osb, pzb,
                                                 [t9, t8] + bcsc[hp].wr())
                                    bcsc[hp].setw(t10)
                                    bpzb.addr(t10)
                                    bcombk[g][j].addr(t10)

                                am0, tm0 = emit_A(0)
                                am1, tm1 = emit_A(1)
                                t2a = emit_pv(0, am0, tm0)
                                am2, tm2 = emit_A(2)
                                t2b = emit_pv(1, am1, tm1)
                                emit_z(0, t2a)
                                am3, tm3 = emit_A(3)
                                t2c = emit_pv(2, am2, tm2)
                                emit_z(1, t2b)
                                t2d = emit_pv(3, am3, tm3)
                                emit_z(2, t2c)
                                emit_z(3, t2d)

                        if APART >= 2:
                            prep(0)
                            for hp in range(8):
                                if hp + 1 < 8:
                                    prep(hp + 1)
                                if APART >= 5:
                                    chunk_stage(hp)
                        for dc in range(NC if APART >= 6 else 0):
                            j = dc % 2
                            if dc % 4 == 0:
                                hf = dc // 4
                                sl_ = xstg[:].rearrange("p (c t) -> p c t", c=4)
                                tdx4 = ldma(sl_,
                                            xT_r[:, hf * 4:hf * 4 + 4, T0:T0 + 512],
                                            bxstg[0].wr() + bxstg[1].wr())
                                bxstg[0].setw(tdx4)
                                bxstg[1].setw(tdx4)
                            tdx = tdx4
                            pv = pwo[j]
                            bp = bpwo[j]
                            tm = None
                            for ic in range(NC):
                                deps2 = ((bp.wr() + list(bw.rs)) if ic == 0 else []) \
                                    + bcsc[ic].rd()
                                tm = pe_mm(
                                    pv[:],
                                    wO[:, ic * D + dc * 128:ic * D + dc * 128 + 128],
                                    csc[ic][:, 0:512], deps2,
                                    start=(ic == 0), stop=(ic == NC - 1),
                                    inc="pe" if ic == NC - 1 else None)
                            bp.setw(tm)
                            for ic in range(NC):
                                bcsc[ic].addr(tm)
                            tr_ = dve_stt(RES[:, dc * 512:dc * 512 + 512], pv[:],
                                          boP[:, dc:dc + 1],
                                          xstg[:, (dc % 4) * 512:(dc % 4) * 512 + 512],
                                          ALU.add, ALU.add,
                                          [tm, tdx, b_small.w] + bRES[dc].wr())
                            bRES[dc].setw(tr_)
                            bp.addr(tr_)
                            bxstg[0].addr(tr_)
                            bxstg[1].addr(tr_)
                        for dc in range(NC if APART >= 6 else 0):
                            kit.stats(RES[:, dc * 512:dc * 512 + 512], bRES[dc], dc, ones_c)
                        if APART >= 6:
                            kit.finish(ones_r)
                        for dc in range(NC if APART >= 6 else 0):
                            rsl = RES[:, dc * 512:dc * 512 + 512]
                            t1 = dve_tt(ALU.mult, rsl, rsl,
                                        kit.pbc[0][:],
                                        bRES[dc].wr() + [kit.bpbc[0].w])
                            t2 = dve_tt(ALU.subtract, rsl, rsl,
                                        kit.pbc[1][:], [t1, kit.bpbc[1].w])
                            bRES[dc].setw(t2)
                            kit.bpbc[0].addr(t2)
                            kit.bpbc[1].addr(t2)
                        ts_ = sdma(y1S_r[:, :, T0:T0 + 512],
                                   RES[:].rearrange("p (c t) -> p c t", c=NC),
                                   [b.w for b in bRES])
                        for dc in range(NC):
                            bRES[dc].addr(ts_)
                        for dc in range(NC if APART >= 6 else 0):
                            kit.stats(RES[:, dc * 512:dc * 512 + 512], bRES[dc], dc, ones_c)
                        if APART >= 6:
                            kit.finish(ones_r)
                        for dc in range(NC if APART >= 6 else 0):
                            t1 = dve_tt(ALU.mult, APL[:],
                                        RES[:, dc * 512:dc * 512 + 512], kit.pbc[0][:],
                                        bRES[dc].rd() + [kit.bpbc[0].w] + bAPL.wr())
                            t2 = dve_tt(ALU.subtract, APL[:], APL[:], kit.pbc[1][:],
                                        [t1, kit.bpbc[1].w])
                            bRES[dc].addr(t2)
                            kit.bpbc[0].addr(t2)
                            kit.bpbc[1].addr(t2)
                            t3 = dve_ts(NX1[dc][:, T0:T0 + 512], APL[:],
                                        ln1wP[:, dc:dc + 1], ln1bP[:, dc:dc + 1],
                                        ALU.mult, ALU.add,
                                        [t2, b_small.w] + bNX1[dc].wr())
                            bAPL.setw(t3)
                            bNX1[dc].setw(t3)
                    flush()
                    p.barrier()

                O_sb = [SB(esO, f"O_{c}", [128, NTOK], BF16, side="right")
                        for c in range(NC)]
                bO = [Buf() for _ in range(NC)]
                esB2 = ExitStack()
                with esB2:
                    Qg = [SB(esB2, f"Qg{i}", [128, NTOK], BF16) for i in range(2)]
                    Kg = [SB(esB2, f"Kg{i}", [128, NTOK], BF16) for i in range(2)]
                    Vg = [SB(esB2, f"Vg{i}", [128, 4 * 65], BF16)
                          for i in range(NT128)]
                    wg = SB(esB2, "wg", [128, NC * 768], BF16)
                    bqkvV = SB(esB2, "bqkvV_b", [1, D], BF16)
                    stgB = [SB(esB2, f"stgB{i}", [128, 512]) for i in range(2)]
                    P_sb = [[SB(esB2, f"P_{s}{i}", [128, 512], BF16)
                             for i in range(3)] for s in range(2)]
                    osb2 = SB(esB2, "osb2", [64, 512], BF16)
                    zrs2 = SB(esB2, "zrs2", [1, 512])
                    zrecb2 = SB(esB2, "zrecb2", [1, 512], BF16)
                    bQg = [Buf(), Buf()]
                    bKg = [Buf(), Buf()]
                    bVg = [Buf() for _ in range(NT128)]
                    bwg = Buf()
                    bbqkvV = Buf()
                    bstgB = [Buf(), Buf()]
                    bP = [[Buf(), Buf(), Buf()], [Buf(), Buf(), Buf()]]
                    bosb2, bzrs2, bzrecb2 = Buf(), Buf(), Buf()

                    psb = PsA(esB2, "B")
                    pqk = [[psb.alloc(f"pqk{s}{i}", [128, 512])
                            for i in range(2)] for s in range(2)]
                    po2 = [psb.alloc("po2A", [65, 512]),
                           psb.alloc("po2B", [65, 512])]
                    pzb2 = psb.alloc("pzb2", [64, 512])
                    pvtm = psb.alloc("pvtm", [128, 256])
                    bpqk = [[Buf(), Buf()], [Buf(), Buf()]]
                    bpo2 = [Buf(), Buf()]
                    bpzb2, bpvtm = Buf(), Buf()

                    td = ldma(stgB[0][0:1, 0:512], bqkvV_d[0:1, 0:512],
                              bstgB[0].wr())
                    bstgB[0].setw(td)
                    tc0 = dve_copy(bqkvV[0:1, 0:512], stgB[0][0:1, 0:512], [td])
                    bstgB[0].addr(tc0)
                    td = ldma(stgB[1][0:1, 0:512], bqkvV_d[0:1, 512:1024],
                              bstgB[1].wr())
                    bstgB[1].setw(td)
                    tc = dve_copy(bqkvV[0:1, 512:1024], stgB[1][0:1, 0:512],
                                  [td, tc0])
                    bstgB[0].addr(tc)
                    bbqkvV.setw(tc)

                    for i in range(NT128):
                        bVg[i].setw(dve_memset(Vg[i][:], 1.0, []))

                    Wq_r = Wqkv_d.rearrange("(c p) o -> p c o", p=128)
                    for g in range(4 if PHASES >= 2 else 0):
                        for part, coff in ((0, g * 256), (1, D + g * 256),
                                           (2, 2 * D + g * 256)):
                            for qc in range(4):
                                j = qc % 2
                                td = ldma(
                                    stgB[j][:].rearrange("p (c o) -> p c o", c=2),
                                    Wq_r[:, qc * 2:qc * 2 + 2, coff:coff + 256],
                                    bstgB[j].wr())
                                bstgB[j].setw(td)
                                dst = wg[:].rearrange("p (c o) -> p c o", o=768)[
                                    :, qc * 2:qc * 2 + 2,
                                    part * 256:part * 256 + 256]
                                tcw = dve_copy(
                                    dst,
                                    stgB[j][:].rearrange("p (c o) -> p c o", c=2),
                                    [td] + (bwg.wr() if (part == 0 and qc == 0)
                                            else [bwg.w]))
                                bwg.setw(tcw)
                                bstgB[j].addr(tcw)

                        for oc in range(2):
                            cq = g * 2 + oc
                            for dstT, bdstT, wcol, biascol in (
                                    (Qg[oc], bQg[oc], oc * 128, cq),
                                    (Kg[oc], bKg[oc], 256 + oc * 128, 8 + cq)):
                                for tt in range(NB):
                                    pp_ = pqk[0][tt % 2]
                                    bpp_ = bpqk[0][tt % 2]
                                    deps = bwg.rd() + bpp_.wr() + \
                                        (bdstT.wr() if tt == 0 else [])
                                    tm = None
                                    for kc in range(NC):
                                        tm = pe_mm(
                                            pp_[:],
                                            wg[:, kc * 768 + wcol:kc * 768 + wcol + 128],
                                            NX1[kc][:, tt * 512:tt * 512 + 512],
                                            (deps + bNX1[kc].rd()) if kc == 0 else [],
                                            start=(kc == 0), stop=(kc == NC - 1),
                                            inc="pe" if kc == NC - 1 else None)
                                        bNX1[kc].addr(tm)
                                    bpp_.setw(tm)
                                    bwg.addr(tm)
                                    te = act_add(
                                        dstT[:, tt * 512:tt * 512 + 512],
                                        pp_[:],
                                        bqkvP[:, biascol:biascol + 1],
                                        [tm, b_small.w])
                                    bdstT.setw(te)
                                    bpp_.addr(te)
                        for t128 in range(NT128):
                            deps = bwg.rd() + bpvtm.wr() + bbqkvV.rd()
                            for kc in range(NC):
                                pe_mm(pvtm[:],
                                      NX1[kc][:, t128 * 128:t128 * 128 + 128],
                                      wg[:, kc * 768 + 512:kc * 768 + 768],
                                      (deps + bNX1[kc].rd()) if kc == 0 else [],
                                      start=(kc == 0), stop=False, inc=None)
                            tm = pe_mm(pvtm[:], ones_r[0:1, 0:128],
                                       bqkvV[0:1, g * 256:g * 256 + 256],
                                       [], start=False, stop=True, inc="pe")
                            bpvtm.setw(tm)
                            bwg.addr(tm)
                            bbqkvV.addr(tm)
                            for kc in range(NC):
                                bNX1[kc].addr(tm)
                            dst = Vg[t128][:].rearrange("p (h x) -> p h x", h=4)[
                                :, :, 0:64]
                            src = pvtm[:].rearrange("p (h x) -> p h x", x=64)
                            te = dve_copy(dst, src, [tm] + bVg[t128].wr())
                            bVg[t128].setw(te)
                            bpvtm.addr(te)

                        for oc in range(2):
                            for qt in range(NB):
                                kmax = 4 * qt + 3

                                def emit_score(s, jj, oc=oc, qt=qt):
                                    ro = s * 64
                                    pp_ = pqk[s][jj % 2]
                                    bpp_ = bpqk[s][jj % 2]
                                    ts_ = pe_mm(
                                        pp_[:],
                                        Kg[oc][ro:ro + 64, jj * 128:jj * 128 + 128],
                                        Qg[oc][ro:ro + 64, qt * 512:qt * 512 + 512],
                                        bKg[oc].rd() + bQg[oc].rd() + bpp_.wr(),
                                        inc="pe", tp=(ro, 0))
                                    bpp_.setw(ts_)
                                    bKg[oc].addr(ts_)
                                    bQg[oc].addr(ts_)
                                    Pb = P_sb[s][jj % 3]
                                    bPb = bP[s][jj % 3]
                                    if jj < 4 * qt:
                                        te = act(AF.Exp, Pb[:], pp_[:],
                                                 [ts_] + bPb.wr(), scale=0.125)
                                        bPb.setw(te)
                                        bpp_.addr(te)
                                    else:
                                        r = jj - 4 * qt
                                        deps = [ts_] + bPb.wr()
                                        tz2 = None
                                        if r > 0:
                                            tz2 = dve_memset(Pb[:, 0:128 * r],
                                                             0.0, deps)
                                        te0 = act(AF.Exp, Pb[:, 128 * r:512],
                                                  pp_[:, 128 * r:512], deps,
                                                  scale=0.125)
                                        te = dve_tt(ALU.mult,
                                                    Pb[:, 128 * r:128 * r + 128],
                                                    Pb[:, 128 * r:128 * r + 128],
                                                    triu[:],
                                                    [te0] + ([tz2] if tz2 else []))
                                        bPb.setw(te)
                                        bpp_.addr(te0)

                                def emit_pv(s, jj, oc=oc, qt=qt):
                                    hh = 2 * oc + s
                                    kmax = 4 * qt + 3
                                    Pb = P_sb[s][jj % 3]
                                    bPb = bP[s][jj % 3]
                                    tpv = pe_mm(po2[s][:],
                                                Vg[jj][:, hh * 65:hh * 65 + 65],
                                                Pb[:],
                                                bPb.rd() + bVg[jj].rd() +
                                                (bpo2[s].wr() if jj == 0 else []),
                                                start=(jj == 0), stop=(jj == kmax),
                                                inc="pe")
                                    bpo2[s].setw(tpv)
                                    bPb.addr(tpv)
                                    bVg[jj].addr(tpv)

                                emit_score(0, 0)
                                emit_score(1, 0)
                                if kmax >= 1:
                                    emit_score(0, 1)
                                    emit_score(1, 1)
                                for jj in range(kmax + 1):
                                    if jj + 2 <= kmax:
                                        emit_score(0, jj + 2)
                                        emit_score(1, jj + 2)
                                    emit_pv(0, jj)
                                    emit_pv(1, jj)

                                for s in range(2):
                                    h = 4 * g + 2 * oc + s
                                    tlast = bpo2[s].w
                                    t5 = act(AF.Ln, zrs2[:], po2[s][64:65, :],
                                             [tlast] + bzrs2.wr())
                                    bzrs2.setw(t5)
                                    bpo2[s].addr(t5)
                                    t7 = act(AF.Exp, zrecb2[:], zrs2[:],
                                             [t5] + bzrecb2.wr(), scale=-1.0)
                                    bzrecb2.setw(t7)
                                    bzrs2.addr(t7)
                                    t8 = pe_mm(pzb2[:], ones_r[0:1, 0:64],
                                               zrecb2[:],
                                               [t7] + bpzb2.wr(), inc="pe")
                                    bpzb2.setw(t8)
                                    bzrecb2.addr(t8)
                                    t9 = dve_copy(osb2[:], po2[s][0:64, :],
                                                  [tlast] + bosb2.wr())
                                    bosb2.setw(t9)
                                    bpo2[s].addr(t9)
                                    ch = h // 2
                                    dst = O_sb[ch][(h % 2) * 64:(h % 2) * 64 + 64,
                                                   qt * 512:qt * 512 + 512]
                                    t10 = dve_tt(ALU.mult, dst, osb2[:], pzb2[:],
                                                 [t9, t8] + bO[ch].wr())
                                    bO[ch].setw(t10)
                                    bpzb2.addr(t10)
                                    bosb2.addr(t10)
                    flush()
                    p.barrier()

            esB3 = ExitStack()
            with esB3:
                woF = SB(esB3, "woF", [128, NC * D], BF16)
                stgC = [SB(esB3, f"stgC{i}", [128, 1024]) for i in range(2)]
                y1sl = SB(esB3, "y1sl", [128, NC * 512])
                X2blk = SB(esB3, "X2blk", [128, NC * 512])
                NX2blk = SB(esB3, "NX2blk", [128, NC * 512], BF16)
                APL2 = SB(esB3, "APL2", [128, 512])
                gsc = SB(esB3, "gsc", [128, 64])
                gbit = SB(esB3, "gbit", [128, 8], BF16)
                gtrow = SB(esB3, "gtrow", [8, 128], BF16)
                iden = SB(esB3, "iden_b", [128, 128], BF16)
                biden = Buf()
                biden.setw(ldma(iden[:], iden_d, []))
                bwoF = Buf()
                bstgC = [Buf(), Buf()]
                by1sl = Buf()
                bX2 = [Buf() for _ in range(NC)]
                bNX2 = [Buf() for _ in range(NC)]
                bAPL2 = Buf()
                bgsc = Buf()
                bgbit = Buf()
                bgtrow = Buf()

                psc = PsA(esB3, "C")
                pwoF = [psc.alloc(f"pwoF{i}", [128, 512]) for i in range(2)]
                bpwoF = [Buf(), Buf()]
                pg = psc.alloc("pg", [128, 8])
                bpg = Buf()
                pgt = psc.alloc("pgt", [8, 128], BF16)
                bpgt = Buf()
                kit2 = LNKit(esB3, "C", psc, pwoF, bpwoF)

                for c in range(NC if PHASES >= 3 else 0):
                    wr_ = WoF_d.rearrange("(c p) o -> p c o", p=128)
                    j = c % 2
                    td = ldma(stgC[j][:], wr_[:, c, :], bstgC[j].wr())
                    bstgC[j].setw(td)
                    tcw = dve_copy(woF[:, c * D:(c + 1) * D],
                                   stgC[j][:], [td] + (bwoF.wr() if c == 0
                                                       else [bwoF.w]))
                    bwoF.setw(tcw)
                    bstgC[j].addr(tcw)

                for tt in range(NB if PHASES >= 3 else 0):
                    T0 = tt * 512
                    tld = ldma(y1sl[:].rearrange("p (c t) -> p c t", c=NC),
                               y1S_r[:, :, T0:T0 + 512], by1sl.wr())
                    by1sl.setw(tld)
                    for dc in range(NC):
                        pv = pwoF[dc % 2]
                        bp = bpwoF[dc % 2]
                        tm = None
                        for ic in range(NC):
                            deps2 = (bp.wr() if ic == 0 else []) + bO[ic].rd()
                            tm = pe_mm(
                                pv[:],
                                woF[:, ic * D + dc * 128:ic * D + dc * 128 + 128],
                                O_sb[ic][:, T0:T0 + 512],
                                deps2 + (bwoF.rd() if ic == 0 else []),
                                start=(ic == 0), stop=(ic == NC - 1),
                                inc="pe" if ic == NC - 1 else None)
                        bp.setw(tm)
                        for ic in range(NC):
                            bO[ic].addr(tm)
                        bwoF.addr(tm)
                        tr_ = dve_stt(X2blk[:, dc * 512:dc * 512 + 512], pv[:],
                                      boFP[:, dc:dc + 1],
                                      y1sl[:, dc * 512:dc * 512 + 512],
                                      ALU.add, ALU.add,
                                      [tm, tld, b_small.w] + bX2[dc].wr())
                        bX2[dc].setw(tr_)
                        bp.addr(tr_)
                        by1sl.addr(tr_)
                    ts_ = sdma(x2S_r[:, :, T0:T0 + 512],
                               X2blk[:].rearrange("p (c t) -> p c t", c=NC),
                               [b.w for b in bX2])
                    for dc in range(NC):
                        bX2[dc].addr(ts_)
                    for dc in range(NC):
                        kit2.stats(X2blk[:, dc * 512:dc * 512 + 512], bX2[dc],
                                   dc, ones_c)
                    kit2.finish(ones_r)
                    for dc in range(NC):
                        t1 = dve_tt(ALU.mult, APL2[:],
                                    X2blk[:, dc * 512:dc * 512 + 512],
                                    kit2.pbc[0][:],
                                    bX2[dc].rd() + [kit2.bpbc[0].w] + bAPL2.wr())
                        t2 = dve_tt(ALU.subtract, APL2[:], APL2[:], kit2.pbc[1][:],
                                    [t1, kit2.bpbc[1].w])
                        bX2[dc].addr(t2)
                        kit2.bpbc[0].addr(t2)
                        kit2.bpbc[1].addr(t2)
                        t3 = dve_ts(NX2blk[:, dc * 512:dc * 512 + 512], APL2[:],
                                    ln2wP[:, dc:dc + 1],
                                    ln2bP[:, dc:dc + 1], ALU.mult, ALU.add,
                                    [t2, b_small.w] + bNX2[dc].wr())
                        bAPL2.setw(t3)
                        bNX2[dc].setw(t3)
                    ts_ = sdma(nx2S_r[:, :, T0:T0 + 512],
                               NX2blk[:].rearrange("p (c t) -> p c t", c=NC),
                               [b.w for b in bNX2])
                    for dc in range(NC):
                        bNX2[dc].addr(ts_)
                    for r in range(4):
                        t128 = tt * 4 + r
                        tm = None
                        deps0 = bpg.wr() + [b_small.w]
                        for kc2 in range(NC):
                            deps0 += bNX2[kc2].rd()
                        for kc in range(NC):
                            tm = pe_mm(pg[:], NX2blk[:, kc * 512 + r * 128:kc * 512 + r * 128 + 128],
                                       gateWb[:, kc * 8:kc * 8 + 8],
                                       deps0 if kc == 0 else [],
                                       start=(kc == 0), stop=(kc == NC - 1),
                                       inc="pe" if kc == NC - 1 else None)
                            bNX2[kc].addr(tm)
                        bpg.setw(tm)
                        G8 = gsc[:, 0:8]
                        m1 = gsc[:, 8:9]
                        nm1 = gsc[:, 9:10]
                        m2 = gsc[:, 10:11]
                        den = gsc[:, 11:12]
                        rden = gsc[:, 12:13]
                        msk = gsc[:, 16:24]
                        g2 = gsc[:, 24:32]
                        ee = gsc[:, 32:40]
                        d0 = dve_copy(G8, pg[:], [tm] + bgsc.wr())
                        bpg.addr(d0)
                        d1 = dve_red(m1, G8, ALU.max, [d0])
                        d1b = dve_red(nm1, G8, ALU.max, [d1], negate=True)
                        d2 = dve_ts(msk, G8, m1, None, ALU.is_ge, ALU.bypass, [d1b])
                        d3 = dve_stt(g2, msk, -1e30, G8, ALU.mult, ALU.add, [d2])
                        d4 = dve_red(m2, g2, ALU.max, [d3])
                        a1 = act(AF.Exp, ee, G8, [d4], bias=nm1)
                        d5 = dve_ts(msk, G8, m2, None, ALU.is_ge, ALU.bypass, [a1])
                        d6 = dve_tt(ALU.mult, ee, ee, msk, [d5])
                        d7 = dve_red(den, ee, ALU.add, [d6])
                        d8 = dve_recip(rden, den, [d7])
                        d9 = dve_ts(g_tm[:, t128 * 8:t128 * 8 + 8], ee, rden, None,
                                    ALU.mult, ALU.bypass, [d8] + b_gtm.wr())
                        b_gtm.setw(d9)
                        bgsc.setw(d9)
                        d10 = dve_copy(gbit[:], g_tm[:, t128 * 8:t128 * 8 + 8],
                                       [d9] + bgbit.wr())
                        bgbit.setw(d10)
                        b_gtm.addr(d10)
                        t12 = pe_tr(pgt[:], gbit[:], iden[:],
                                    [d10] + biden.rd() + bpgt.wr())
                        bpgt.setw(t12)
                        bgbit.addr(t12)
                        d11 = dve_copy(gtrow[:], pgt[:], [t12] + bgtrow.wr())
                        bgtrow.setw(d11)
                        bpgt.addr(d11)
                        tsg = sdma(gTS[:, t128 * 128:t128 * 128 + 128], gtrow[:],
                                   [d11])
                        bgtrow.addr(tsg)
                flush()
                p.barrier()

        esM = ExitStack()
        with esM:
            fc1T = SB(esM, "fc1T", [128, 32 * 1024], BF16)
            fc2T = SB(esM, "fc2T", [128, 32 * 1024], BF16)
            nx2sl = [SB(esM, f"nx2sl{i}", [128, NC * 512], BF16) for i in range(2)]
            stgM = [SB(esM, f"stgM{i}", [128, 2048]) for i in range(2)]
            tmpM = [SB(esM, f"tmpM{i}", [128, 512]) for i in range(4)]
            fc1bT = SB(esM, "fc1bT_b", [128, 256])
            esM2 = ExitStack()
            Hs = [SB(esM2, f"Hs{i}", [128, 512], BF16) for i in range(32)]
            bfc1 = [Buf() for _ in range(32)]
            bfc2 = [Buf() for _ in range(32)]
            bHs = [Buf() for _ in range(32)]
            bnx2sl = [Buf(), Buf()]
            bstgM = [Buf(), Buf()]
            btmpM = [Buf(), Buf(), Buf(), Buf()]
            baccS = Buf()
            bfc1bT = Buf()
            bfc2bB = Buf()
            bgTm = Buf()
            bstgm2 = Buf()

            psm = PsA(esM, "M")
            pf = [psm.alloc(f"pf{i}", [128, 512]) for i in range(2)]
            pD = [psm.alloc(f"pD{i}", [128, 512]) for i in range(4)]
            bpf = [Buf(), Buf()]
            bpD = [Buf() for _ in range(4)]

            bfc1bT.setw(ldma(fc1bT[:], fc1bT_d, []))

            def load_pair(e, k, which, deps_extra, queue=0):
                j = (2 * k + which) % 2
                if which == 0:
                    src = fc1ts_d[e, 2 * k:2 * k + 2].rearrange(
                        "f (c p) g -> p f c g", p=128)
                    dstg = stgM[j][:].rearrange("p (f c g) -> p f c g",
                                                f=2, c=NC)
                    dst = fc1T[:, 2 * k * 1024:(2 * k + 2) * 1024]
                    bsl = [bfc1[2 * k], bfc1[2 * k + 1]]
                else:
                    src = fc2ts_d[e, 2 * k:2 * k + 2].rearrange(
                        "f p d -> p f d")
                    dstg = stgM[j][:].rearrange("p (f d) -> p f d", f=2)
                    dst = fc2T[:, 2 * k * 1024:(2 * k + 2) * 1024]
                    bsl = [bfc2[2 * k], bfc2[2 * k + 1]]
                deps = bstgM[j].wr() + deps_extra
                for b in bsl:
                    deps += b.wr()
                if queue == 0:
                    td = ldma(dstg, src, deps)
                else:
                    td = pdma(dstg, src, deps)
                bstgM[j].setw(td)
                tcw = dve_copy(dst, stgM[j][:], [td])
                bstgM[j].addr(tcw)
                for b in bsl:
                    b.setw(tcw)

            for k in range(16 if PHASES >= 4 else 0):
                load_pair(0, k, 0, [], queue=k % 2)
                load_pair(0, k, 1, [], queue=(k + 1) % 2)

            for e in range(8 if PHASES >= 4 else 0):
                for sl in range(NB):
                    T0 = sl * 512
                    jn = sl % 2
                    if NB == 1 or (e == 0 and sl == 0):
                        tld = ldma(nx2sl[jn][:].rearrange("p (c t) -> p c t", c=NC),
                                   nx2S_r[:, :, T0:T0 + 512], bnx2sl[jn].wr())
                        bnx2sl[jn].setw(tld)
                    else:
                        tld = bnx2sl[jn].w
                    if NB > 1 and (sl < NB - 1 or e < 7):
                        npf = (sl + 1) % NB
                        jn2 = npf % 2
                        T0n = npf * 512
                        tldn = ldma(nx2sl[jn2][:].rearrange("p (c t) -> p c t", c=NC),
                                    nx2S_r[:, :, T0n:T0n + 512], bnx2sl[jn2].wr())
                        bnx2sl[jn2].setw(tldn)
                    def emit_fc1(fc):
                        ppf = pf[fc % 2]
                        bppf = bpf[fc % 2]
                        tm = None
                        for kc in range(NC):
                            tm = pe_mm(
                                ppf[:],
                                fc1T[:, fc * 1024 + kc * 128:fc * 1024 + kc * 128 + 128],
                                nx2sl[jn][:, kc * 512:kc * 512 + 512],
                                ([tld] + bppf.wr() + bfc1[fc].rd()) if kc == 0 else [],
                                start=(kc == 0), stop=(kc == NC - 1),
                                inc="pe" if kc == NC - 1 else None)
                        bppf.setw(tm)
                        bfc1[fc].addr(tm)
                        bnx2sl[jn].addr(tm)
                        if e < 7 and sl == NB - 1 and fc % 2 == 1:
                            load_pair(e + 1, fc // 2, 0,
                                      [tm] + bfc1[fc - 1].rd())
                        tg = act(AF.Gelu, Hs[fc][:], ppf[:],
                                 [tm] + bfc1bT.rd() + bHs[fc].wr(),
                                 bias=fc1bT[:, e * 32 + fc:e * 32 + fc + 1])
                        bHs[fc].setw(tg)
                        bppf.addr(tg)

                    def emit_fc2A(fc):
                        tg = bHs[fc].w
                        for r in range(4):
                            tm2 = pe_mm(pD[r][:], Hs[fc][:, r * 128:r * 128 + 128],
                                        fc2T[:, fc * 1024:fc * 1024 + 512],
                                        [tg] + bfc2[fc].rd() +
                                        (bpD[r].wr() if fc == 0 else []),
                                        start=(fc == 0), stop=(fc == 31),
                                        inc="pe" if (fc == 31 or r == 3) else None)
                            bpD[r].setw(tm2)
                            bHs[fc].addr(tm2)
                            bfc2[fc].addr(tm2)

                    emit_fc1(0)
                    for fc in range(1, 32):
                        emit_fc1(fc)
                        emit_fc2A(fc - 1)
                    emit_fc2A(31)
                    for r in range(4):
                        jt = r
                        t128 = sl * 4 + r
                        gcol = g_tm[:, t128 * 8 + e:t128 * 8 + e + 1]
                        tev = dve_ts(tmpM[jt][:], pD[r][:], gcol, None, ALU.mult,
                                     ALU.bypass,
                                     [bpD[r].w, b_gtm.w] + btmpM[jt].wr())
                        btmpM[jt].setw(tev)
                        bpD[r].addr(tev)
                        b_gtm.addr(tev)
                        tdm = pdma(accS[T0 + r * 128:T0 + r * 128 + 128, 0:512],
                                   tmpM[jt][:], [tev] + baccS.rd(), accum=(e > 0))
                        btmpM[jt].addr(tdm)
                        baccS.setw(tdm)
                    for r in range(4):
                        for fc in range(32):
                            tm2 = pe_mm(pD[r][:], Hs[fc][:, r * 128:r * 128 + 128],
                                        fc2T[:, fc * 1024 + 512:fc * 1024 + 1024],
                                        (bHs[fc].rd() + bfc2[fc].rd()) +
                                        (bpD[r].wr() if fc == 0 else []),
                                        start=(fc == 0), stop=(fc == 31),
                                        inc="pe" if (fc == 31 or r == 3) else None)
                            bpD[r].setw(tm2)
                            bHs[fc].addr(tm2)
                            bfc2[fc].addr(tm2)
                            if r == 3 and e < 7 and sl == NB - 1 and fc % 2 == 1:
                                load_pair(e + 1, fc // 2, 1,
                                          list(bfc2[fc].rs) +
                                          list(bfc2[fc - 1].rs))
                    for r in range(4):
                        jt = r
                        t128 = sl * 4 + r
                        gcol = g_tm[:, t128 * 8 + e:t128 * 8 + e + 1]
                        tev = dve_ts(tmpM[jt][:], pD[r][:], gcol, None, ALU.mult,
                                     ALU.bypass,
                                     [bpD[r].w, b_gtm.w] + btmpM[jt].wr())
                        btmpM[jt].setw(tev)
                        bpD[r].addr(tev)
                        b_gtm.addr(tev)
                        tdm = pdma(accS[T0 + r * 128:T0 + r * 128 + 128, 512:1024],
                                   tmpM[jt][:], [tev] + baccS.rd(), accum=(e > 0))
                        btmpM[jt].addr(tdm)
                        baccS.setw(tdm)
            flush()
            p.barrier()
            esM2.close()

        esF = ExitStack()
        with esF:
            accsl = SB(esF, "accsl", [128, 4 * 1024])
            idenF = SB(esF, "idenF_b", [128, 128])
            bidenF = Buf()
            bidenF.setw(ldma(idenF[:], idenF_d, []))
            fc2bB = SB(esF, "fc2bB", [8, D], BF16)
            gTm = SB(esF, "gTm", [8, NTOK], BF16)
            stgm2 = SB(esF, "stgm2", [8, D])
            bfc2bB = Buf()
            bgTm = Buf()
            bstgm2 = Buf()
            td = ldma(stgm2[:], fc2b_d, bstgm2.wr())
            bstgm2.setw(td)
            tcb = dve_copy(fc2bB[:], stgm2[:], [td])
            bstgm2.addr(tcb)
            bfc2bB.setw(tcb)
            bgTm.setw(ldma(gTm[:], gTS, []))
            y1sl2 = SB(esF, "y1sl2", [128, NC * 512])
            x2sl2 = SB(esF, "x2sl2", [128, NC * 512])
            R2 = [SB(esF, f"R2_{i}", [128, 512]) for i in range(NC)]
            OUTB = SB(esF, "OUTB", [128, NC * 512])
            baccsl = Buf()
            by1sl2 = Buf()
            bx2sl2 = Buf()
            bR2 = [Buf() for _ in range(NC)]
            bOUTB = Buf()
            psf = PsA(esF, "F")
            ptrF = [psf.alloc("ptrF0", [128, 128]), psf.alloc("ptrF1", [128, 128])]
            bptrF = [Buf(), Buf()]
            pbcF = [psf.alloc(f"pbcF{i}", [128, 512]) for i in range(2)]
            bpbcF = [Buf(), Buf()]
            kit3 = LNKit(esF, "F", psf, pbcF, bpbcF)

            accS_r = accS.rearrange("(s r p) d -> s p r d", p=128, r=4)

            for tt in range(NB if PHASES >= 5 else 0):
                T0 = tt * 512
                ta = ldma(accsl[:].rearrange("p (r d) -> p r d", r=4),
                          accS_r[tt], baccsl.wr())
                baccsl.setw(ta)
                t1_ = ldma(y1sl2[:].rearrange("p (c t) -> p c t", c=NC),
                           y1S_r[:, :, T0:T0 + 512], by1sl2.wr())
                by1sl2.setw(t1_)
                t2_ = ldma(x2sl2[:].rearrange("p (c t) -> p c t", c=NC),
                           x2S_r[:, :, T0:T0 + 512], bx2sl2.wr())
                bx2sl2.setw(t2_)
                for r in range(4):
                    t128 = tt * 4 + r
                    for dc in range(NC):
                        jf = (r * NC + dc) % 2
                        pF = ptrF[jf]
                        bpF = bptrF[jf]
                        ttr = pe_tr(pF[:],
                                    accsl[:, r * 1024 + dc * 128:
                                          r * 1024 + dc * 128 + 128],
                                    idenF[:],
                                    baccsl.rd() + bpF.wr() + bidenF.rd())
                        baccsl.addr(ttr)
                        ttr = pe_mm(pF[:], fc2bB[0:8, dc * 128:dc * 128 + 128],
                                    gTm[:, t128 * 128:t128 * 128 + 128],
                                    [ttr] + bfc2bB.rd() + bgTm.rd(),
                                    start=False, stop=True, inc="pe")
                        bpF.setw(ttr)
                        bfc2bB.addr(ttr)
                        bgTm.addr(ttr)
                        da = dve_tt(ALU.add, R2[dc][:, r * 128:r * 128 + 128],
                                    pF[:],
                                    y1sl2[:, dc * 512 + r * 128:
                                          dc * 512 + r * 128 + 128],
                                    [ttr, t1_] + (bR2[dc].wr() if r == 0
                                                  else [bR2[dc].w]))
                        db = dve_tt(ALU.add, R2[dc][:, r * 128:r * 128 + 128],
                                    R2[dc][:, r * 128:r * 128 + 128],
                                    x2sl2[:, dc * 512 + r * 128:
                                          dc * 512 + r * 128 + 128],
                                    [da, t2_])
                        bR2[dc].setw(db)
                        bpF.addr(db)
                        by1sl2.addr(db)
                        bx2sl2.addr(db)
                for dc in range(NC):
                    kit3.stats(R2[dc][:], bR2[dc], dc, ones_c)
                kit3.finish(ones_r)
                outws = []
                for dc in range(NC):
                    osl = OUTB[:, dc * 512:dc * 512 + 512]
                    t1 = dve_tt(ALU.mult, osl, R2[dc][:], kit3.pbc[0][:],
                                bR2[dc].rd() + [kit3.bpbc[0].w] + bOUTB.wr())
                    t2 = dve_tt(ALU.subtract, osl, osl, kit3.pbc[1][:],
                                [t1, kit3.bpbc[1].w])
                    bOUTB.setw(t2)
                    outws.append(t2)
                    bR2[dc].addr(t2)
                    kit3.bpbc[0].addr(t2)
                    kit3.bpbc[1].addr(t2)
                ts_ = sdma(out_r[:, :, T0:T0 + 512],
                           OUTB[:].rearrange("p (c t) -> p c t", c=NC), outws)
                bOUTB.addr(ts_)
            p.op("dve",
                 lambda e: e.memset(OUTB[0:1, 0:8], 0.0),
                 [p.tok("dl"), p.tok("dacc")])
            flush()

    nc.finalize()
    return nc


def pack_cols(v):
    return np.ascontiguousarray(np.asarray(v, np.float32).reshape(-1, 128).T)


def prep_inputs(inputs, NTOK=4096):
    f32 = lambda a: np.asarray(a, np.float32)
    x = f32(inputs["L0_attn_x"])[:, :NTOK, :]
    omega = f32(inputs["L0_attn_omega"])
    com = {
        "wqT": np.ascontiguousarray(f32(inputs["L0_attn_wq"]).T),
        "wkT": np.ascontiguousarray(f32(inputs["L0_attn_wk"]).T),
        "wvT": np.ascontiguousarray(f32(inputs["L0_attn_wv"]).T),
        "woT": np.ascontiguousarray(f32(inputs["L0_attn_wo"]).T),
        "omA": np.ascontiguousarray(
            np.concatenate([omega.T, np.full((64, 64), -0.5, np.float32)],
                           axis=0)),
        "boP": pack_cols(inputs["L0_attn_bo"]),
        "Wqkv": np.ascontiguousarray(f32(inputs["L0_ffn_Wqkv"])),
        "bqkvP": pack_cols(inputs["L0_ffn_bqkv"]),
        "bqkvV": np.ascontiguousarray(
            f32(inputs["L0_ffn_bqkv"])[2048:].reshape(1, 1024)),
        "WoF": np.ascontiguousarray(f32(inputs["L0_ffn_Wo"])),
        "boFP": pack_cols(inputs["L0_ffn_bo"]),
        "ln1wP": pack_cols(inputs["L0_ffn_ln1_w"]),
        "ln1bP": pack_cols(inputs["L0_ffn_ln1_b"]),
        "ln2wP": pack_cols(inputs["L0_ffn_ln2_w"]),
        "ln2bP": pack_cols(inputs["L0_ffn_ln2_b"]),
        "gateWP": np.ascontiguousarray(
            f32(inputs["L0_ffn_gate_W"]).reshape(8, 128, 8)
            .transpose(1, 0, 2).reshape(128, 64)),
        "fc1ts": np.ascontiguousarray(
            f32(inputs["L0_ffn_fc1_w"]).transpose(0, 2, 1)
            .reshape(8, 1024, 32, 128).transpose(0, 2, 1, 3)),
        "fc2ts": np.ascontiguousarray(
            f32(inputs["L0_ffn_fc2_w"]).transpose(0, 2, 1)
            .reshape(8, 32, 128, 1024)),
        "fc1bT": np.ascontiguousarray(
            f32(inputs["L0_ffn_fc1_b"]).reshape(8, 32, 128)
            .transpose(2, 0, 1).reshape(128, 256)),
        "fc2b": np.ascontiguousarray(f32(inputs["L0_ffn_fc2_b"])),
        "iden": np.eye(128, dtype=ml_dtypes.bfloat16),
        "idenF": np.eye(128, dtype=np.float32),
        "triu": np.triu(np.ones((128, 128))).astype(ml_dtypes.bfloat16),
        "fconst": np.ascontiguousarray(
            np.broadcast_to(np.array([[-np.log(8.0), 1e-5, 1e-6]], np.float32),
                            (128, 3))),
    }
    in_maps = []
    for b in range(x.shape[0]):
        m = dict(com)
        m["xT"] = np.ascontiguousarray(x[b].T)
        in_maps.append(m)
    return in_maps


from concourse.bass_utils import run_bass_kernel_spmd


def kernel(**inputs):
    in_maps = prep_inputs(inputs, 4096)
    nc = build(4096, debug=False)
    res = run_bass_kernel_spmd(nc, in_maps, core_ids=list(range(8)))
    out = np.stack([np.asarray(r["out"], np.float32).T for r in res.results])
    return np.ascontiguousarray(out)
